# version 15
# baseline (speedup 1.0000x reference)
import numpy as np
from collections import defaultdict
from contextlib import ExitStack

import concourse.bass as bass
import concourse.mybir as mybir
from concourse.bass_utils import run_bass_kernel_spmd

F32 = mybir.dt.float32
BF16 = mybir.dt.bfloat16
AL = mybir.AluOpType
AF = mybir.ActivationFunctionType
AX = mybir.AxisListType

D = 2048
DC = 16
T = 1024
TB = 256
B = 4
H = 16
PW = 512
CH = 64
NCH = 4
LN_EPS = 1e-5
RMS_EPS = 1.1920928955078125e-07
NCORES = 8

ENGINES = ("pe", "act", "dve", "pool", "sp")


class Build:

    def __init__(self, nc):
        self.nc = nc
        self.streams = {e: [] for e in ENGINES}
        self.cnt = defaultdict(int)
        self.lastwait = {e: defaultdict(int) for e in ENGINES}
        self.sem_names = []

    def sem(self, name):
        if name not in self.sem_names:
            self.sem_names.append(name)
        return name

    def dep(self, eng, tok):
        if tok is None:
            return
        s, v = tok
        if v <= self.lastwait[eng][s]:
            return
        self.lastwait[eng][s] = v
        self.streams[eng].append(("wait", s, v))

    def deps(self, eng, toks):
        best = {}
        for t in toks:
            if t is None:
                continue
            s, v = t
            if v > best.get(s, -1):
                best[s] = v
        for s, v in best.items():
            self.dep(eng, (s, v))

    SERIALIZED = {"dve": "s_dvs", "act": "s_acs"}

    def run(self, eng, fn, inc=None, base=None):
        tok = None
        if eng in self.SERIALIZED:
            s = self.SERIALIZED[eng]
            self.sem(s)
            idx = self.cnt[s] + 1
            waitv = (idx - 1) if base is None else min(base, idx - 1)
            if waitv > 0:
                self.dep(eng, (s, waitv))
            self.cnt[s] = idx
            if inc is not None:
                tok = (s, idx)
            self.streams[eng].append(("op", fn, (s, 1)))
            return tok
        if inc is not None:
            s, amt = inc
            self.sem(s)
            self.cnt[s] += amt
            tok = (s, self.cnt[s])
        self.streams[eng].append(("op", fn, inc))
        return tok

    def vidx(self, eng):
        return self.cnt[self.SERIALIZED[eng]]

    def replay(self, eng, obj, handles):
        for item in self.streams[eng]:
            if item[0] == "wait":
                _, s, v = item
                obj.wait_ge(handles[s], v)
            else:
                _, fn, inc = item
                inst = fn(obj)
                if inc is not None:
                    s, amt = inc
                    inst.then_inc(handles[s], amt)


def build_nc():
    nc = bass.Bass("TRN2", target_bir_lowering=False, debug=False)

    def dinp(name, shape, dt=F32):
        return nc.dram_tensor(name, list(shape), dt, kind="ExternalInput").ap()

    x_fm = dinp("x_fm", [DC, 128, T])
    w_in_t = dinp("w_in_t", [H, 5, 128, DC, 128])
    w_out_t = dinp("w_out_t", [16, 128, DC, 128])
    w3_t = dinp("w3_t", [16, 128, DC, 128])
    sw_t = dinp("sw_t", [8, 128, DC, TB])
    w12T_in = dinp("w12T", [128, 16, 128])
    b1T_in = dinp("b1T", [128, 8])
    b2T_in = dinp("b2T", [128, 8])
    b3row_in = dinp("b3row", [1, D])
    swb_in = dinp("swb", [4, 8])
    onehot_in = dinp("onehot", [128, 8])

    out = nc.dram_tensor("out", [DC, 128, T], F32, kind="ExternalOutput").ap()

    rt_in = [nc.dram_tensor(f"rt_in{i}", [32], F32).ap() for i in range(2)]
    rt_out = [nc.dram_tensor(f"rt_out{i}", [32], F32).ap() for i in range(2)]
    sc_in = [nc.dram_tensor(f"sc_in{i}", [128, 64], F32).ap() for i in range(2)]
    sc_out = [nc.dram_tensor(f"sc_out{i}", [NCORES * 128, 64], F32).ap() for i in range(2)]
    selb = [nc.dram_tensor(f"selb{i}", [4, 40], F32).ap() for i in range(2)]

    def salloc(name, shape, dt):
        return nc.alloc_sbuf_tensor(name, list(shape), dt)

    X = salloc("X", [128, DC * T], F32)
    A_BF = salloc("A_BF", [128, 16384], BF16)
    B_BF = salloc("B_BF", [128, 8192], BF16)
    C_F = salloc("C_F", [128, 8192], F32)
    D_BF = salloc("D_BF", [128, 16384], BF16)
    WBUF = salloc("WBUF", [128, 2 * 2048], BF16)
    SCR = salloc("SCR", [128, 4 * 512], F32)
    SCB = salloc("SCB", [128, 3 * 512], BF16)
    ONES_F = salloc("ONES_F", [128, 128], F32)
    ONES_B = salloc("ONES_B", [128, 512], BF16)
    CONSTS = salloc("CONSTS", [128, 6], F32)
    S_LOC = salloc("S_LOC", [128, 64], F32)
    S_ALL = salloc("S_ALL", [128, NCORES * 64], F32)
    PREF = salloc("PREF", [128, 9 * 32], F32)
    SINIT = salloc("SINIT", [128, 32], F32)
    PFE = salloc("PFE", [128, 128], F32)
    APF = salloc("APF", [128, 128], F32)
    Z0 = salloc("Z0", [128, 128], F32)
    SREAL = salloc("SREAL", [128, 128], F32)
    CSC = salloc("CSC", [128, 64], F32)
    LGS = salloc("LGS", [1, 32], F32)
    LG = salloc("LG", [4, 64], F32)
    MSK = salloc("MSK", [4, 40], F32)
    GROW = salloc("GROW", [1, 160], F32)
    GSEL = salloc("GSEL", [128, 160], F32)
    B1T = salloc("B1T", [128, 8], F32)
    B2T = salloc("B2T", [128, 8], F32)
    B1S = salloc("B1S", [128, 8], F32)
    B2G = salloc("B2G", [128, 4], F32)
    SWB = salloc("SWB", [4, 8], F32)
    OH = salloc("OH", [128, 8], F32)
    SELS = salloc("SELS", [128, 2 * 128], BF16)
    SELF = salloc("SELF", [128, 2], F32)

    ps = {}
    def palloc(name, shape):
        ps[name] = nc.alloc_psum_tensor(name, list(shape), F32)
        return ps[name]

    palloc("mmA", [128, 512])
    palloc("mmB", [128, 512])
    palloc("bcA", [128, 512])
    palloc("bcB", [128, 512])
    palloc("st1", [128, 1024])
    palloc("st2", [128, 1024])

    bd = Build(nc)
    V, P, ACT, PE, SP = "dve", "pool", "act", "pe", "sp"
    for s in ("s_gp", "s_xin", "s_sm", "s_exw", "s_b3", "s_io", "s_w0", "s_w1",
              "s_w2", "s_w3", "s_sw0", "s_sw1", "s_pe", "s_dv", "s_ac",
              "s_cc", "s_st", "s_dvs", "s_acs"):
        bd.sem(s)

    bank_free = {k: None for k in ps}
    wring_free = [None, None, None, None]
    BN4 = ["mmA", "mmB", "bcA", "bcB"]
    free_tok = defaultdict(lambda: None)

    def Xc(dc):
        return X[:, dc * T:(dc + 1) * T]

    def pe_mm(bank_ap, lhsT, rhs, start, stop, inc=None):
        return bd.run(PE, lambda e, bank_ap=bank_ap, lhsT=lhsT, rhs=rhs, start=start,
                      stop=stop: e.matmul(bank_ap, lhsT, rhs, start=start, stop=stop),
                      inc=inc)

    tx_toks = []
    for dc in range(DC):
        tx_toks.append(bd.run(SP, lambda e, dc=dc: e.dma_start(out=Xc(dc), in_=x_fm[dc]),
                              inc=("s_xin", 16)))
    t_smalls = None
    for (dst, src) in ((B1T, b1T_in), (B2T, b2T_in), (SWB, swb_in), (OH, onehot_in)):
        t_smalls = bd.run(SP, lambda e, d=dst, s=src: e.dma_start(out=d[:], in_=s),
                          inc=("s_sm", 16))
    bd.run(V, lambda e: e.memset(ONES_F[:], 1.0))
    bd.run(V, lambda e: e.memset(CONSTS[:, 0:1], LN_EPS))
    bd.run(V, lambda e: e.memset(CONSTS[:, 1:2], 2.0))
    bd.run(V, lambda e: e.memset(CONSTS[:, 2:3], RMS_EPS))
    bd.run(V, lambda e: e.memset(CONSTS[:, 3:4], 1.0 / 128.0))
    bd.run(V, lambda e: e.memset(CONSTS[:, 4:5], -0.5))
    bd.run(V, lambda e: e.memset(CONSTS[:, 5:6], -1.0))
    t_ones = bd.run(V, lambda e: e.memset(ONES_B[:], 1.0), inc=("s_dv", 1))
    bd.dep(ACT, t_ones)
    C_EPS = CONSTS[:, 0:1]
    C_TWO = CONSTS[:, 1:2]
    C_REPS = CONSTS[:, 2:3]
    C_I128 = CONSTS[:, 3:4]
    C_NH = CONSTS[:, 4:5]
    C_N1 = CONSTS[:, 5:6]

    xup_toks = list(tx_toks)
    x_ready = {dc: [("s_xin", 16 * DC)] for dc in range(DC)}

    def emit_ln(xln_dst, col0, width, xbf_scr, x2bf_scr, ascr, nscr, apply_toks=None):
        nh = (width + 511) // 512
        bd.dep(PE, t_ones)
        bd.dep(PE, bank_free["st1"])
        bd.dep(PE, bank_free["st2"])
        pe_tok = [None, None]
        t_stats = None
        for dc in range(DC):
            xs = Xc(dc)[:, col0:col0 + width]
            bd.dep(V, pe_tok[dc % 2])
            bd.deps(V, x_ready[dc])
            t_c = bd.run(V, lambda e, dc=dc, xs=xs: e.tensor_copy(xbf_scr(dc), xs),
                         inc=("s_dv", 1))
            t_s = bd.run(V, lambda e, dc=dc: e.tensor_tensor(
                x2bf_scr(dc), xbf_scr(dc), xbf_scr(dc), AL.mult),
                base=t_c[1], inc=("s_dv", 1))
            bd.dep(PE, t_c)
            bd.dep(PE, t_s)
            for h in range(nh):
                w = min(512, width - h * 512)
                pe_mm(ps["st1"][:, h * 512:h * 512 + w], ONES_B[:, 0:128],
                      xbf_scr(dc)[:, h * 512:h * 512 + w], start=(dc == 0), stop=(dc == DC - 1))
                t_stats = pe_mm(ps["st2"][:, h * 512:h * 512 + w], ONES_B[:, 0:128],
                                x2bf_scr(dc)[:, h * 512:h * 512 + w],
                                start=(dc == 0), stop=(dc == DC - 1), inc=("s_pe", 1))
            pe_tok[dc % 2] = t_stats
        t_last = None
        t_v = None
        for h in range(nh):
            hs = h * 512
            w = min(512, width - hs)
            SM = SCR[0:1, 0:w]
            SV = SCR[0:1, 512:512 + w]
            S3 = SCR[0:1, 1024:1024 + w]
            S4 = SCR[0:1, 1536:1536 + w]
            bd.dep(V, t_stats)
            if h > 0:
                bd.dep(V, t_bc)
            bd.run(V, lambda e, hs=hs, w=w, SM=SM: e.tensor_scalar(
                SM, ps["st1"][0:1, hs:hs + w], 1.0 / D, None, AL.mult))
            t_v = bd.run(V, lambda e, hs=hs, w=w, SV=SV: e.tensor_scalar(
                SV, ps["st2"][0:1, hs:hs + w], 1.0 / D, None, AL.mult),
                inc=("s_dv", 1))
            bd.run(V, lambda e, SM=SM, S3=S3: e.tensor_tensor(S3, SM, SM, AL.mult))
            t_var = bd.run(V, lambda e, SV=SV, S3=S3, S4=S4: e.tensor_tensor(
                S4, SV, S3, AL.subtract), inc=("s_dv", 1))
            bd.dep(ACT, t_var)
            bd.run(ACT, lambda e, S3=S3, S4=S4: e.activation(
                S3, S4, AF.Ln, bias=CONSTS[0:1, 0:1]), inc=("s_ac", 1))
            t_sd = bd.run(ACT, lambda e, SV=SV, S3=S3: e.activation(
                SV, S3, AF.Exp, scale=CONSTS[0:1, 4:5]), inc=("s_ac", 1))
            bd.dep(V, t_sd)
            bd.run(V, lambda e, SM=SM, SV=SV, S3=S3: e.tensor_tensor(
                S3, SM, SV, AL.mult))
            t_rows = bd.run(V, lambda e, S3=S3, S4=S4: e.tensor_scalar(
                S4, S3, -1.0, None, AL.mult), inc=("s_dv", 1))
            bd.dep(PE, bank_free["bcA"])
            bd.dep(PE, bank_free["bcB"])
            bd.dep(PE, t_rows)
            pe_mm(ps["bcA"][:, 0:w], ONES_F[0:1, :], SV, True, True)
            t_bc = pe_mm(ps["bcB"][:, 0:w], ONES_F[0:1, :], S4,
                         True, True, inc=("s_pe", 1))
            bd.dep(V, t_bc)
            mt = {}
            at = {}
            g0 = bd.vidx("dve")
            for dc in range(DC):
                xs = Xc(dc)[:, col0 + hs:col0 + hs + w]
                scr = ascr(dc % nscr)[:, 0:w]
                bd.deps(V, x_ready[dc])
                mt[dc] = bd.run(V, lambda e, xs=xs, scr=scr, w=w: e.tensor_tensor(
                    scr, xs, ps["bcA"][:, 0:w], AL.mult),
                    base=(at[dc - nscr][1] if dc - nscr in at else g0),
                    inc=("s_dv", 1))
                if dc >= 1:
                    d2 = dc - 1
                    dst = xln_dst(d2)[:, hs:hs + w]
                    scr2 = ascr(d2 % nscr)[:, 0:w]
                    at[d2] = bd.run(V, lambda e, dst=dst, scr2=scr2, w=w: e.tensor_tensor(
                        dst, scr2, ps["bcB"][:, 0:w], AL.add), base=mt[d2][1],
                        inc=("s_dv", 1))
                    if apply_toks is not None:
                        apply_toks.setdefault(d2, []).append(at[d2])
            d2 = DC - 1
            dst = xln_dst(d2)[:, hs:hs + w]
            scr2 = ascr(d2 % nscr)[:, 0:w]
            t_last = bd.run(V, lambda e, dst=dst, scr2=scr2, w=w: e.tensor_tensor(
                dst, scr2, ps["bcB"][:, 0:w], AL.add), base=mt[d2][1],
                inc=("s_dv", 1))
            if apply_toks is not None:
                apply_toks.setdefault(d2, []).append(t_last)
            bank_free["bcA"] = t_last
            bank_free["bcB"] = t_last
        bank_free["st1"] = t_v
        bank_free["st2"] = t_v
        return t_last

    sw_free = [None, None]

    def emit_ffn(call):
        XLN = lambda dc: A_BF[:, dc * 1024:(dc + 1) * 1024]
        SWbuf = lambda i: D_BF[:, (i % 2) * 4096:(i % 2) * 4096 + 4096]
        Pbuf = lambda i: D_BF[:, 8192 + (i % 2) * 1024: 8192 + (i % 2) * 1024 + 1024]
        LNxb = lambda dc: D_BF[:, 8192 + (dc % 2) * 1024: 8192 + (dc % 2) * 1024 + 1024]
        LNx2 = lambda dc: D_BF[:, 10240 + (dc % 2) * 1024: 10240 + (dc % 2) * 1024 + 1024]
        EXW1 = lambda e_: B_BF[:, e_ * 128:(e_ + 1) * 128]
        EXW2 = lambda e_: B_BF[:, 1024 + e_ * 128: 1024 + (e_ + 1) * 128]
        SELW1 = lambda bk: B_BF[:, 2048 + bk * 128: 2048 + (bk + 1) * 128]
        SELW2 = lambda bk: B_BF[:, 3072 + bk * 128: 3072 + (bk + 1) * 128]
        HB = lambda i: B_BF[:, 4096 + (i % 2) * 256: 4096 + (i % 2) * 256 + 256]
        C_bf = C_F.bitcast(BF16)
        Y1 = lambda dc: C_bf[:, dc * 1024:(dc + 1) * 1024]

        bd.deps(V, [wring_free[2], wring_free[3]])
        D_F = D_BF.bitcast(F32)
        ln_toks = {}
        t_ln = emit_ln(XLN, 0, T, LNxb, LNx2,
                       lambda i: D_F[:, 4096 + i * 512: 4096 + (i + 1) * 512], 4,
                       apply_toks=ln_toks)

        bd.dep(P, free_tok["b_arena"])
        t_exw2 = bd.run(P, lambda e: e.dma_start(
            out=B_BF[:, 0:2048],
        in_=w12T_in.rearrange("p e q -> p (e q)")),
            inc=("s_exw", 16))

        bd.dep(P, t_ln)
        sw_dma = [None, None]
        lg_toks = []
        pbuf_free = [None, None]
        for e_ in range(8):
            t_p_v = t_p_g = None
            if e_ < 2:
                bd.deps(P, sw_free[e_ % 2] or [])
                bd.dep(P, free_tok["d_arena"])
                sw_dma[e_ % 2] = bd.run(P, lambda e, e_=e_: e.dma_start(
                    out=SWbuf(e_),
                    in_=sw_t[e_].rearrange("p dc t -> p (dc t)")),
                    inc=("s_sw0" if e_ % 2 == 0 else "s_sw1", 16))
            stats = "st1" if e_ % 2 == 0 else "st2"
            bd.dep(V, sw_dma[e_ % 2])
            bd.dep(P, sw_dma[e_ % 2])
            t_mm = None
            rg0 = bd.vidx("dve")
            GPDC = (5, 10, 15)
            for dc in range(DC):
                swv = SWbuf(e_)[:, dc * 256:(dc + 1) * 256]
                sw_b = swv.rearrange("p (o t) -> p o t", o=1).broadcast_to([128, 4, 256])
                pb = Pbuf(dc)
                eng = P if dc in GPDC else V
                bd.dep(eng, pbuf_free[dc % 2])
                bd.deps(eng, ln_toks.get(dc, []))
                t_p = bd.run(eng, lambda e, dc=dc, sw_b=sw_b, pb=pb, e_=e_: e.tensor_tensor(
                    pb.rearrange("p (b t) -> p b t", b=B),
                    XLN(dc).rearrange("p (b t) -> p b t", b=B), sw_b, AL.mult),
                    **(dict(base=rg0, inc=("s_dv", 1)) if eng == V
                       else dict(inc=("s_gp", 1))))
                bd.dep(PE, bank_free[stats])
                bd.dep(PE, t_p)
                for hf in range(2):
                    t_mm = pe_mm(ps[stats][:, hf * 512:(hf + 1) * 512], ONES_B[:, 0:128],
                                 pb[:, hf * 512:(hf + 1) * 512],
                                 start=(dc == 0), stop=(dc == DC - 1),
                                 inc=("s_pe", 1) if hf == 1 else None)
                pbuf_free[dc % 2] = t_mm
                if eng == V:
                    t_p_v = t_p
                else:
                    t_p_g = t_p
            sw_free[e_ % 2] = [t_p_v, t_p_g]
            if e_ + 2 < 8:
                bd.deps(P, sw_free[e_ % 2] or [])
                sw_dma[e_ % 2] = bd.run(P, lambda e, e_=e_: e.dma_start(
                    out=SWbuf(e_ + 2),
                    in_=sw_t[e_ + 2].rearrange("p dc t -> p (dc t)")),
                    inc=("s_sw0" if e_ % 2 == 0 else "s_sw1", 16))
            bd.dep(V, t_mm)
            bd.dep(V, free_tok["LGS"])
            lt = None
            for b_ in range(B):
                lt = bd.run(V, lambda e, e_=e_, b_=b_, stats=stats: e.tensor_reduce(
                    LGS[:, b_ * 8 + e_: b_ * 8 + e_ + 1],
                    ps[stats][0:1, b_ * 256:(b_ + 1) * 256], AX.X, AL.add),
                    inc=("s_dv", 1) if b_ == B - 1 else None)
            bank_free[stats] = lt
            lg_toks.append(lt)

        B3R = D_BF[0:1, 0:2048]
        bd.deps(P, sw_free[0] or [])
        t_b3 = bd.run(P, lambda e: e.dma_start(out=B3R, in_=b3row_in), inc=("s_b3", 16))

        bd.deps(SP, lg_toks)
        t_s1 = bd.run(SP, lambda e: e.dma_start(
            out=rt_in[call].rearrange("(o c) -> o c", o=1), in_=LGS[0:1, :]), inc=("s_io", 16))
        free_tok["LGS"] = t_s1
        bd.dep(P, t_s1)
        t_cc = bd.run(P, lambda e: e.collective_compute(
            "AllReduce", AL.add, replica_groups=[list(range(NCORES))],
            ins=[rt_in[call].opt()], outs=[rt_out[call].opt()]), inc=("s_cc", 1))
        bd.dep(SP, t_cc)
        t_lg = bd.run(SP, lambda e: e.dma_start(
            out=LG[:, 0:8], in_=rt_out[call].rearrange("(b e) -> b e", b=4)),
            inc=("s_io", 16))

        lg0 = LG[:, 0:8]; ex_ = LG[:, 8:16]; pr = LG[:, 16:24]; pr2 = LG[:, 24:32]
        cs = LG[:, 32:40]; tmp8 = LG[:, 40:48]; mx = LG[:, 48:49]; nmx = LG[:, 49:50]
        sm = LG[:, 50:51]; rs = LG[:, 51:52]
        msk1 = MSK[:, 0:8]; msk2 = MSK[:, 8:16]; gm1 = MSK[:, 16:24]
        gm2 = MSK[:, 24:32]; gmc = MSK[:, 32:40]
        bd.dep(V, t_lg)
        bd.dep(V, t_smalls)
        bd.dep(V, free_tok["MSK"])
        bd.run(V, lambda e: e.tensor_tensor(lg0, lg0, SWB[:], AL.add))
        bd.run(V, lambda e: e.reduce_max(mx, lg0, AX.X))
        t_nm = bd.run(V, lambda e: e.tensor_scalar(nmx, mx, -1.0, None, AL.mult), inc=("s_dv", 1))
        bd.dep(ACT, t_nm)
        t_ex = bd.run(ACT, lambda e: e.activation(ex_, lg0, AF.Exp, bias=nmx[:, 0:1]),
                      inc=("s_ac", 1))
        bd.dep(V, t_ex)
        bd.run(V, lambda e: e.reduce_sum(sm, ex_, AX.X))
        bd.run(V, lambda e: e.reciprocal(rs, sm))
        bd.run(V, lambda e: e.tensor_scalar(pr, ex_, rs[:, 0:1], None, AL.mult))
        bd.run(V, lambda e: e.reduce_max(mx, pr, AX.X))
        bd.run(V, lambda e: e.tensor_scalar(tmp8, pr, mx[:, 0:1], None, AL.is_ge))
        bd.run(V, lambda e: e.tensor_tensor_scan(cs, ONES_F[0:4, 0:8], tmp8, 0.0, AL.mult, AL.add))
        bd.run(V, lambda e: e.tensor_scalar(cs, cs, 1.0, None, AL.is_equal))
        bd.run(V, lambda e: e.tensor_tensor(msk1, tmp8, cs, AL.mult))
        bd.run(V, lambda e: e.tensor_tensor(gm1, pr, msk1, AL.mult))
        bd.run(V, lambda e: e.tensor_scalar(tmp8, msk1, -1.0, 1.0, AL.mult, AL.add))
        bd.run(V, lambda e: e.tensor_tensor(pr2, pr, tmp8, AL.mult))
        bd.run(V, lambda e: e.reduce_max(mx, pr2, AX.X))
        bd.run(V, lambda e: e.tensor_scalar(tmp8, pr2, mx[:, 0:1], None, AL.is_ge))
        bd.run(V, lambda e: e.tensor_tensor_scan(cs, ONES_F[0:4, 0:8], tmp8, 0.0, AL.mult, AL.add))
        bd.run(V, lambda e: e.tensor_scalar(cs, cs, 1.0, None, AL.is_equal))
        bd.run(V, lambda e: e.tensor_tensor(msk2, tmp8, cs, AL.mult))
        bd.run(V, lambda e: e.tensor_tensor(gm2, pr2, msk2, AL.mult))
        t_msk = bd.run(V, lambda e: e.tensor_tensor(gmc, gm1, gm2, AL.add), inc=("s_dv", 1))

        bd.dep(SP, t_msk)
        t_sb = bd.run(SP, lambda e: e.dma_start(out=selb[call], in_=MSK[:, 0:40]),
                      inc=("s_io", 16))
        free_tok["MSK"] = t_sb
        bd.dep(SP, t_sb)
        t_sr = bd.run(SP, lambda e: e.dma_start(
            out=GROW[:], in_=selb[call].rearrange("b c -> (b c)")[None, :]), inc=("s_io", 16))
        bd.dep(PE, t_sr)
        bd.dep(PE, bank_free["bcA"])
        t_gb = pe_mm(ps["bcA"][:, 0:160], ONES_F[0:1, :], GROW[:], True, True, inc=("s_pe", 1))
        bd.dep(V, t_gb)
        t_gs = bd.run(V, lambda e: e.tensor_copy(GSEL[:, 0:160], ps["bcA"][:, 0:160]),
                      inc=("s_dv", 1))
        bank_free["bcA"] = t_gs

        bd.dep(V, t_exw2)
        sel_last = None
        chains = []
        for b_ in range(B):
            for k_ in range(2):
                bk = b_ * 2 + k_
                moff = b_ * 40 + (0 if k_ == 0 else 8)
                goff = b_ * 40 + (16 if k_ == 0 else 24)
                chains.append((SELW1(bk), EXW1, moff))
                chains.append((SELW2(bk), EXW2, goff))
        for grp in range(0, len(chains), 4):
            batch = chains[grp:grp + 4]
            prev = [None] * len(batch)
            for e_ in range(8):
                for lane, (dst, src, off) in enumerate(batch):
                    pp = [B_BF[:, 6144 + lane * 256:6144 + lane * 256 + 128],
                          B_BF[:, 6144 + lane * 256 + 128:6144 + lane * 256 + 256]]
                    o = dst if e_ == 7 else pp[e_ % 2]
                    if e_ == 0:
                        prev[lane] = bd.run(V, lambda e, o=o, src=src, off=off:
                            e.tensor_scalar(o, src(0), GSEL[:, off:off + 1], None,
                            AL.mult), base=max(free_tok["sel_lane"] or 0, t_gs[1]),
                            inc=("s_dv", 1))
                    else:
                        prev[lane] = bd.run(V, lambda e, o=o, src=src, e_=e_, off=off,
                            i=pp[(e_ + 1) % 2]: e.scalar_tensor_tensor(
                            o, src(e_), GSEL[:, off + e_: off + e_ + 1], i,
                            AL.mult, AL.add), base=prev[lane][1], inc=("s_dv", 1))
            sel_last = prev[-1]
            free_tok["sel_lane"] = max(t[1] for t in prev)
        bchains = []
        for b_ in range(B):
            for k_ in range(2):
                bk = b_ * 2 + k_
                moff = b_ * 40 + (0 if k_ == 0 else 8)
                bchains.append((B1S[:, bk:bk + 1], B1T, moff))
        for b_ in range(B):
            bchains.append((B2G[:, b_:b_ + 1], B2T, b_ * 40 + 32))
        for grp in range(0, len(bchains), 4):
            batch = bchains[grp:grp + 4]
            prev = [None] * len(batch)
            for e_ in range(8):
                for lane, (dst, tsrc, off) in enumerate(batch):
                    ppf = [PREF[:, 272 + 2 * lane:273 + 2 * lane],
                           PREF[:, 273 + 2 * lane:274 + 2 * lane]]
                    o = dst if e_ == 7 else ppf[e_ % 2]
                    if e_ == 0:
                        prev[lane] = bd.run(V, lambda e, o=o, tsrc=tsrc, off=off:
                            e.tensor_scalar(o, tsrc[:, 0:1], GSEL[:, off:off + 1], None,
                            AL.mult), base=max(free_tok["selb_lane"] or 0, t_gs[1]),
                            inc=("s_dv", 1))
                    else:
                        prev[lane] = bd.run(V, lambda e, o=o, tsrc=tsrc, e_=e_, off=off,
                            i=ppf[(e_ + 1) % 2]: e.scalar_tensor_tensor(
                            o, tsrc[:, e_:e_ + 1], GSEL[:, off + e_: off + e_ + 1], i,
                            AL.mult, AL.add), base=prev[lane][1], inc=("s_dv", 1))
            sel_last = prev[-1]
            free_tok["selb_lane"] = max(t[1] for t in prev)
        t_sel = sel_last

        y1_toks = []
        obanks = ["mmA", "mmB", "bcA", "bcB"]
        hbanks = ["st1", "st2"]
        HB8 = lambda s: B_BF[:, 4096 + s * 256: 4096 + (s + 1) * 256]
        hb_i = 0
        bd.dep(PE, t_sel)
        bd.dep(PE, t_ln)
        for b_ in range(B):
            for g in range(4):
                hcs = [g * 4 + i for i in range(4)]
                gel = {}
                mm1t = {}
                for k_ in range(2):
                    bk = b_ * 2 + k_
                    for i, hc in enumerate(hcs):
                        s = k_ * 4 + i
                        hname = hbanks[hb_i % 2]
                        hb = ps[hname]
                        bd.dep(PE, bank_free[hname])
                        t_mm1 = pe_mm(hb[:, 0:256], SELW1(bk),
                                      XLN(hc)[:, b_ * 256:(b_ + 1) * 256], True, True,
                                      inc=("s_pe", 1))
                        bd.dep(ACT, t_mm1)
                        t_h = bd.run(ACT, lambda e, hb=hb, bk=bk, s=s: e.activation(
                            HB8(s), hb[:, 0:256], AF.Gelu, bias=B1S[:, bk:bk + 1]),
                            inc=("s_ac", 1))
                        bank_free[hname] = t_h
                        gel[s] = t_h
                        hb_i += 1
                for k_ in range(2):
                    bk = b_ * 2 + k_
                    for i, hc in enumerate(hcs):
                        s = k_ * 4 + i
                        ob = ps[obanks[i]]
                        bd.dep(PE, gel[s])
                        if k_ == 0:
                            bd.dep(PE, bank_free[obanks[i]])
                        t_mm2 = pe_mm(ob[:, 0:256], SELW2(bk), HB8(s),
                                      start=(k_ == 0), stop=(k_ == 1),
                                      inc=("s_pe", 1))
                        if k_ == 1:
                            bd.dep(V, t_mm2)
                            bd.dep(V, t_sel)
                            t_y = bd.run(V, lambda e, ob=ob, hc=hc, b_=b_: e.scalar_tensor_tensor(
                                Y1(hc)[:, b_ * 256:(b_ + 1) * 256], ob[:, 0:256],
                                B2G[:, b_:b_ + 1],
                                XLN(hc)[:, b_ * 256:(b_ + 1) * 256], AL.add, AL.add),
                                inc=("s_dv", 1))
                            bank_free[obanks[i]] = t_y
                            y1_toks.append(t_y)
        free_tok["a_ext"] = t_mm2
        for j in range(16):
            wb_i = j % 2
            bd.dep(P, wring_free[wb_i])
            t_wd = bd.run(P, lambda e, j=j, wb_i=wb_i: e.dma_start(
                out=WBUF[:, wb_i * 2048:wb_i * 2048 + 2048],
                in_=w3_t[j].rearrange("p dc q -> p (dc q)")),
                inc=("s_w0" if wb_i == 0 else "s_w1", 16))
            bd.dep(PE, t_wd)
            bd.dep(PE, t_b3)
            bd.deps(PE, y1_toks)
            bnA = BN4[(j % 2) * 2]
            bnB = BN4[(j % 2) * 2 + 1]
            bd.dep(PE, bank_free[bnA])
            bd.dep(PE, bank_free[bnB])
            for dc in range(DC):
                wv = WBUF[:, wb_i * 2048 + dc * 128:wb_i * 2048 + (dc + 1) * 128]
                pe_mm(ps[bnA][:, 0:512], wv, Y1(dc)[:, 0:512], start=(dc == 0), stop=False)
                pe_mm(ps[bnB][:, 0:512], wv, Y1(dc)[:, 512:1024], start=(dc == 0), stop=False)
            t_mja = bd.run(PE, lambda e, bnA=bnA, j=j, B3R=B3R: e.matmul(
                ps[bnA][:, 0:512], B3R[:, j * 128:(j + 1) * 128],
                ONES_B[0:1, 0:512], start=False, stop=True), inc=("s_pe", 1))
            t_mjb = bd.run(PE, lambda e, bnB=bnB, j=j, B3R=B3R: e.matmul(
                ps[bnB][:, 0:512], B3R[:, j * 128:(j + 1) * 128],
                ONES_B[0:1, 0:512], start=False, stop=True), inc=("s_pe", 1))
            wring_free[wb_i] = t_mjb
            for hf, (bn, t_mj) in enumerate(((bnA, t_mja), (bnB, t_mjb))):
                bd.dep(V, t_mj)
                bd.dep(V, free_tok["scr_w3"])
                scr = SCR[:, 512:1024]
                xs = Xc(j)[:, hf * 512:hf * 512 + 512]
                t_e1 = bd.run(V, lambda e, bn=bn, scr=scr, xs=xs: e.scalar_tensor_tensor(
                    scr, ps[bn][:, 0:512], 0.5, xs, AL.mult, AL.add), inc=("s_dv", 1))
                bank_free[bn] = t_e1
                bd.dep(ACT, t_e1)
                t_e2 = bd.run(ACT, lambda e, scr=scr, xs=xs: e.copy(xs, scr), inc=("s_ac", 1))
                free_tok["scr_w3"] = t_e2
                xup_toks.append(t_e2)
                x_ready[j].append(t_e2)

    def emit_attn_pair(p):
        pc0 = p * PW
        A_F = A_BF.bitcast(F32)
        C_BF = C_F.bitcast(BF16)
        XLNP = lambda dc: A_BF[:, dc * 512:(dc + 1) * 512]
        YB = lambda hc: A_BF[:, hc * 512:(hc + 1) * 512]
        PREFIX = lambda hc: B_BF[:, hc * 512:(hc + 1) * 512]
        ACC = lambda hc: C_BF[:, hc * 512:(hc + 1) * 512]
        P1 = lambda hc: D_BF[:, hc * 512:(hc + 1) * 512]
        P2M = lambda hc: D_BF[:, 8192 + hc * 512: 8192 + (hc + 1) * 512]
        LNxb = lambda dc: D_BF[:, (dc % 2) * 512:(dc % 2) * 512 + 512]
        LNx2 = lambda dc: D_BF[:, 1024 + (dc % 2) * 512:1024 + (dc % 2) * 512 + 512]
        SSET = [
            dict(sig=SCR[:, 0:512], a=SCR[:, 512:1024], k=SCR[:, 1024:1536],
                 u=SCR[:, 1536:2048], pf=A_F[:, 6144:6656], x2=A_F[:, 6656:7168],
                 qb=SCB[:, 0:512], vb=SCB[:, 512:1024], q2b=SCB[:, 1024:1536],
                 stb="st1"),
            dict(sig=C_F[:, 4096:4608], a=C_F[:, 4608:5120], k=C_F[:, 5120:5632],
                 u=C_F[:, 5632:6144], pf=C_F[:, 6144:6656], x2=C_F[:, 6656:7168],
                 qb=C_BF[:, 14336:14848], vb=C_BF[:, 14848:15360],
                 q2b=C_BF[:, 15360:15872], stb="st2"),
        ]
        WR = [WBUF[:, 0:2048], WBUF[:, 2048:4096],
              A_BF[:, 8192:10240], A_BF[:, 10240:12288]]
        WSEM = ["s_w0", "s_w1", "s_w2", "s_w3"]

        D_F = D_BF.bitcast(F32)
        lnp_toks = {}
        t_lnp = emit_ln(XLNP, pc0, PW, LNxb, LNx2,
                        lambda i: D_F[:, i * 512:(i + 1) * 512], 2,
                        apply_toks=lnp_toks)

        t_bc_toks = []
        phead_toks = []
        prev_tacc = [None, None]
        prev_lnpf = [None, None]
        prev_p2m = [None, None]
        bd.dep(P, free_tok["a_ext"])
        bd.dep(V, free_tok["S_LOC"])
        for hc in range(H):
            par = hc % 2
            S = SSET[par]
            s_sig, s_a, s_k, s_u = S["sig"], S["a"], S["k"], S["u"]
            s_pf, s_x2 = S["pf"], S["x2"]
            QBF, VBF, Q2BF = S["qb"], S["vb"], S["q2b"]
            stb = S["stb"]
            evac = {}
            for sidx, jp in enumerate((3, 1, 2, 4, 0)):
                wb_i = (p * 80 + hc * 5 + sidx + 2) % 4
                bd.dep(P, wring_free[wb_i])
                t_wd = bd.run(P, lambda e, hc=hc, jp=jp, wb_i=wb_i: e.dma_start(
                    out=WR[wb_i],
                    in_=w_in_t[hc, jp].rearrange("p dc q -> p (dc q)")),
                    inc=(WSEM[wb_i], 16))
                bn = BN4[(hc * 5 + sidx) % 4]
                bd.dep(PE, t_wd)
                bd.dep(PE, bank_free[bn])
                t_mm = None
                for dc in range(DC):
                    bd.deps(PE, lnp_toks.get(dc, []))
                    wv = WR[wb_i][:, dc * 128:(dc + 1) * 128]
                    t_mm = pe_mm(ps[bn][:, 0:512], wv, XLNP(dc),
                                 start=(dc == 0), stop=(dc == DC - 1),
                                 inc=("s_pe", 1) if dc == DC - 1 else None)
                wring_free[wb_i] = t_mm
                pb = ps[bn][:, 0:512]
                if jp == 3:
                    bd.dep(ACT, t_mm)
                    bd.dep(ACT, prev_tacc[par])
                    t1 = bd.run(ACT, lambda e, pb=pb, s_sig=s_sig: e.activation(
                        s_sig, pb, AF.Sigmoid, bias=C_TWO), inc=("s_ac", 1))
                    bank_free[bn] = t1
                    bd.dep(V, t1)
                    evac["a"] = bd.run(V, lambda e, s_a=s_a, s_sig=s_sig: e.tensor_scalar(
                        s_a, s_sig, 0.6, 0.9995, AL.max, AL.min), inc=("s_dv", 1))
                elif jp == 1:
                    bd.dep(ACT, t_mm)
                    t1 = bd.run(ACT, lambda e, pb=pb, s_k=s_k: e.activation(
                        s_k, pb, AF.Tanh), inc=("s_ac", 1))
                    bank_free[bn] = t1
                    evac["k"] = t1
                elif jp == 2:
                    bd.dep(ACT, t_mm)
                    t1 = bd.run(ACT, lambda e, pb=pb, VBF=VBF: e.copy(VBF, pb),
                                inc=("s_ac", 1))
                    bank_free[bn] = t1
                    bd.dep(V, t1)
                    bd.dep(V, evac["k"])
                    evac["u"] = bd.run(V, lambda e, s_u=s_u, s_k=s_k, VBF=VBF:
                                       e.tensor_tensor(s_u, s_k, VBF, AL.mult),
                                       inc=("s_dv", 1))
                elif jp == 4:
                    bd.dep(ACT, t_mm)
                    bd.dep(ACT, evac["a"])
                    t1 = bd.run(ACT, lambda e, pb=pb, s_sig=s_sig: e.activation(
                        s_sig, pb, AF.Sigmoid), inc=("s_ac", 1))
                    bank_free[bn] = t1
                    evac["g"] = t1
                else:
                    bd.dep(ACT, t_mm)
                    bd.dep(ACT, prev_p2m[par])
                    t1 = bd.run(ACT, lambda e, pb=pb, QBF=QBF: e.copy(QBF, pb),
                                inc=("s_ac", 1))
                    t1b = bd.run(ACT, lambda e, pb=pb, Q2BF=Q2BF: e.activation(
                        Q2BF, pb, AF.Square), inc=("s_ac", 1))
                    bank_free[bn] = t1b
                    bd.dep(PE, t1b)
                    bd.dep(PE, bank_free[stb])
                    t_q2 = pe_mm(ps[stb][:, 0:512], ONES_B[:, 0:128], Q2BF, True, True,
                                 inc=("s_pe", 1))
                    bd.dep(ACT, t_q2)
                    t_r1 = bd.run(ACT, lambda e, s_x2=s_x2, stb=stb: e.activation(
                        s_x2, ps[stb][:, 0:512], AF.Ln, bias=C_REPS, scale=C_I128),
                        inc=("s_ac", 1))
                    bank_free[stb] = t_r1
                    bd.dep(ACT, evac["u"])
                    t_r2 = bd.run(ACT, lambda e, s_k=s_k, s_x2=s_x2: e.activation(
                        s_k, s_x2, AF.Exp, scale=C_NH), inc=("s_ac", 1))
                    bd.dep(V, t_r2)
                    bd.dep(V, t_q2)
                    bd.run(V, lambda e, Q2BF=Q2BF, QBF=QBF, s_k=s_k:
                           e.tensor_tensor(Q2BF, QBF, s_k, AL.mult))
                    bd.dep(V, evac["g"])
                    bd.run(V, lambda e, hc=hc, Q2BF=Q2BF, s_sig=s_sig:
                           e.tensor_tensor(P1(hc), Q2BF, s_sig, AL.mult))
                    t_p2m = bd.run(V, lambda e, hc=hc, s_sig=s_sig, VBF=VBF:
                                   e.scalar_tensor_tensor(P2M(hc), s_sig, 1.0, VBF,
                                   AL.subtract, AL.mult), inc=("s_dv", 1))
                    phead_toks.append(t_p2m)
                    prev_p2m[par] = t_p2m

            bd.dep(V, t_ones)
            bd.dep(V, prev_lnpf[par])
            clip_i = evac["a"][1]
            for b_ in range(2):
                for c in range(NCH):
                    cs0 = b_ * 256 + c * CH
                    t_scan = bd.run(V, lambda e, cs0=cs0, s_pf=s_pf, s_a=s_a:
                        e.tensor_tensor_scan(
                        s_pf[:, cs0:cs0 + CH], s_a[:, cs0:cs0 + CH],
                        ONES_F[:, 0:CH], 1.0, AL.mult, AL.mult), base=clip_i,
                        inc=("s_dv", 1) if (b_ == 1 and c == NCH - 1) else None)
            scan_i = bd.vidx(V)
            bd.run(V, lambda e, hc=hc, s_pf=s_pf: e.tensor_copy(PREFIX(hc), s_pf),
                   base=scan_i)
            for b_ in range(2):
                base = hc * 8 + b_ * 4
                for c in range(NCH):
                    bd.run(V, lambda e, base=base, c=c, b_=b_, s_pf=s_pf: e.tensor_copy(
                        PFE[:, base + c:base + c + 1],
                        s_pf[:, b_ * 256 + c * CH + CH - 1: b_ * 256 + c * CH + CH]),
                        base=scan_i)
            for b_ in range(2):
                base = hc * 8 + b_ * 4
                bd.run(V, lambda e, base=base: e.tensor_copy(
                    APF[:, base:base + 1], ONES_F[:, 0:1]))
                for c in range(1, NCH):
                    bd.run(V, lambda e, base=base, c=c: e.tensor_tensor(
                        APF[:, base + c:base + c + 1], APF[:, base + c - 1:base + c],
                        PFE[:, base + c - 1:base + c], AL.mult))
                bd.run(V, lambda e, hc=hc, b_=b_, base=base: e.tensor_tensor(
                    S_LOC[:, hc * 2 + b_:hc * 2 + b_ + 1],
                    APF[:, base + 3:base + 4], PFE[:, base + 3:base + 4], AL.mult))
            bd.dep(ACT, t_scan)
            bd.dep(ACT, phead_toks[-1])
            t_lp = bd.run(ACT, lambda e, s_x2=s_x2, s_pf=s_pf: e.activation(
                s_x2, s_pf, AF.Ln), inc=("s_ac", 1))
            prev_lnpf[par] = t_lp
            t_rp = bd.run(ACT, lambda e, s_sig=s_sig, s_x2=s_x2: e.activation(
                s_sig, s_x2, AF.Exp, scale=C_N1), inc=("s_ac", 1))
            bd.dep(V, t_rp)
            t_term = bd.run(V, lambda e, s_a=s_a, s_sig=s_sig, s_u=s_u:
                            e.scalar_tensor_tensor(s_a, s_sig, 1e6, s_u,
                            AL.min, AL.mult), inc=("s_dv", 1))
            term_i = t_term[1]
            t_acc = None
            for b_ in range(2):
                for c in range(NCH):
                    cs0 = b_ * 256 + c * CH
                    t_acc = bd.run(V, lambda e, hc=hc, cs0=cs0, s_a=s_a:
                        e.tensor_tensor_scan(
                        ACC(hc)[:, cs0:cs0 + CH], ONES_F[:, 0:CH],
                        s_a[:, cs0:cs0 + CH], 0.0, AL.mult, AL.add), base=term_i,
                        inc=("s_dv", 1) if (b_ == 1 and c == NCH - 1) else None)
            for b_ in range(2):
                base = hc * 8 + b_ * 4
                bd.run(V, lambda e, base=base: e.memset(Z0[:, base:base + 1], 0.0))
                for c in range(NCH):
                    accv = ACC(hc)[:, b_ * 256 + c * CH + CH - 1: b_ * 256 + c * CH + CH]
                    dst = (Z0[:, base + c + 1:base + c + 2] if c < NCH - 1
                           else S_LOC[:, 32 + hc * 2 + b_: 32 + hc * 2 + b_ + 1])
                    tok = bd.run(V, lambda e, dst=dst, base=base, c=c, accv=accv:
                                 e.scalar_tensor_tensor(dst,
                                 accv, Z0[:, base + c:base + c + 1],
                                 PFE[:, base + c:base + c + 1], AL.add, AL.mult),
                                 inc=("s_dv", 1) if c == NCH - 1 else None)
            t_bc_toks.append(tok)
            phead_toks.append(t_acc)
            prev_tacc[par] = tok

        bd.deps(SP, t_bc_toks)
        t_s1 = bd.run(SP, lambda e, p=p: e.dma_start(out=sc_in[p], in_=S_LOC[:]),
                      inc=("s_io", 16))
        free_tok["S_LOC"] = t_s1
        bd.dep(P, t_s1)
        t_cc = bd.run(P, lambda e, p=p: e.collective_compute(
            "AllGather", AL.bypass, replica_groups=[list(range(NCORES))],
            ins=[sc_in[p].opt()], outs=[sc_out[p].opt()]), inc=("s_cc", 1))
        bd.dep(SP, t_cc)
        bd.dep(SP, free_tok["S_ALL"])
        t_sa = bd.run(SP, lambda e, p=p: e.dma_start(
            out=S_ALL[:].rearrange("p (r c) -> p r c", r=NCORES),
            in_=sc_out[p].rearrange("(r p) c -> p r c", p=128)), inc=("s_io", 16))

        bd.dep(V, t_sa)
        bd.run(V, lambda e: e.memset(PREF[:, 0:32], 0.0))
        for r in range(NCORES):
            Ar = S_ALL[:, r * 64:r * 64 + 32]
            Br = S_ALL[:, r * 64 + 32:r * 64 + 64]
            bd.run(V, lambda e, Ar=Ar, r=r: e.tensor_tensor(
                CSC[:, 0:32], Ar, PREF[:, r * 32:(r + 1) * 32], AL.mult))
            bd.run(V, lambda e, Br=Br, r=r: e.tensor_tensor(
                PREF[:, (r + 1) * 32:(r + 2) * 32], CSC[:, 0:32], Br, AL.add))
        pp = [CSC[:, 0:32], CSC[:, 32:64]]
        for r in range(NCORES):
            o = SINIT[:] if r == NCORES - 1 else pp[r % 2]
            if r == 0:
                bd.run(V, lambda e, o=o: e.tensor_scalar(
                    o, PREF[:, 0:32], OH[:, 0:1], None, AL.mult))
            else:
                bd.run(V, lambda e, o=o, r=r, i=pp[(r + 1) % 2]: e.scalar_tensor_tensor(
                    o, PREF[:, r * 32:(r + 1) * 32], OH[:, r:r + 1], i, AL.mult, AL.add))
        sr0 = bd.vidx("dve")
        for hc in range(H):
            for b_ in range(2):
                base = hc * 8 + b_ * 4
                sic = SINIT[:, hc * 2 + b_:hc * 2 + b_ + 1]
                for c in range(NCH):
                    bd.run(V, lambda e, base=base, c=c, sic=sic: e.scalar_tensor_tensor(
                        SREAL[:, base + c:base + c + 1], APF[:, base + c:base + c + 1],
                        sic, Z0[:, base + c:base + c + 1], AL.mult, AL.add), base=sr0)
        t_sreal = bd.run(V, lambda e: e.engine_nop(), inc=("s_dv", 1))
        free_tok["S_ALL"] = t_sreal

        y_toks = []
        bd.dep(V, t_sreal)
        bd.deps(V, wring_free)
        g0 = bd.vidx("dve")
        last_t1 = [g0, g0]
        for hc in range(H):
            par = hc % 2
            sa = SSET[par]["a"]
            sk = SSET[par]["k"]
            for b_ in range(2):
                base = hc * 8 + b_ * 4
                for c in range(NCH):
                    cs0 = b_ * 256 + c * CH
                    bd.run(V, lambda e, hc=hc, cs0=cs0, base=base, c=c, sa=sa:
                        e.scalar_tensor_tensor(
                        sa[:, cs0:cs0 + CH], ACC(hc)[:, cs0:cs0 + CH],
                        SREAL[:, base + c:base + c + 1], PREFIX(hc)[:, cs0:cs0 + CH],
                        AL.add, AL.mult), base=last_t1[par])
            t1 = bd.run(V, lambda e, hc=hc, sa=sa, sk=sk: e.tensor_tensor(
                sk, P1(hc), sa, AL.mult), inc=("s_dv", 1))
            last_t1[par] = t1[1]
            y_toks.append(bd.run(V, lambda e, hc=hc, sk=sk: e.tensor_tensor(
                YB(hc), sk, P2M(hc), AL.subtract), inc=("s_dv", 1)))
        free_tok["b_arena"] = y_toks[-1]
        free_tok["d_arena"] = y_toks[-1]

        for j in range(16):
            wb_i = j % 2
            bd.dep(P, wring_free[wb_i])
            t_wd = bd.run(P, lambda e, j=j, wb_i=wb_i: e.dma_start(
                out=WBUF[:, wb_i * 2048:wb_i * 2048 + 2048],
                in_=w_out_t[j].rearrange("p dc q -> p (dc q)")),
                inc=("s_w0" if wb_i == 0 else "s_w1", 16))
            bn = "mmA" if wb_i == 0 else "mmB"
            bd.dep(PE, t_wd)
            bd.dep(PE, bank_free[bn])
            t_mj = None
            for ch in range(DC):
                if ch % 4 == 0:
                    bd.dep(PE, y_toks[min(ch + 3, DC - 1)])
                wv = WBUF[:, wb_i * 2048 + ch * 128:wb_i * 2048 + (ch + 1) * 128]
                t_mj = pe_mm(ps[bn][:, 0:512], wv, YB(ch), start=(ch == 0), stop=(ch == DC - 1),
                             inc=("s_pe", 1) if ch == DC - 1 else None)
            wring_free[wb_i] = t_mj
            bd.dep(V, t_mj)
            bd.dep(V, free_tok["scr_wo"])
            scr = SCR[:, 0:512]
            xs = Xc(j)[:, pc0:pc0 + PW]
            t_e1 = bd.run(V, lambda e, bn=bn, scr=scr, xs=xs: e.scalar_tensor_tensor(
                scr, ps[bn][:, 0:512], 0.0, xs, AL.bypass, AL.add), inc=("s_dv", 1))
            bank_free[bn] = t_e1
            bd.dep(ACT, t_e1)
            t_e2 = bd.run(ACT, lambda e, scr=scr, xs=xs: e.copy(xs, scr), inc=("s_ac", 1))
            free_tok["scr_wo"] = t_e2
            xup_toks.append(t_e2)
            x_ready[j].append(t_e2)

    emit_ffn(0)
    emit_attn_pair(0)
    emit_attn_pair(1)
    emit_ffn(1)

    bd.deps(SP, xup_toks)
    for dc in range(DC):
        bd.run(SP, lambda e, dc=dc: e.dma_start(out=out[dc], in_=Xc(dc)), inc=("s_st", 16))
    bd.dep(SP, ("s_st", bd.cnt["s_st"]))

    with ExitStack() as stack:
        handles = {}
        for name in bd.sem_names:
            handles[name] = stack.enter_context(nc.semaphore(name))
        block = stack.enter_context(nc.Block())

        @block.tensor
        def _(t):
            bd.replay("pe", t, handles)

        @block.scalar
        def _(s):
            bd.replay("act", s, handles)

        @block.vector
        def _(v):
            bd.replay("dve", v, handles)

        @block.gpsimd
        def _(g):
            bd.replay("pool", g, handles)

        @block.sync
        def _(sy):
            bd.replay("sp", sy, handles)

    return nc


def kernel(L0_attn_x, L0_attn_w_in, L0_attn_w_out, L0_ffn_switch_w, L0_ffn_switch_b,
           L0_ffn_w1, L0_ffn_b1, L0_ffn_w2, L0_ffn_b2, L0_ffn_w3, L0_ffn_b3):
    x = np.ascontiguousarray(np.asarray(L0_attn_x, dtype=np.float32))
    w_in = np.ascontiguousarray(np.asarray(L0_attn_w_in, dtype=np.float32))
    w_out = np.ascontiguousarray(np.asarray(L0_attn_w_out, dtype=np.float32))
    sw = np.ascontiguousarray(np.asarray(L0_ffn_switch_w, dtype=np.float32))
    swb = np.ascontiguousarray(np.asarray(L0_ffn_switch_b, dtype=np.float32))
    w1 = np.ascontiguousarray(np.asarray(L0_ffn_w1, dtype=np.float32))
    b1 = np.ascontiguousarray(np.asarray(L0_ffn_b1, dtype=np.float32))
    w2 = np.ascontiguousarray(np.asarray(L0_ffn_w2, dtype=np.float32))
    b2 = np.ascontiguousarray(np.asarray(L0_ffn_b2, dtype=np.float32))
    w3 = np.ascontiguousarray(np.asarray(L0_ffn_w3, dtype=np.float32))
    b3 = np.ascontiguousarray(np.asarray(L0_ffn_b3, dtype=np.float32))

    wt = w_in.reshape(5, H, 128, DC, 128)
    w_in_t = np.ascontiguousarray(wt.transpose(1, 0, 4, 3, 2))
    wo = w_out.reshape(16, 128, DC, 128)
    w_out_t = np.ascontiguousarray(wo.transpose(0, 3, 2, 1))
    w3r = w3.reshape(16, 128, DC, 128)
    w3_t = np.ascontiguousarray(w3r.transpose(0, 3, 2, 1))
    sw3 = sw.reshape(8, 2048, 2048)
    w1T = np.ascontiguousarray(w1.transpose(0, 2, 1))
    w2T = np.ascontiguousarray(w2.transpose(0, 2, 1))
    w12T = np.ascontiguousarray(np.concatenate([w1T, w2T], axis=0).transpose(1, 0, 2))
    b1T = np.ascontiguousarray(b1.T)
    b2T = np.ascontiguousarray(b2.T)
    b3row = np.ascontiguousarray(b3[None, :])
    swb4 = np.ascontiguousarray(np.broadcast_to(swb[None, :], (4, 8)))

    nc = build_nc()

    in_maps = []
    for c in range(NCORES):
        sl = slice(256 * c, 256 * c + 256)
        x_fm = np.ascontiguousarray(
            x[:, sl, :].transpose(2, 0, 1).reshape(DC, 128, T))
        sw_c = np.ascontiguousarray(
            sw3[:, sl, :].transpose(0, 2, 1).reshape(8, DC, 128, TB).transpose(0, 2, 1, 3))
        oh = np.zeros((128, 8), np.float32)
        oh[:, c] = 1.0
        in_maps.append({
            "x_fm": x_fm, "w_in_t": w_in_t, "w_out_t": w_out_t, "w3_t": w3_t,
            "sw_t": sw_c, "w12T": w12T, "b1T": b1T, "b2T": b2T,
            "b3row": b3row, "swb": swb4, "onehot": oh,
        })

    res = run_bass_kernel_spmd(nc, in_maps, core_ids=list(range(NCORES)))

    outp = np.empty((4, 2048, 2048), np.float32)
    for c in range(NCORES):
        r = np.asarray(res.results[c]["out"])
        r = r.reshape(2048, 4, 256).transpose(1, 2, 0)
        outp[:, 256 * c:256 * c + 256, :] = r
    return outp


# version 16
# speedup vs baseline: 1.0452x; 1.0452x over previous
import numpy as np
from collections import defaultdict
from contextlib import ExitStack

import concourse.bass as bass
import concourse.mybir as mybir
from concourse.bass_utils import run_bass_kernel_spmd

F32 = mybir.dt.float32
BF16 = mybir.dt.bfloat16
AL = mybir.AluOpType
AF = mybir.ActivationFunctionType
AX = mybir.AxisListType

D = 2048
DC = 16
T = 1024
TB = 256
B = 4
H = 16
PW = 512
CH = 64
NCH = 4
LN_EPS = 1e-5
RMS_EPS = 1.1920928955078125e-07
NCORES = 8

ENGINES = ("pe", "act", "dve", "pool", "sp")


class Build:

    def __init__(self, nc):
        self.nc = nc
        self.streams = {e: [] for e in ENGINES}
        self.cnt = defaultdict(int)
        self.lastwait = {e: defaultdict(int) for e in ENGINES}
        self.sem_names = []

    def sem(self, name):
        if name not in self.sem_names:
            self.sem_names.append(name)
        return name

    def dep(self, eng, tok):
        if tok is None:
            return
        s, v = tok
        if v <= self.lastwait[eng][s]:
            return
        self.lastwait[eng][s] = v
        self.streams[eng].append(("wait", s, v))

    def deps(self, eng, toks):
        best = {}
        for t in toks:
            if t is None:
                continue
            s, v = t
            if v > best.get(s, -1):
                best[s] = v
        for s, v in best.items():
            self.dep(eng, (s, v))

    SERIALIZED = {"dve": "s_dvs", "act": "s_acs"}

    def run(self, eng, fn, inc=None, base=None):
        tok = None
        if eng in self.SERIALIZED:
            s = self.SERIALIZED[eng]
            self.sem(s)
            idx = self.cnt[s] + 1
            waitv = (idx - 1) if base is None else min(base, idx - 1)
            if waitv > 0:
                self.dep(eng, (s, waitv))
            self.cnt[s] = idx
            if inc is not None:
                tok = (s, idx)
            self.streams[eng].append(("op", fn, (s, 1)))
            return tok
        if inc is not None:
            s, amt = inc
            self.sem(s)
            self.cnt[s] += amt
            tok = (s, self.cnt[s])
        self.streams[eng].append(("op", fn, inc))
        return tok

    def vidx(self, eng):
        return self.cnt[self.SERIALIZED[eng]]

    def replay(self, eng, obj, handles):
        for item in self.streams[eng]:
            if item[0] == "wait":
                _, s, v = item
                obj.wait_ge(handles[s], v)
            else:
                _, fn, inc = item
                inst = fn(obj)
                if inc is not None:
                    s, amt = inc
                    inst.then_inc(handles[s], amt)


def build_nc():
    nc = bass.Bass("TRN2", target_bir_lowering=False, debug=False)

    def dinp(name, shape, dt=F32):
        return nc.dram_tensor(name, list(shape), dt, kind="ExternalInput").ap()

    x_fm = dinp("x_fm", [DC, 128, T])
    w_in_t = dinp("w_in_t", [H, 5, 128, DC, 128])
    w_out_t = dinp("w_out_t", [16, 128, DC, 128])
    w3_t = dinp("w3_t", [16, 128, DC, 128])
    sw_t = dinp("sw_t", [8, 128, DC, TB])
    w12T_in = dinp("w12T", [128, 16, 128])
    b1T_in = dinp("b1T", [128, 8])
    b2T_in = dinp("b2T", [128, 8])
    b3row_in = dinp("b3row", [1, D])
    swb_in = dinp("swb", [4, 8])
    onehot_in = dinp("onehot", [128, 8])

    out = nc.dram_tensor("out", [DC, 128, T], F32, kind="ExternalOutput").ap()

    rt_in = [nc.dram_tensor(f"rt_in{i}", [32], F32).ap() for i in range(2)]
    rt_out = [nc.dram_tensor(f"rt_out{i}", [32], F32).ap() for i in range(2)]
    sc_in = [nc.dram_tensor(f"sc_in{i}", [128, 64], F32).ap() for i in range(2)]
    sc_out = [nc.dram_tensor(f"sc_out{i}", [NCORES * 128, 64], F32).ap() for i in range(2)]
    selb = [nc.dram_tensor(f"selb{i}", [4, 40], F32).ap() for i in range(2)]

    def salloc(name, shape, dt):
        return nc.alloc_sbuf_tensor(name, list(shape), dt)

    X = salloc("X", [128, DC * T], F32)
    A_BF = salloc("A_BF", [128, 16384], BF16)
    B_BF = salloc("B_BF", [128, 8192], BF16)
    C_F = salloc("C_F", [128, 8192], F32)
    D_BF = salloc("D_BF", [128, 16384], BF16)
    WBUF = salloc("WBUF", [128, 2 * 2048], BF16)
    SCR = salloc("SCR", [128, 4 * 512], F32)
    SCB = salloc("SCB", [128, 3 * 512], BF16)
    ONES_F = salloc("ONES_F", [128, 128], F32)
    ONES_B = salloc("ONES_B", [128, 512], BF16)
    CONSTS = salloc("CONSTS", [128, 6], F32)
    S_LOC = salloc("S_LOC", [128, 64], F32)
    S_ALL = salloc("S_ALL", [128, NCORES * 64], F32)
    PREF = salloc("PREF", [128, 9 * 32], F32)
    SINIT = salloc("SINIT", [128, 32], F32)
    PFE = salloc("PFE", [128, 128], F32)
    APF = salloc("APF", [128, 128], F32)
    Z0 = salloc("Z0", [128, 128], F32)
    SREAL = salloc("SREAL", [128, 128], F32)
    CSC = salloc("CSC", [128, 64], F32)
    LGS = salloc("LGS", [1, 32], F32)
    LG = salloc("LG", [4, 64], F32)
    MSK = salloc("MSK", [4, 40], F32)
    GROW = salloc("GROW", [1, 160], F32)
    GSEL = salloc("GSEL", [128, 160], F32)
    B1T = salloc("B1T", [128, 8], F32)
    B2T = salloc("B2T", [128, 8], F32)
    B1S = salloc("B1S", [128, 8], F32)
    B2G = salloc("B2G", [128, 4], F32)
    SWB = salloc("SWB", [4, 8], F32)
    OH = salloc("OH", [128, 8], F32)
    SELS = salloc("SELS", [128, 2 * 128], BF16)
    SELF = salloc("SELF", [128, 2], F32)

    ps = {}
    def palloc(name, shape):
        ps[name] = nc.alloc_psum_tensor(name, list(shape), F32)
        return ps[name]

    palloc("mmA", [128, 512])
    palloc("mmB", [128, 512])
    palloc("bcA", [128, 512])
    palloc("bcB", [128, 512])
    palloc("st1", [128, 1024])
    palloc("st2", [128, 1024])

    bd = Build(nc)
    V, P, ACT, PE, SP = "dve", "pool", "act", "pe", "sp"
    for s in ("s_xin", "s_sm", "s_exw", "s_b3", "s_io", "s_w0", "s_w1",
              "s_w2", "s_w3", "s_sw0", "s_sw1", "s_pe", "s_dv", "s_ac",
              "s_cc", "s_st", "s_dvs", "s_acs"):
        bd.sem(s)

    bank_free = {k: None for k in ps}
    wring_free = [None, None, None, None]
    BN4 = ["mmA", "mmB", "bcA", "bcB"]
    free_tok = defaultdict(lambda: None)

    def Xc(dc):
        return X[:, dc * T:(dc + 1) * T]

    def pe_mm(bank_ap, lhsT, rhs, start, stop, inc=None):
        return bd.run(PE, lambda e, bank_ap=bank_ap, lhsT=lhsT, rhs=rhs, start=start,
                      stop=stop: e.matmul(bank_ap, lhsT, rhs, start=start, stop=stop),
                      inc=inc)

    tx_toks = []
    for dc in range(DC):
        tx_toks.append(bd.run(SP, lambda e, dc=dc: e.dma_start(out=Xc(dc), in_=x_fm[dc]),
                              inc=("s_xin", 16)))
    t_smalls = None
    for (dst, src) in ((B1T, b1T_in), (B2T, b2T_in), (SWB, swb_in), (OH, onehot_in)):
        t_smalls = bd.run(SP, lambda e, d=dst, s=src: e.dma_start(out=d[:], in_=s),
                          inc=("s_sm", 16))
    bd.run(V, lambda e: e.memset(ONES_F[:], 1.0))
    bd.run(V, lambda e: e.memset(CONSTS[:, 0:1], LN_EPS))
    bd.run(V, lambda e: e.memset(CONSTS[:, 1:2], 2.0))
    bd.run(V, lambda e: e.memset(CONSTS[:, 2:3], RMS_EPS))
    bd.run(V, lambda e: e.memset(CONSTS[:, 3:4], 1.0 / 128.0))
    bd.run(V, lambda e: e.memset(CONSTS[:, 4:5], -0.5))
    bd.run(V, lambda e: e.memset(CONSTS[:, 5:6], -1.0))
    t_ones = bd.run(V, lambda e: e.memset(ONES_B[:], 1.0), inc=("s_dv", 1))
    bd.dep(ACT, t_ones)
    C_EPS = CONSTS[:, 0:1]
    C_TWO = CONSTS[:, 1:2]
    C_REPS = CONSTS[:, 2:3]
    C_I128 = CONSTS[:, 3:4]
    C_NH = CONSTS[:, 4:5]
    C_N1 = CONSTS[:, 5:6]

    xup_toks = list(tx_toks)
    x_ready = {dc: [("s_xin", 16 * DC)] for dc in range(DC)}

    def emit_ln(xln_dst, col0, width, xbf_scr, x2bf_scr, ascr, nscr, apply_toks=None):
        nh = (width + 511) // 512
        bd.dep(PE, t_ones)
        bd.dep(PE, bank_free["st1"])
        bd.dep(PE, bank_free["st2"])
        pe_tok = [None, None]
        t_stats = None
        for dc in range(DC):
            xs = Xc(dc)[:, col0:col0 + width]
            bd.dep(V, pe_tok[dc % 2])
            bd.deps(V, x_ready[dc])
            t_c = bd.run(V, lambda e, dc=dc, xs=xs: e.tensor_copy(xbf_scr(dc), xs),
                         inc=("s_dv", 1))
            t_s = bd.run(V, lambda e, dc=dc: e.tensor_tensor(
                x2bf_scr(dc), xbf_scr(dc), xbf_scr(dc), AL.mult),
                base=t_c[1], inc=("s_dv", 1))
            bd.dep(PE, t_c)
            bd.dep(PE, t_s)
            for h in range(nh):
                w = min(512, width - h * 512)
                pe_mm(ps["st1"][:, h * 512:h * 512 + w], ONES_B[:, 0:128],
                      xbf_scr(dc)[:, h * 512:h * 512 + w], start=(dc == 0), stop=(dc == DC - 1))
                t_stats = pe_mm(ps["st2"][:, h * 512:h * 512 + w], ONES_B[:, 0:128],
                                x2bf_scr(dc)[:, h * 512:h * 512 + w],
                                start=(dc == 0), stop=(dc == DC - 1), inc=("s_pe", 1))
            pe_tok[dc % 2] = t_stats
        t_last = None
        t_v = None
        for h in range(nh):
            hs = h * 512
            w = min(512, width - hs)
            SM = SCR[0:1, 0:w]
            SV = SCR[0:1, 512:512 + w]
            S3 = SCR[0:1, 1024:1024 + w]
            S4 = SCR[0:1, 1536:1536 + w]
            bd.dep(V, t_stats)
            if h > 0:
                bd.dep(V, t_bc)
            bd.run(V, lambda e, hs=hs, w=w, SM=SM: e.tensor_scalar(
                SM, ps["st1"][0:1, hs:hs + w], 1.0 / D, None, AL.mult))
            t_v = bd.run(V, lambda e, hs=hs, w=w, SV=SV: e.tensor_scalar(
                SV, ps["st2"][0:1, hs:hs + w], 1.0 / D, None, AL.mult),
                inc=("s_dv", 1))
            bd.run(V, lambda e, SM=SM, S3=S3: e.tensor_tensor(S3, SM, SM, AL.mult))
            t_var = bd.run(V, lambda e, SV=SV, S3=S3, S4=S4: e.tensor_tensor(
                S4, SV, S3, AL.subtract), inc=("s_dv", 1))
            bd.dep(ACT, t_var)
            bd.run(ACT, lambda e, S3=S3, S4=S4: e.activation(
                S3, S4, AF.Ln, bias=CONSTS[0:1, 0:1]), inc=("s_ac", 1))
            t_sd = bd.run(ACT, lambda e, SV=SV, S3=S3: e.activation(
                SV, S3, AF.Exp, scale=CONSTS[0:1, 4:5]), inc=("s_ac", 1))
            bd.dep(V, t_sd)
            bd.run(V, lambda e, SM=SM, SV=SV, S3=S3: e.tensor_tensor(
                S3, SM, SV, AL.mult))
            t_rows = bd.run(V, lambda e, S3=S3, S4=S4: e.tensor_scalar(
                S4, S3, -1.0, None, AL.mult), inc=("s_dv", 1))
            bd.dep(PE, bank_free["bcA"])
            bd.dep(PE, bank_free["bcB"])
            bd.dep(PE, t_rows)
            pe_mm(ps["bcA"][:, 0:w], ONES_F[0:1, :], SV, True, True)
            t_bc = pe_mm(ps["bcB"][:, 0:w], ONES_F[0:1, :], S4,
                         True, True, inc=("s_pe", 1))
            bd.dep(V, t_bc)
            mt = {}
            at = {}
            g0 = bd.vidx("dve")
            for dc in range(DC):
                xs = Xc(dc)[:, col0 + hs:col0 + hs + w]
                scr = ascr(dc % nscr)[:, 0:w]
                bd.deps(V, x_ready[dc])
                mt[dc] = bd.run(V, lambda e, xs=xs, scr=scr, w=w: e.tensor_tensor(
                    scr, xs, ps["bcA"][:, 0:w], AL.mult),
                    base=(at[dc - nscr][1] if dc - nscr in at else g0),
                    inc=("s_dv", 1))
                if dc >= 1:
                    d2 = dc - 1
                    dst = xln_dst(d2)[:, hs:hs + w]
                    scr2 = ascr(d2 % nscr)[:, 0:w]
                    at[d2] = bd.run(V, lambda e, dst=dst, scr2=scr2, w=w: e.tensor_tensor(
                        dst, scr2, ps["bcB"][:, 0:w], AL.add), base=mt[d2][1],
                        inc=("s_dv", 1))
                    if apply_toks is not None:
                        apply_toks.setdefault(d2, []).append(at[d2])
            d2 = DC - 1
            dst = xln_dst(d2)[:, hs:hs + w]
            scr2 = ascr(d2 % nscr)[:, 0:w]
            t_last = bd.run(V, lambda e, dst=dst, scr2=scr2, w=w: e.tensor_tensor(
                dst, scr2, ps["bcB"][:, 0:w], AL.add), base=mt[d2][1],
                inc=("s_dv", 1))
            if apply_toks is not None:
                apply_toks.setdefault(d2, []).append(t_last)
            bank_free["bcA"] = t_last
            bank_free["bcB"] = t_last
        bank_free["st1"] = t_v
        bank_free["st2"] = t_v
        return t_last

    sw_free = [None, None]

    def emit_ffn(call):
        XLN = lambda dc: A_BF[:, dc * 1024:(dc + 1) * 1024]
        SWbuf = lambda i: D_BF[:, (i % 2) * 4096:(i % 2) * 4096 + 4096]
        Pbuf = lambda i: D_BF[:, 8192 + (i % 2) * 1024: 8192 + (i % 2) * 1024 + 1024]
        LNxb = lambda dc: D_BF[:, 8192 + (dc % 2) * 1024: 8192 + (dc % 2) * 1024 + 1024]
        LNx2 = lambda dc: D_BF[:, 10240 + (dc % 2) * 1024: 10240 + (dc % 2) * 1024 + 1024]
        EXW1 = lambda e_: B_BF[:, e_ * 128:(e_ + 1) * 128]
        EXW2 = lambda e_: B_BF[:, 1024 + e_ * 128: 1024 + (e_ + 1) * 128]
        SELW1 = lambda bk: B_BF[:, 2048 + bk * 128: 2048 + (bk + 1) * 128]
        SELW2 = lambda bk: B_BF[:, 3072 + bk * 128: 3072 + (bk + 1) * 128]
        HB = lambda i: B_BF[:, 4096 + (i % 2) * 256: 4096 + (i % 2) * 256 + 256]
        C_bf = C_F.bitcast(BF16)
        Y1 = lambda dc: C_bf[:, dc * 1024:(dc + 1) * 1024]

        bd.deps(V, [wring_free[2], wring_free[3]])
        D_F = D_BF.bitcast(F32)
        ln_toks = {}
        t_ln = emit_ln(XLN, 0, T, LNxb, LNx2,
                       lambda i: D_F[:, 4096 + i * 512: 4096 + (i + 1) * 512], 4,
                       apply_toks=ln_toks)

        bd.dep(P, free_tok["b_arena"])
        t_exw2 = bd.run(P, lambda e: e.dma_start(
            out=B_BF[:, 0:2048],
        in_=w12T_in.rearrange("p e q -> p (e q)")),
            inc=("s_exw", 16))

        sw_dma = [None, None]
        lg_toks = []
        pbuf_free = [None, None]
        for e_ in range(8):
            if e_ < 2:
                bd.dep(P, sw_free[e_ % 2])
                bd.dep(P, free_tok["d_arena"])
                sw_dma[e_ % 2] = bd.run(P, lambda e, e_=e_: e.dma_start(
                    out=SWbuf(e_),
                    in_=sw_t[e_].rearrange("p dc t -> p (dc t)")),
                    inc=("s_sw0" if e_ % 2 == 0 else "s_sw1", 16))
            stats = "st1" if e_ % 2 == 0 else "st2"
            bd.dep(V, sw_dma[e_ % 2])
            t_mm = None
            rg0 = bd.vidx("dve")
            for dc in range(DC):
                swv = SWbuf(e_)[:, dc * 256:(dc + 1) * 256]
                sw_b = swv.rearrange("p (o t) -> p o t", o=1).broadcast_to([128, 4, 256])
                pb = Pbuf(dc)
                bd.dep(V, pbuf_free[dc % 2])
                bd.deps(V, ln_toks.get(dc, []))
                t_p = bd.run(V, lambda e, dc=dc, sw_b=sw_b, pb=pb, e_=e_: e.tensor_tensor(
                    pb.rearrange("p (b t) -> p b t", b=B),
                    XLN(dc).rearrange("p (b t) -> p b t", b=B), sw_b, AL.mult),
                    base=rg0, inc=("s_dv", 1))
                bd.dep(PE, bank_free[stats])
                bd.dep(PE, t_p)
                for hf in range(2):
                    t_mm = pe_mm(ps[stats][:, hf * 512:(hf + 1) * 512], ONES_B[:, 0:128],
                                 pb[:, hf * 512:(hf + 1) * 512],
                                 start=(dc == 0), stop=(dc == DC - 1),
                                 inc=("s_pe", 1) if hf == 1 else None)
                pbuf_free[dc % 2] = t_mm
                t_p_last = t_p
            sw_free[e_ % 2] = t_p_last
            if e_ + 2 < 8:
                bd.dep(P, sw_free[e_ % 2])
                sw_dma[e_ % 2] = bd.run(P, lambda e, e_=e_: e.dma_start(
                    out=SWbuf(e_ + 2),
                    in_=sw_t[e_ + 2].rearrange("p dc t -> p (dc t)")),
                    inc=("s_sw0" if e_ % 2 == 0 else "s_sw1", 16))
            bd.dep(V, t_mm)
            bd.dep(V, free_tok["LGS"])
            lt = None
            for b_ in range(B):
                lt = bd.run(V, lambda e, e_=e_, b_=b_, stats=stats: e.tensor_reduce(
                    LGS[:, b_ * 8 + e_: b_ * 8 + e_ + 1],
                    ps[stats][0:1, b_ * 256:(b_ + 1) * 256], AX.X, AL.add),
                    inc=("s_dv", 1) if b_ == B - 1 else None)
            bank_free[stats] = lt
            lg_toks.append(lt)

        B3R = D_BF[0:1, 0:2048]
        bd.dep(P, sw_free[0])
        t_b3 = bd.run(P, lambda e: e.dma_start(out=B3R, in_=b3row_in), inc=("s_b3", 16))

        bd.deps(SP, lg_toks)
        t_s1 = bd.run(SP, lambda e: e.dma_start(
            out=rt_in[call].rearrange("(o c) -> o c", o=1), in_=LGS[0:1, :]), inc=("s_io", 16))
        free_tok["LGS"] = t_s1
        bd.dep(P, t_s1)
        t_cc = bd.run(P, lambda e: e.collective_compute(
            "AllReduce", AL.add, replica_groups=[list(range(NCORES))],
            ins=[rt_in[call].opt()], outs=[rt_out[call].opt()]), inc=("s_cc", 1))
        bd.dep(SP, t_cc)
        t_lg = bd.run(SP, lambda e: e.dma_start(
            out=LG[:, 0:8], in_=rt_out[call].rearrange("(b e) -> b e", b=4)),
            inc=("s_io", 16))

        lg0 = LG[:, 0:8]; ex_ = LG[:, 8:16]; pr = LG[:, 16:24]; pr2 = LG[:, 24:32]
        cs = LG[:, 32:40]; tmp8 = LG[:, 40:48]; mx = LG[:, 48:49]; nmx = LG[:, 49:50]
        sm = LG[:, 50:51]; rs = LG[:, 51:52]
        msk1 = MSK[:, 0:8]; msk2 = MSK[:, 8:16]; gm1 = MSK[:, 16:24]
        gm2 = MSK[:, 24:32]; gmc = MSK[:, 32:40]
        bd.dep(V, t_lg)
        bd.dep(V, t_smalls)
        bd.dep(V, free_tok["MSK"])
        bd.run(V, lambda e: e.tensor_tensor(lg0, lg0, SWB[:], AL.add))
        bd.run(V, lambda e: e.reduce_max(mx, lg0, AX.X))
        t_nm = bd.run(V, lambda e: e.tensor_scalar(nmx, mx, -1.0, None, AL.mult), inc=("s_dv", 1))
        bd.dep(ACT, t_nm)
        t_ex = bd.run(ACT, lambda e: e.activation(ex_, lg0, AF.Exp, bias=nmx[:, 0:1]),
                      inc=("s_ac", 1))
        bd.dep(V, t_ex)
        bd.run(V, lambda e: e.reduce_sum(sm, ex_, AX.X))
        bd.run(V, lambda e: e.reciprocal(rs, sm))
        bd.run(V, lambda e: e.tensor_scalar(pr, ex_, rs[:, 0:1], None, AL.mult))
        bd.run(V, lambda e: e.reduce_max(mx, pr, AX.X))
        bd.run(V, lambda e: e.tensor_scalar(tmp8, pr, mx[:, 0:1], None, AL.is_ge))
        bd.run(V, lambda e: e.tensor_tensor_scan(cs, ONES_F[0:4, 0:8], tmp8, 0.0, AL.mult, AL.add))
        bd.run(V, lambda e: e.tensor_scalar(cs, cs, 1.0, None, AL.is_equal))
        bd.run(V, lambda e: e.tensor_tensor(msk1, tmp8, cs, AL.mult))
        bd.run(V, lambda e: e.tensor_tensor(gm1, pr, msk1, AL.mult))
        bd.run(V, lambda e: e.tensor_scalar(tmp8, msk1, -1.0, 1.0, AL.mult, AL.add))
        bd.run(V, lambda e: e.tensor_tensor(pr2, pr, tmp8, AL.mult))
        bd.run(V, lambda e: e.reduce_max(mx, pr2, AX.X))
        bd.run(V, lambda e: e.tensor_scalar(tmp8, pr2, mx[:, 0:1], None, AL.is_ge))
        bd.run(V, lambda e: e.tensor_tensor_scan(cs, ONES_F[0:4, 0:8], tmp8, 0.0, AL.mult, AL.add))
        bd.run(V, lambda e: e.tensor_scalar(cs, cs, 1.0, None, AL.is_equal))
        bd.run(V, lambda e: e.tensor_tensor(msk2, tmp8, cs, AL.mult))
        bd.run(V, lambda e: e.tensor_tensor(gm2, pr2, msk2, AL.mult))
        t_msk = bd.run(V, lambda e: e.tensor_tensor(gmc, gm1, gm2, AL.add), inc=("s_dv", 1))

        bd.dep(SP, t_msk)
        t_sb = bd.run(SP, lambda e: e.dma_start(out=selb[call], in_=MSK[:, 0:40]),
                      inc=("s_io", 16))
        free_tok["MSK"] = t_sb
        bd.dep(SP, t_sb)
        t_sr = bd.run(SP, lambda e: e.dma_start(
            out=GROW[:], in_=selb[call].rearrange("b c -> (b c)")[None, :]), inc=("s_io", 16))
        bd.dep(PE, t_sr)
        bd.dep(PE, bank_free["bcA"])
        t_gb = pe_mm(ps["bcA"][:, 0:160], ONES_F[0:1, :], GROW[:], True, True, inc=("s_pe", 1))
        bd.dep(V, t_gb)
        t_gs = bd.run(V, lambda e: e.tensor_copy(GSEL[:, 0:160], ps["bcA"][:, 0:160]),
                      inc=("s_dv", 1))
        bank_free["bcA"] = t_gs

        bd.dep(V, t_exw2)
        sel_last = None
        chains = []
        for b_ in range(B):
            for k_ in range(2):
                bk = b_ * 2 + k_
                moff = b_ * 40 + (0 if k_ == 0 else 8)
                goff = b_ * 40 + (16 if k_ == 0 else 24)
                chains.append((SELW1(bk), EXW1, moff))
                chains.append((SELW2(bk), EXW2, goff))
        for grp in range(0, len(chains), 4):
            batch = chains[grp:grp + 4]
            prev = [None] * len(batch)
            for e_ in range(8):
                for lane, (dst, src, off) in enumerate(batch):
                    pp = [B_BF[:, 6144 + lane * 256:6144 + lane * 256 + 128],
                          B_BF[:, 6144 + lane * 256 + 128:6144 + lane * 256 + 256]]
                    o = dst if e_ == 7 else pp[e_ % 2]
                    if e_ == 0:
                        prev[lane] = bd.run(V, lambda e, o=o, src=src, off=off:
                            e.tensor_scalar(o, src(0), GSEL[:, off:off + 1], None,
                            AL.mult), base=max(free_tok["sel_lane"] or 0, t_gs[1]),
                            inc=("s_dv", 1))
                    else:
                        prev[lane] = bd.run(V, lambda e, o=o, src=src, e_=e_, off=off,
                            i=pp[(e_ + 1) % 2]: e.scalar_tensor_tensor(
                            o, src(e_), GSEL[:, off + e_: off + e_ + 1], i,
                            AL.mult, AL.add), base=prev[lane][1], inc=("s_dv", 1))
            sel_last = prev[-1]
            free_tok["sel_lane"] = max(t[1] for t in prev)
        bchains = []
        for b_ in range(B):
            for k_ in range(2):
                bk = b_ * 2 + k_
                moff = b_ * 40 + (0 if k_ == 0 else 8)
                bchains.append((B1S[:, bk:bk + 1], B1T, moff))
        for b_ in range(B):
            bchains.append((B2G[:, b_:b_ + 1], B2T, b_ * 40 + 32))
        for grp in range(0, len(bchains), 4):
            batch = bchains[grp:grp + 4]
            prev = [None] * len(batch)
            for e_ in range(8):
                for lane, (dst, tsrc, off) in enumerate(batch):
                    ppf = [PREF[:, 272 + 2 * lane:273 + 2 * lane],
                           PREF[:, 273 + 2 * lane:274 + 2 * lane]]
                    o = dst if e_ == 7 else ppf[e_ % 2]
                    if e_ == 0:
                        prev[lane] = bd.run(V, lambda e, o=o, tsrc=tsrc, off=off:
                            e.tensor_scalar(o, tsrc[:, 0:1], GSEL[:, off:off + 1], None,
                            AL.mult), base=max(free_tok["selb_lane"] or 0, t_gs[1]),
                            inc=("s_dv", 1))
                    else:
                        prev[lane] = bd.run(V, lambda e, o=o, tsrc=tsrc, e_=e_, off=off,
                            i=ppf[(e_ + 1) % 2]: e.scalar_tensor_tensor(
                            o, tsrc[:, e_:e_ + 1], GSEL[:, off + e_: off + e_ + 1], i,
                            AL.mult, AL.add), base=prev[lane][1], inc=("s_dv", 1))
            sel_last = prev[-1]
            free_tok["selb_lane"] = max(t[1] for t in prev)
        t_sel = sel_last

        y1_toks = []
        obanks = ["mmA", "mmB", "bcA", "bcB"]
        hbanks = ["st1", "st2"]
        HB8 = lambda s: B_BF[:, 4096 + s * 256: 4096 + (s + 1) * 256]
        hb_i = 0
        bd.dep(PE, t_sel)
        bd.dep(PE, t_ln)
        for b_ in range(B):
            for g in range(4):
                hcs = [g * 4 + i for i in range(4)]
                gel = {}
                mm1t = {}
                for k_ in range(2):
                    bk = b_ * 2 + k_
                    for i, hc in enumerate(hcs):
                        s = k_ * 4 + i
                        hname = hbanks[hb_i % 2]
                        hb = ps[hname]
                        bd.dep(PE, bank_free[hname])
                        t_mm1 = pe_mm(hb[:, 0:256], SELW1(bk),
                                      XLN(hc)[:, b_ * 256:(b_ + 1) * 256], True, True,
                                      inc=("s_pe", 1))
                        bd.dep(ACT, t_mm1)
                        t_h = bd.run(ACT, lambda e, hb=hb, bk=bk, s=s: e.activation(
                            HB8(s), hb[:, 0:256], AF.Gelu, bias=B1S[:, bk:bk + 1]),
                            inc=("s_ac", 1))
                        bank_free[hname] = t_h
                        gel[s] = t_h
                        hb_i += 1
                for k_ in range(2):
                    bk = b_ * 2 + k_
                    for i, hc in enumerate(hcs):
                        s = k_ * 4 + i
                        ob = ps[obanks[i]]
                        bd.dep(PE, gel[s])
                        if k_ == 0:
                            bd.dep(PE, bank_free[obanks[i]])
                        t_mm2 = pe_mm(ob[:, 0:256], SELW2(bk), HB8(s),
                                      start=(k_ == 0), stop=(k_ == 1),
                                      inc=("s_pe", 1))
                        if k_ == 1:
                            bd.dep(V, t_mm2)
                            bd.dep(V, t_sel)
                            t_y = bd.run(V, lambda e, ob=ob, hc=hc, b_=b_: e.scalar_tensor_tensor(
                                Y1(hc)[:, b_ * 256:(b_ + 1) * 256], ob[:, 0:256],
                                B2G[:, b_:b_ + 1],
                                XLN(hc)[:, b_ * 256:(b_ + 1) * 256], AL.add, AL.add),
                                inc=("s_dv", 1))
                            bank_free[obanks[i]] = t_y
                            y1_toks.append(t_y)
        free_tok["a_ext"] = t_mm2
        for j in range(16):
            wb_i = j % 2
            bd.dep(P, wring_free[wb_i])
            t_wd = bd.run(P, lambda e, j=j, wb_i=wb_i: e.dma_start(
                out=WBUF[:, wb_i * 2048:wb_i * 2048 + 2048],
                in_=w3_t[j].rearrange("p dc q -> p (dc q)")),
                inc=("s_w0" if wb_i == 0 else "s_w1", 16))
            bd.dep(PE, t_wd)
            bd.dep(PE, t_b3)
            bd.deps(PE, y1_toks)
            bnA = BN4[(j % 2) * 2]
            bnB = BN4[(j % 2) * 2 + 1]
            bd.dep(PE, bank_free[bnA])
            bd.dep(PE, bank_free[bnB])
            for dc in range(DC):
                wv = WBUF[:, wb_i * 2048 + dc * 128:wb_i * 2048 + (dc + 1) * 128]
                pe_mm(ps[bnA][:, 0:512], wv, Y1(dc)[:, 0:512], start=(dc == 0), stop=False)
                pe_mm(ps[bnB][:, 0:512], wv, Y1(dc)[:, 512:1024], start=(dc == 0), stop=False)
            t_mja = bd.run(PE, lambda e, bnA=bnA, j=j, B3R=B3R: e.matmul(
                ps[bnA][:, 0:512], B3R[:, j * 128:(j + 1) * 128],
                ONES_B[0:1, 0:512], start=False, stop=True), inc=("s_pe", 1))
            t_mjb = bd.run(PE, lambda e, bnB=bnB, j=j, B3R=B3R: e.matmul(
                ps[bnB][:, 0:512], B3R[:, j * 128:(j + 1) * 128],
                ONES_B[0:1, 0:512], start=False, stop=True), inc=("s_pe", 1))
            wring_free[wb_i] = t_mjb
            for hf, (bn, t_mj) in enumerate(((bnA, t_mja), (bnB, t_mjb))):
                bd.dep(V, t_mj)
                bd.dep(V, free_tok["scr_w3"])
                scr = SCR[:, 512:1024]
                xs = Xc(j)[:, hf * 512:hf * 512 + 512]
                t_e1 = bd.run(V, lambda e, bn=bn, scr=scr, xs=xs: e.scalar_tensor_tensor(
                    scr, ps[bn][:, 0:512], 0.5, xs, AL.mult, AL.add), inc=("s_dv", 1))
                bank_free[bn] = t_e1
                bd.dep(ACT, t_e1)
                t_e2 = bd.run(ACT, lambda e, scr=scr, xs=xs: e.copy(xs, scr), inc=("s_ac", 1))
                free_tok["scr_w3"] = t_e2
                xup_toks.append(t_e2)
                x_ready[j].append(t_e2)

    def emit_attn_pair(p):
        pc0 = p * PW
        A_F = A_BF.bitcast(F32)
        C_BF = C_F.bitcast(BF16)
        XLNP = lambda dc: A_BF[:, dc * 512:(dc + 1) * 512]
        YB = lambda hc: A_BF[:, hc * 512:(hc + 1) * 512]
        PREFIX = lambda hc: B_BF[:, hc * 512:(hc + 1) * 512]
        ACC = lambda hc: C_BF[:, hc * 512:(hc + 1) * 512]
        P1 = lambda hc: D_BF[:, hc * 512:(hc + 1) * 512]
        P2M = lambda hc: D_BF[:, 8192 + hc * 512: 8192 + (hc + 1) * 512]
        LNxb = lambda dc: D_BF[:, (dc % 2) * 512:(dc % 2) * 512 + 512]
        LNx2 = lambda dc: D_BF[:, 1024 + (dc % 2) * 512:1024 + (dc % 2) * 512 + 512]
        SSET = [
            dict(sig=SCR[:, 0:512], a=SCR[:, 512:1024], k=SCR[:, 1024:1536],
                 u=SCR[:, 1536:2048], pf=A_F[:, 6144:6656], x2=A_F[:, 6656:7168],
                 qb=SCB[:, 0:512], vb=SCB[:, 512:1024], q2b=SCB[:, 1024:1536],
                 stb="st1"),
            dict(sig=C_F[:, 4096:4608], a=C_F[:, 4608:5120], k=C_F[:, 5120:5632],
                 u=C_F[:, 5632:6144], pf=C_F[:, 6144:6656], x2=C_F[:, 6656:7168],
                 qb=C_BF[:, 14336:14848], vb=C_BF[:, 14848:15360],
                 q2b=C_BF[:, 15360:15872], stb="st2"),
        ]
        WR = [WBUF[:, 0:2048], WBUF[:, 2048:4096],
              A_BF[:, 8192:10240], A_BF[:, 10240:12288]]
        WSEM = ["s_w0", "s_w1", "s_w2", "s_w3"]

        D_F = D_BF.bitcast(F32)
        lnp_toks = {}
        t_lnp = emit_ln(XLNP, pc0, PW, LNxb, LNx2,
                        lambda i: D_F[:, i * 512:(i + 1) * 512], 2,
                        apply_toks=lnp_toks)

        t_bc_toks = []
        phead_toks = []
        prev_tacc = [None, None]
        prev_lnpf = [None, None]
        prev_p2m = [None, None]
        bd.dep(P, free_tok["a_ext"])
        bd.dep(V, free_tok["S_LOC"])
        for hc in range(H):
            par = hc % 2
            S = SSET[par]
            s_sig, s_a, s_k, s_u = S["sig"], S["a"], S["k"], S["u"]
            s_pf, s_x2 = S["pf"], S["x2"]
            QBF, VBF, Q2BF = S["qb"], S["vb"], S["q2b"]
            stb = S["stb"]
            evac = {}
            for sidx, jp in enumerate((3, 1, 2, 4, 0)):
                wb_i = (p * 80 + hc * 5 + sidx + 2) % 4
                bd.dep(P, wring_free[wb_i])
                t_wd = bd.run(P, lambda e, hc=hc, jp=jp, wb_i=wb_i: e.dma_start(
                    out=WR[wb_i],
                    in_=w_in_t[hc, jp].rearrange("p dc q -> p (dc q)")),
                    inc=(WSEM[wb_i], 16))
                bn = BN4[(hc * 5 + sidx) % 4]
                bd.dep(PE, t_wd)
                bd.dep(PE, bank_free[bn])
                t_mm = None
                for dc in range(DC):
                    bd.deps(PE, lnp_toks.get(dc, []))
                    wv = WR[wb_i][:, dc * 128:(dc + 1) * 128]
                    t_mm = pe_mm(ps[bn][:, 0:512], wv, XLNP(dc),
                                 start=(dc == 0), stop=(dc == DC - 1),
                                 inc=("s_pe", 1) if dc == DC - 1 else None)
                wring_free[wb_i] = t_mm
                pb = ps[bn][:, 0:512]
                if jp == 3:
                    bd.dep(ACT, t_mm)
                    bd.dep(ACT, prev_tacc[par])
                    t1 = bd.run(ACT, lambda e, pb=pb, s_sig=s_sig: e.activation(
                        s_sig, pb, AF.Sigmoid, bias=C_TWO), inc=("s_ac", 1))
                    bank_free[bn] = t1
                    bd.dep(V, t1)
                    evac["a"] = bd.run(V, lambda e, s_a=s_a, s_sig=s_sig: e.tensor_scalar(
                        s_a, s_sig, 0.6, 0.9995, AL.max, AL.min), inc=("s_dv", 1))
                elif jp == 1:
                    bd.dep(ACT, t_mm)
                    t1 = bd.run(ACT, lambda e, pb=pb, s_k=s_k: e.activation(
                        s_k, pb, AF.Tanh), inc=("s_ac", 1))
                    bank_free[bn] = t1
                    evac["k"] = t1
                elif jp == 2:
                    bd.dep(ACT, t_mm)
                    t1 = bd.run(ACT, lambda e, pb=pb, VBF=VBF: e.copy(VBF, pb),
                                inc=("s_ac", 1))
                    bank_free[bn] = t1
                    bd.dep(V, t1)
                    bd.dep(V, evac["k"])
                    evac["u"] = bd.run(V, lambda e, s_u=s_u, s_k=s_k, VBF=VBF:
                                       e.tensor_tensor(s_u, s_k, VBF, AL.mult),
                                       inc=("s_dv", 1))
                elif jp == 4:
                    bd.dep(ACT, t_mm)
                    bd.dep(ACT, evac["a"])
                    t1 = bd.run(ACT, lambda e, pb=pb, s_sig=s_sig: e.activation(
                        s_sig, pb, AF.Sigmoid), inc=("s_ac", 1))
                    bank_free[bn] = t1
                    evac["g"] = t1
                else:
                    bd.dep(ACT, t_mm)
                    bd.dep(ACT, prev_p2m[par])
                    t1 = bd.run(ACT, lambda e, pb=pb, QBF=QBF: e.copy(QBF, pb),
                                inc=("s_ac", 1))
                    t1b = bd.run(ACT, lambda e, pb=pb, Q2BF=Q2BF: e.activation(
                        Q2BF, pb, AF.Square), inc=("s_ac", 1))
                    bank_free[bn] = t1b
                    bd.dep(PE, t1b)
                    bd.dep(PE, bank_free[stb])
                    t_q2 = pe_mm(ps[stb][:, 0:512], ONES_B[:, 0:128], Q2BF, True, True,
                                 inc=("s_pe", 1))
                    bd.dep(ACT, t_q2)
                    t_r1 = bd.run(ACT, lambda e, s_x2=s_x2, stb=stb: e.activation(
                        s_x2, ps[stb][:, 0:512], AF.Ln, bias=C_REPS, scale=C_I128),
                        inc=("s_ac", 1))
                    bank_free[stb] = t_r1
                    bd.dep(ACT, evac["u"])
                    t_r2 = bd.run(ACT, lambda e, s_k=s_k, s_x2=s_x2: e.activation(
                        s_k, s_x2, AF.Exp, scale=C_NH), inc=("s_ac", 1))
                    bd.dep(V, t_r2)
                    bd.dep(V, t_q2)
                    bd.run(V, lambda e, Q2BF=Q2BF, QBF=QBF, s_k=s_k:
                           e.tensor_tensor(Q2BF, QBF, s_k, AL.mult))
                    bd.dep(V, evac["g"])
                    bd.run(V, lambda e, hc=hc, Q2BF=Q2BF, s_sig=s_sig:
                           e.tensor_tensor(P1(hc), Q2BF, s_sig, AL.mult))
                    t_p2m = bd.run(V, lambda e, hc=hc, s_sig=s_sig, VBF=VBF:
                                   e.scalar_tensor_tensor(P2M(hc), s_sig, 1.0, VBF,
                                   AL.subtract, AL.mult), inc=("s_dv", 1))
                    phead_toks.append(t_p2m)
                    prev_p2m[par] = t_p2m

            bd.dep(V, t_ones)
            bd.dep(V, prev_lnpf[par])
            clip_i = evac["a"][1]
            for b_ in range(2):
                for c in range(NCH):
                    cs0 = b_ * 256 + c * CH
                    t_scan = bd.run(V, lambda e, cs0=cs0, s_pf=s_pf, s_a=s_a:
                        e.tensor_tensor_scan(
                        s_pf[:, cs0:cs0 + CH], s_a[:, cs0:cs0 + CH],
                        ONES_F[:, 0:CH], 1.0, AL.mult, AL.mult), base=clip_i,
                        inc=("s_dv", 1) if (b_ == 1 and c == NCH - 1) else None)
            scan_i = bd.vidx(V)
            bd.run(V, lambda e, hc=hc, s_pf=s_pf: e.tensor_copy(PREFIX(hc), s_pf),
                   base=scan_i)
            for b_ in range(2):
                base = hc * 8 + b_ * 4
                for c in range(NCH):
                    bd.run(V, lambda e, base=base, c=c, b_=b_, s_pf=s_pf: e.tensor_copy(
                        PFE[:, base + c:base + c + 1],
                        s_pf[:, b_ * 256 + c * CH + CH - 1: b_ * 256 + c * CH + CH]),
                        base=scan_i)
            for b_ in range(2):
                base = hc * 8 + b_ * 4
                bd.run(V, lambda e, base=base: e.tensor_copy(
                    APF[:, base:base + 1], ONES_F[:, 0:1]))
                for c in range(1, NCH):
                    bd.run(V, lambda e, base=base, c=c: e.tensor_tensor(
                        APF[:, base + c:base + c + 1], APF[:, base + c - 1:base + c],
                        PFE[:, base + c - 1:base + c], AL.mult))
                bd.run(V, lambda e, hc=hc, b_=b_, base=base: e.tensor_tensor(
                    S_LOC[:, hc * 2 + b_:hc * 2 + b_ + 1],
                    APF[:, base + 3:base + 4], PFE[:, base + 3:base + 4], AL.mult))
            bd.dep(ACT, t_scan)
            bd.dep(ACT, phead_toks[-1])
            t_lp = bd.run(ACT, lambda e, s_x2=s_x2, s_pf=s_pf: e.activation(
                s_x2, s_pf, AF.Ln), inc=("s_ac", 1))
            prev_lnpf[par] = t_lp
            t_rp = bd.run(ACT, lambda e, s_sig=s_sig, s_x2=s_x2: e.activation(
                s_sig, s_x2, AF.Exp, scale=C_N1), inc=("s_ac", 1))
            bd.dep(V, t_rp)
            t_term = bd.run(V, lambda e, s_a=s_a, s_sig=s_sig, s_u=s_u:
                            e.scalar_tensor_tensor(s_a, s_sig, 1e6, s_u,
                            AL.min, AL.mult), inc=("s_dv", 1))
            term_i = t_term[1]
            t_acc = None
            for b_ in range(2):
                for c in range(NCH):
                    cs0 = b_ * 256 + c * CH
                    t_acc = bd.run(V, lambda e, hc=hc, cs0=cs0, s_a=s_a:
                        e.tensor_tensor_scan(
                        ACC(hc)[:, cs0:cs0 + CH], ONES_F[:, 0:CH],
                        s_a[:, cs0:cs0 + CH], 0.0, AL.mult, AL.add), base=term_i,
                        inc=("s_dv", 1) if (b_ == 1 and c == NCH - 1) else None)
            for b_ in range(2):
                base = hc * 8 + b_ * 4
                bd.run(V, lambda e, base=base: e.memset(Z0[:, base:base + 1], 0.0))
                for c in range(NCH):
                    accv = ACC(hc)[:, b_ * 256 + c * CH + CH - 1: b_ * 256 + c * CH + CH]
                    dst = (Z0[:, base + c + 1:base + c + 2] if c < NCH - 1
                           else S_LOC[:, 32 + hc * 2 + b_: 32 + hc * 2 + b_ + 1])
                    tok = bd.run(V, lambda e, dst=dst, base=base, c=c, accv=accv:
                                 e.scalar_tensor_tensor(dst,
                                 accv, Z0[:, base + c:base + c + 1],
                                 PFE[:, base + c:base + c + 1], AL.add, AL.mult),
                                 inc=("s_dv", 1) if c == NCH - 1 else None)
            t_bc_toks.append(tok)
            phead_toks.append(t_acc)
            prev_tacc[par] = tok

        bd.deps(SP, t_bc_toks)
        t_s1 = bd.run(SP, lambda e, p=p: e.dma_start(out=sc_in[p], in_=S_LOC[:]),
                      inc=("s_io", 16))
        free_tok["S_LOC"] = t_s1
        bd.dep(P, t_s1)
        t_cc = bd.run(P, lambda e, p=p: e.collective_compute(
            "AllGather", AL.bypass, replica_groups=[list(range(NCORES))],
            ins=[sc_in[p].opt()], outs=[sc_out[p].opt()]), inc=("s_cc", 1))
        bd.dep(SP, t_cc)
        bd.dep(SP, free_tok["S_ALL"])
        t_sa = bd.run(SP, lambda e, p=p: e.dma_start(
            out=S_ALL[:].rearrange("p (r c) -> p r c", r=NCORES),
            in_=sc_out[p].rearrange("(r p) c -> p r c", p=128)), inc=("s_io", 16))

        bd.dep(V, t_sa)
        bd.run(V, lambda e: e.memset(PREF[:, 0:32], 0.0))
        for r in range(NCORES):
            Ar = S_ALL[:, r * 64:r * 64 + 32]
            Br = S_ALL[:, r * 64 + 32:r * 64 + 64]
            bd.run(V, lambda e, Ar=Ar, r=r: e.tensor_tensor(
                CSC[:, 0:32], Ar, PREF[:, r * 32:(r + 1) * 32], AL.mult))
            bd.run(V, lambda e, Br=Br, r=r: e.tensor_tensor(
                PREF[:, (r + 1) * 32:(r + 2) * 32], CSC[:, 0:32], Br, AL.add))
        pp = [CSC[:, 0:32], CSC[:, 32:64]]
        for r in range(NCORES):
            o = SINIT[:] if r == NCORES - 1 else pp[r % 2]
            if r == 0:
                bd.run(V, lambda e, o=o: e.tensor_scalar(
                    o, PREF[:, 0:32], OH[:, 0:1], None, AL.mult))
            else:
                bd.run(V, lambda e, o=o, r=r, i=pp[(r + 1) % 2]: e.scalar_tensor_tensor(
                    o, PREF[:, r * 32:(r + 1) * 32], OH[:, r:r + 1], i, AL.mult, AL.add))
        sr0 = bd.vidx("dve")
        for hc in range(H):
            for b_ in range(2):
                base = hc * 8 + b_ * 4
                sic = SINIT[:, hc * 2 + b_:hc * 2 + b_ + 1]
                for c in range(NCH):
                    bd.run(V, lambda e, base=base, c=c, sic=sic: e.scalar_tensor_tensor(
                        SREAL[:, base + c:base + c + 1], APF[:, base + c:base + c + 1],
                        sic, Z0[:, base + c:base + c + 1], AL.mult, AL.add), base=sr0)
        t_sreal = bd.run(V, lambda e: e.engine_nop(), inc=("s_dv", 1))
        free_tok["S_ALL"] = t_sreal

        y_toks = []
        bd.dep(V, t_sreal)
        bd.deps(V, wring_free)
        g0 = bd.vidx("dve")
        last_t1 = [g0, g0]
        for hc in range(H):
            par = hc % 2
            sa = SSET[par]["a"]
            sk = SSET[par]["k"]
            for b_ in range(2):
                base = hc * 8 + b_ * 4
                for c in range(NCH):
                    cs0 = b_ * 256 + c * CH
                    bd.run(V, lambda e, hc=hc, cs0=cs0, base=base, c=c, sa=sa:
                        e.scalar_tensor_tensor(
                        sa[:, cs0:cs0 + CH], ACC(hc)[:, cs0:cs0 + CH],
                        SREAL[:, base + c:base + c + 1], PREFIX(hc)[:, cs0:cs0 + CH],
                        AL.add, AL.mult), base=last_t1[par])
            t1 = bd.run(V, lambda e, hc=hc, sa=sa, sk=sk: e.tensor_tensor(
                sk, P1(hc), sa, AL.mult), inc=("s_dv", 1))
            last_t1[par] = t1[1]
            y_toks.append(bd.run(V, lambda e, hc=hc, sk=sk: e.tensor_tensor(
                YB(hc), sk, P2M(hc), AL.subtract), inc=("s_dv", 1)))
        free_tok["b_arena"] = y_toks[-1]
        free_tok["d_arena"] = y_toks[-1]

        for j in range(16):
            wb_i = j % 2
            bd.dep(P, wring_free[wb_i])
            t_wd = bd.run(P, lambda e, j=j, wb_i=wb_i: e.dma_start(
                out=WBUF[:, wb_i * 2048:wb_i * 2048 + 2048],
                in_=w_out_t[j].rearrange("p dc q -> p (dc q)")),
                inc=("s_w0" if wb_i == 0 else "s_w1", 16))
            bn = "mmA" if wb_i == 0 else "mmB"
            bd.dep(PE, t_wd)
            bd.dep(PE, bank_free[bn])
            t_mj = None
            for ch in range(DC):
                if ch % 4 == 0:
                    bd.dep(PE, y_toks[min(ch + 3, DC - 1)])
                wv = WBUF[:, wb_i * 2048 + ch * 128:wb_i * 2048 + (ch + 1) * 128]
                t_mj = pe_mm(ps[bn][:, 0:512], wv, YB(ch), start=(ch == 0), stop=(ch == DC - 1),
                             inc=("s_pe", 1) if ch == DC - 1 else None)
            wring_free[wb_i] = t_mj
            bd.dep(V, t_mj)
            bd.dep(V, free_tok["scr_wo"])
            scr = SCR[:, 0:512]
            xs = Xc(j)[:, pc0:pc0 + PW]
            t_e1 = bd.run(V, lambda e, bn=bn, scr=scr, xs=xs: e.scalar_tensor_tensor(
                scr, ps[bn][:, 0:512], 0.0, xs, AL.bypass, AL.add), inc=("s_dv", 1))
            bank_free[bn] = t_e1
            bd.dep(ACT, t_e1)
            t_e2 = bd.run(ACT, lambda e, scr=scr, xs=xs: e.copy(xs, scr), inc=("s_ac", 1))
            free_tok["scr_wo"] = t_e2
            xup_toks.append(t_e2)
            x_ready[j].append(t_e2)

    emit_ffn(0)
    emit_attn_pair(0)
    emit_attn_pair(1)
    emit_ffn(1)

    for dc in range(DC):
        bd.deps(SP, x_ready[dc])
        bd.run(SP, lambda e, dc=dc: e.dma_start(out=out[dc], in_=Xc(dc)), inc=("s_st", 16))
    bd.dep(SP, ("s_st", bd.cnt["s_st"]))

    with ExitStack() as stack:
        handles = {}
        for name in bd.sem_names:
            handles[name] = stack.enter_context(nc.semaphore(name))
        block = stack.enter_context(nc.Block())

        @block.tensor
        def _(t):
            bd.replay("pe", t, handles)

        @block.scalar
        def _(s):
            bd.replay("act", s, handles)

        @block.vector
        def _(v):
            bd.replay("dve", v, handles)

        @block.gpsimd
        def _(g):
            bd.replay("pool", g, handles)

        @block.sync
        def _(sy):
            bd.replay("sp", sy, handles)

    return nc


def kernel(L0_attn_x, L0_attn_w_in, L0_attn_w_out, L0_ffn_switch_w, L0_ffn_switch_b,
           L0_ffn_w1, L0_ffn_b1, L0_ffn_w2, L0_ffn_b2, L0_ffn_w3, L0_ffn_b3):
    x = np.ascontiguousarray(np.asarray(L0_attn_x, dtype=np.float32))
    w_in = np.ascontiguousarray(np.asarray(L0_attn_w_in, dtype=np.float32))
    w_out = np.ascontiguousarray(np.asarray(L0_attn_w_out, dtype=np.float32))
    sw = np.ascontiguousarray(np.asarray(L0_ffn_switch_w, dtype=np.float32))
    swb = np.ascontiguousarray(np.asarray(L0_ffn_switch_b, dtype=np.float32))
    w1 = np.ascontiguousarray(np.asarray(L0_ffn_w1, dtype=np.float32))
    b1 = np.ascontiguousarray(np.asarray(L0_ffn_b1, dtype=np.float32))
    w2 = np.ascontiguousarray(np.asarray(L0_ffn_w2, dtype=np.float32))
    b2 = np.ascontiguousarray(np.asarray(L0_ffn_b2, dtype=np.float32))
    w3 = np.ascontiguousarray(np.asarray(L0_ffn_w3, dtype=np.float32))
    b3 = np.ascontiguousarray(np.asarray(L0_ffn_b3, dtype=np.float32))

    wt = w_in.reshape(5, H, 128, DC, 128)
    w_in_t = np.ascontiguousarray(wt.transpose(1, 0, 4, 3, 2))
    wo = w_out.reshape(16, 128, DC, 128)
    w_out_t = np.ascontiguousarray(wo.transpose(0, 3, 2, 1))
    w3r = w3.reshape(16, 128, DC, 128)
    w3_t = np.ascontiguousarray(w3r.transpose(0, 3, 2, 1))
    sw3 = sw.reshape(8, 2048, 2048)
    w1T = np.ascontiguousarray(w1.transpose(0, 2, 1))
    w2T = np.ascontiguousarray(w2.transpose(0, 2, 1))
    w12T = np.ascontiguousarray(np.concatenate([w1T, w2T], axis=0).transpose(1, 0, 2))
    b1T = np.ascontiguousarray(b1.T)
    b2T = np.ascontiguousarray(b2.T)
    b3row = np.ascontiguousarray(b3[None, :])
    swb4 = np.ascontiguousarray(np.broadcast_to(swb[None, :], (4, 8)))

    nc = build_nc()

    in_maps = []
    for c in range(NCORES):
        sl = slice(256 * c, 256 * c + 256)
        x_fm = np.ascontiguousarray(
            x[:, sl, :].transpose(2, 0, 1).reshape(DC, 128, T))
        sw_c = np.ascontiguousarray(
            sw3[:, sl, :].transpose(0, 2, 1).reshape(8, DC, 128, TB).transpose(0, 2, 1, 3))
        oh = np.zeros((128, 8), np.float32)
        oh[:, c] = 1.0
        in_maps.append({
            "x_fm": x_fm, "w_in_t": w_in_t, "w_out_t": w_out_t, "w3_t": w3_t,
            "sw_t": sw_c, "w12T": w12T, "b1T": b1T, "b2T": b2T,
            "b3row": b3row, "swb": swb4, "onehot": oh,
        })

    res = run_bass_kernel_spmd(nc, in_maps, core_ids=list(range(NCORES)))

    outp = np.empty((4, 2048, 2048), np.float32)
    for c in range(NCORES):
        r = np.asarray(res.results[c]["out"])
        r = r.reshape(2048, 4, 256).transpose(1, 2, 0)
        outp[:, 256 * c:256 * c + 256, :] = r
    return outp


# version 17
# speedup vs baseline: 1.0503x; 1.0049x over previous
import numpy as np
from collections import defaultdict
from contextlib import ExitStack

import concourse.bass as bass
import concourse.mybir as mybir
from concourse.bass_utils import run_bass_kernel_spmd

F32 = mybir.dt.float32
BF16 = mybir.dt.bfloat16
AL = mybir.AluOpType
AF = mybir.ActivationFunctionType
AX = mybir.AxisListType

D = 2048
DC = 16
T = 1024
TB = 256
B = 4
H = 16
PW = 512
CH = 64
NCH = 4
LN_EPS = 1e-5
RMS_EPS = 1.1920928955078125e-07
NCORES = 8

ENGINES = ("pe", "act", "dve", "pool", "sp")


class Build:

    def __init__(self, nc):
        self.nc = nc
        self.streams = {e: [] for e in ENGINES}
        self.cnt = defaultdict(int)
        self.lastwait = {e: defaultdict(int) for e in ENGINES}
        self.sem_names = []

    def sem(self, name):
        if name not in self.sem_names:
            self.sem_names.append(name)
        return name

    def dep(self, eng, tok):
        if tok is None:
            return
        s, v = tok
        if v <= self.lastwait[eng][s]:
            return
        self.lastwait[eng][s] = v
        self.streams[eng].append(("wait", s, v))

    def deps(self, eng, toks):
        best = {}
        for t in toks:
            if t is None:
                continue
            s, v = t
            if v > best.get(s, -1):
                best[s] = v
        for s, v in best.items():
            self.dep(eng, (s, v))

    SERIALIZED = {"dve": "s_dvs", "act": "s_acs"}

    def run(self, eng, fn, inc=None, base=None):
        tok = None
        if eng in self.SERIALIZED:
            s = self.SERIALIZED[eng]
            self.sem(s)
            idx = self.cnt[s] + 1
            waitv = (idx - 1) if base is None else min(base, idx - 1)
            if waitv > 0:
                self.dep(eng, (s, waitv))
            self.cnt[s] = idx
            if inc is not None:
                tok = (s, idx)
            self.streams[eng].append(("op", fn, (s, 1)))
            return tok
        if inc is not None:
            s, amt = inc
            self.sem(s)
            self.cnt[s] += amt
            tok = (s, self.cnt[s])
        self.streams[eng].append(("op", fn, inc))
        return tok

    def vidx(self, eng):
        return self.cnt[self.SERIALIZED[eng]]

    def replay(self, eng, obj, handles):
        for item in self.streams[eng]:
            if item[0] == "wait":
                _, s, v = item
                obj.wait_ge(handles[s], v)
            else:
                _, fn, inc = item
                inst = fn(obj)
                if inc is not None:
                    s, amt = inc
                    inst.then_inc(handles[s], amt)


def build_nc():
    nc = bass.Bass("TRN2", target_bir_lowering=False, debug=False)

    def dinp(name, shape, dt=F32):
        return nc.dram_tensor(name, list(shape), dt, kind="ExternalInput").ap()

    x_fm = dinp("x_fm", [DC, 128, T])
    w_in_t = dinp("w_in_t", [H, 5, 128, DC, 128])
    w_out_t = dinp("w_out_t", [16, 128, DC, 128])
    w3_t = dinp("w3_t", [16, 128, DC, 128])
    sw_t = dinp("sw_t", [8, 128, DC, TB])
    w12T_in = dinp("w12T", [128, 16, 128])
    b1T_in = dinp("b1T", [128, 8])
    b2T_in = dinp("b2T", [128, 8])
    b3row_in = dinp("b3row", [1, D])
    swb_in = dinp("swb", [4, 8])
    onehot_in = dinp("onehot", [128, 8])

    out = nc.dram_tensor("out", [DC, 128, T], F32, kind="ExternalOutput").ap()

    rt_in = [nc.dram_tensor(f"rt_in{i}", [32], F32).ap() for i in range(2)]
    rt_out = [nc.dram_tensor(f"rt_out{i}", [32], F32).ap() for i in range(2)]
    sc_in = [nc.dram_tensor(f"sc_in{i}", [128, 64], F32).ap() for i in range(2)]
    sc_out = [nc.dram_tensor(f"sc_out{i}", [NCORES * 128, 64], F32).ap() for i in range(2)]
    selb = [nc.dram_tensor(f"selb{i}", [4, 40], F32).ap() for i in range(2)]

    def salloc(name, shape, dt):
        return nc.alloc_sbuf_tensor(name, list(shape), dt)

    X = salloc("X", [128, DC * T], F32)
    A_BF = salloc("A_BF", [128, 16384], BF16)
    B_BF = salloc("B_BF", [128, 8192], BF16)
    C_F = salloc("C_F", [128, 8192], F32)
    D_BF = salloc("D_BF", [128, 16384], BF16)
    WBUF = salloc("WBUF", [128, 2 * 2048], BF16)
    SCR = salloc("SCR", [128, 4 * 512], F32)
    SCB = salloc("SCB", [128, 3 * 512], BF16)
    ONES_F = salloc("ONES_F", [128, 128], F32)
    ONES_B = salloc("ONES_B", [128, 512], BF16)
    CONSTS = salloc("CONSTS", [128, 6], F32)
    S_LOC = salloc("S_LOC", [128, 64], F32)
    S_ALL = salloc("S_ALL", [128, NCORES * 64], F32)
    PREF = salloc("PREF", [128, 9 * 32], F32)
    SINIT = salloc("SINIT", [128, 32], F32)
    PFE = salloc("PFE", [128, 128], F32)
    APF = salloc("APF", [128, 128], F32)
    Z0 = salloc("Z0", [128, 128], F32)
    SREAL = salloc("SREAL", [128, 128], F32)
    CSC = salloc("CSC", [128, 64], F32)
    LGS = salloc("LGS", [1, 32], F32)
    LG = salloc("LG", [4, 64], F32)
    MSK = salloc("MSK", [4, 40], F32)
    GROW = salloc("GROW", [1, 160], F32)
    GSEL = salloc("GSEL", [128, 160], F32)
    B1T = salloc("B1T", [128, 8], F32)
    B2T = salloc("B2T", [128, 8], F32)
    B1S = salloc("B1S", [128, 8], F32)
    B2G = salloc("B2G", [128, 4], F32)
    SWB = salloc("SWB", [4, 8], F32)
    OH = salloc("OH", [128, 8], F32)
    SELS = salloc("SELS", [128, 2 * 128], BF16)
    SELF = salloc("SELF", [128, 2], F32)

    ps = {}
    def palloc(name, shape):
        ps[name] = nc.alloc_psum_tensor(name, list(shape), F32)
        return ps[name]

    palloc("mmA", [128, 512])
    palloc("mmB", [128, 512])
    palloc("bcA", [128, 512])
    palloc("bcB", [128, 512])
    palloc("st1", [128, 1024])
    palloc("st2", [128, 1024])

    bd = Build(nc)
    V, P, ACT, PE, SP = "dve", "pool", "act", "pe", "sp"
    for s in ("s_xin", "s_sm", "s_exw", "s_b3", "s_io", "s_w0", "s_w1",
              "s_w2", "s_w3", "s_sw0", "s_sw1", "s_pe", "s_dv", "s_ac",
              "s_cc", "s_st", "s_dvs", "s_acs"):
        bd.sem(s)

    bank_free = {k: None for k in ps}
    wring_free = [None, None, None, None]
    BN4 = ["mmA", "mmB", "bcA", "bcB"]
    free_tok = defaultdict(lambda: None)

    def Xc(dc):
        return X[:, dc * T:(dc + 1) * T]

    def pe_mm(bank_ap, lhsT, rhs, start, stop, inc=None):
        return bd.run(PE, lambda e, bank_ap=bank_ap, lhsT=lhsT, rhs=rhs, start=start,
                      stop=stop: e.matmul(bank_ap, lhsT, rhs, start=start, stop=stop),
                      inc=inc)

    tx_toks = []
    for dc in range(DC):
        tx_toks.append(bd.run(SP, lambda e, dc=dc: e.dma_start(out=Xc(dc), in_=x_fm[dc]),
                              inc=("s_xin", 16)))
    t_smalls = None
    for (dst, src) in ((B1T, b1T_in), (B2T, b2T_in), (SWB, swb_in), (OH, onehot_in)):
        t_smalls = bd.run(SP, lambda e, d=dst, s=src: e.dma_start(out=d[:], in_=s),
                          inc=("s_sm", 16))
    bd.run(V, lambda e: e.memset(ONES_F[:], 1.0))
    bd.run(V, lambda e: e.memset(CONSTS[:, 0:1], LN_EPS))
    bd.run(V, lambda e: e.memset(CONSTS[:, 1:2], 2.0))
    bd.run(V, lambda e: e.memset(CONSTS[:, 2:3], RMS_EPS))
    bd.run(V, lambda e: e.memset(CONSTS[:, 3:4], 1.0 / 128.0))
    bd.run(V, lambda e: e.memset(CONSTS[:, 4:5], -0.5))
    bd.run(V, lambda e: e.memset(CONSTS[:, 5:6], -1.0))
    t_ones = bd.run(V, lambda e: e.memset(ONES_B[:], 1.0), inc=("s_dv", 1))
    bd.dep(ACT, t_ones)
    C_EPS = CONSTS[:, 0:1]
    C_TWO = CONSTS[:, 1:2]
    C_REPS = CONSTS[:, 2:3]
    C_I128 = CONSTS[:, 3:4]
    C_NH = CONSTS[:, 4:5]
    C_N1 = CONSTS[:, 5:6]

    xup_toks = list(tx_toks)
    x_ready = {dc: [("s_xin", 16 * DC)] for dc in range(DC)}

    def emit_ln(xln_dst, col0, width, xbf_scr, x2bf_scr, ascr, nscr, apply_toks=None):
        nh = (width + 511) // 512
        bd.dep(PE, t_ones)
        bd.dep(PE, bank_free["st1"])
        bd.dep(PE, bank_free["st2"])
        pe_tok = [None, None]
        t_stats = None
        for dc in range(DC):
            xs = Xc(dc)[:, col0:col0 + width]
            bd.dep(V, pe_tok[dc % 2])
            bd.deps(V, x_ready[dc])
            t_c = bd.run(V, lambda e, dc=dc, xs=xs: e.tensor_copy(xbf_scr(dc), xs),
                         inc=("s_dv", 1))
            t_s = bd.run(V, lambda e, dc=dc: e.tensor_tensor(
                x2bf_scr(dc), xbf_scr(dc), xbf_scr(dc), AL.mult),
                base=t_c[1], inc=("s_dv", 1))
            bd.dep(PE, t_c)
            bd.dep(PE, t_s)
            for h in range(nh):
                w = min(512, width - h * 512)
                pe_mm(ps["st1"][:, h * 512:h * 512 + w], ONES_B[:, 0:128],
                      xbf_scr(dc)[:, h * 512:h * 512 + w], start=(dc == 0), stop=(dc == DC - 1))
                t_stats = pe_mm(ps["st2"][:, h * 512:h * 512 + w], ONES_B[:, 0:128],
                                x2bf_scr(dc)[:, h * 512:h * 512 + w],
                                start=(dc == 0), stop=(dc == DC - 1), inc=("s_pe", 1))
            pe_tok[dc % 2] = t_stats
        t_last = None
        t_v = None
        for h in range(nh):
            hs = h * 512
            w = min(512, width - hs)
            SM = SCR[0:1, 0:w]
            SV = SCR[0:1, 512:512 + w]
            S3 = SCR[0:1, 1024:1024 + w]
            S4 = SCR[0:1, 1536:1536 + w]
            bd.dep(V, t_stats)
            if h > 0:
                bd.dep(V, t_bc)
            bd.run(V, lambda e, hs=hs, w=w, SM=SM: e.tensor_scalar(
                SM, ps["st1"][0:1, hs:hs + w], 1.0 / D, None, AL.mult))
            t_v = bd.run(V, lambda e, hs=hs, w=w, SV=SV: e.tensor_scalar(
                SV, ps["st2"][0:1, hs:hs + w], 1.0 / D, None, AL.mult),
                inc=("s_dv", 1))
            bd.run(V, lambda e, SM=SM, S3=S3: e.tensor_tensor(S3, SM, SM, AL.mult))
            t_var = bd.run(V, lambda e, SV=SV, S3=S3, S4=S4: e.tensor_tensor(
                S4, SV, S3, AL.subtract), inc=("s_dv", 1))
            bd.dep(ACT, t_var)
            bd.run(ACT, lambda e, S3=S3, S4=S4: e.activation(
                S3, S4, AF.Ln, bias=CONSTS[0:1, 0:1]), inc=("s_ac", 1))
            t_sd = bd.run(ACT, lambda e, SV=SV, S3=S3: e.activation(
                SV, S3, AF.Exp, scale=CONSTS[0:1, 4:5]), inc=("s_ac", 1))
            bd.dep(V, t_sd)
            bd.run(V, lambda e, SM=SM, SV=SV, S3=S3: e.tensor_tensor(
                S3, SM, SV, AL.mult))
            t_rows = bd.run(V, lambda e, S3=S3, S4=S4: e.tensor_scalar(
                S4, S3, -1.0, None, AL.mult), inc=("s_dv", 1))
            bd.dep(PE, bank_free["bcA"])
            bd.dep(PE, bank_free["bcB"])
            bd.dep(PE, t_rows)
            pe_mm(ps["bcA"][:, 0:w], ONES_F[0:1, :], SV, True, True)
            t_bc = pe_mm(ps["bcB"][:, 0:w], ONES_F[0:1, :], S4,
                         True, True, inc=("s_pe", 1))
            bd.dep(V, t_bc)
            mt = {}
            at = {}
            g0 = bd.vidx("dve")
            for dc in range(DC):
                xs = Xc(dc)[:, col0 + hs:col0 + hs + w]
                scr = ascr(dc % nscr)[:, 0:w]
                bd.deps(V, x_ready[dc])
                mt[dc] = bd.run(V, lambda e, xs=xs, scr=scr, w=w: e.tensor_tensor(
                    scr, xs, ps["bcA"][:, 0:w], AL.mult),
                    base=(at[dc - nscr][1] if dc - nscr in at else g0),
                    inc=("s_dv", 1))
                if dc >= 1:
                    d2 = dc - 1
                    dst = xln_dst(d2)[:, hs:hs + w]
                    scr2 = ascr(d2 % nscr)[:, 0:w]
                    at[d2] = bd.run(V, lambda e, dst=dst, scr2=scr2, w=w: e.tensor_tensor(
                        dst, scr2, ps["bcB"][:, 0:w], AL.add), base=mt[d2][1],
                        inc=("s_dv", 1))
                    if apply_toks is not None:
                        apply_toks.setdefault(d2, []).append(at[d2])
            d2 = DC - 1
            dst = xln_dst(d2)[:, hs:hs + w]
            scr2 = ascr(d2 % nscr)[:, 0:w]
            t_last = bd.run(V, lambda e, dst=dst, scr2=scr2, w=w: e.tensor_tensor(
                dst, scr2, ps["bcB"][:, 0:w], AL.add), base=mt[d2][1],
                inc=("s_dv", 1))
            if apply_toks is not None:
                apply_toks.setdefault(d2, []).append(t_last)
            bank_free["bcA"] = t_last
            bank_free["bcB"] = t_last
        bank_free["st1"] = t_v
        bank_free["st2"] = t_v
        return t_last

    sw_free = [None, None]

    def emit_ffn(call):
        XLN = lambda dc: A_BF[:, dc * 1024:(dc + 1) * 1024]
        SWbuf = lambda i: D_BF[:, (i % 2) * 4096:(i % 2) * 4096 + 4096]
        Pbuf = lambda i: D_BF[:, 8192 + (i % 2) * 1024: 8192 + (i % 2) * 1024 + 1024]
        LNxb = lambda dc: D_BF[:, 8192 + (dc % 2) * 1024: 8192 + (dc % 2) * 1024 + 1024]
        LNx2 = lambda dc: D_BF[:, 10240 + (dc % 2) * 1024: 10240 + (dc % 2) * 1024 + 1024]
        EXW1 = lambda e_: B_BF[:, e_ * 128:(e_ + 1) * 128]
        EXW2 = lambda e_: B_BF[:, 1024 + e_ * 128: 1024 + (e_ + 1) * 128]
        SELW1 = lambda bk: B_BF[:, 2048 + bk * 128: 2048 + (bk + 1) * 128]
        SELW2 = lambda bk: B_BF[:, 3072 + bk * 128: 3072 + (bk + 1) * 128]
        HB = lambda i: B_BF[:, 4096 + (i % 2) * 256: 4096 + (i % 2) * 256 + 256]
        C_bf = C_F.bitcast(BF16)
        Y1 = lambda dc: C_bf[:, dc * 1024:(dc + 1) * 1024]

        bd.deps(V, [wring_free[2], wring_free[3]])
        D_F = D_BF.bitcast(F32)
        ln_toks = {}
        t_ln = emit_ln(XLN, 0, T, LNxb, LNx2,
                       lambda i: D_F[:, 4096 + i * 512: 4096 + (i + 1) * 512], 4,
                       apply_toks=ln_toks)

        bd.dep(P, free_tok["b_arena"])
        t_exw2 = bd.run(P, lambda e: e.dma_start(
            out=B_BF[:, 0:2048],
        in_=w12T_in.rearrange("p e q -> p (e q)")),
            inc=("s_exw", 16))

        sw_dma = [None, None]
        lg_toks = []
        pbuf_free = [None, None]
        for e_ in range(8):
            if e_ < 2:
                bd.dep(P, sw_free[e_ % 2])
                bd.dep(P, free_tok["d_arena"] if e_ == 0 else free_tok["d_arena2"])
                sw_dma[e_ % 2] = bd.run(P, lambda e, e_=e_: e.dma_start(
                    out=SWbuf(e_),
                    in_=sw_t[e_].rearrange("p dc t -> p (dc t)")),
                    inc=("s_sw0" if e_ % 2 == 0 else "s_sw1", 16))
            stats = "st1" if e_ % 2 == 0 else "st2"
            bd.dep(V, sw_dma[e_ % 2])
            t_mm = None
            rg0 = bd.vidx("dve")
            for dc in range(DC):
                swv = SWbuf(e_)[:, dc * 256:(dc + 1) * 256]
                sw_b = swv.rearrange("p (o t) -> p o t", o=1).broadcast_to([128, 4, 256])
                pb = Pbuf(dc)
                bd.dep(V, pbuf_free[dc % 2])
                bd.deps(V, ln_toks.get(dc, []))
                t_p = bd.run(V, lambda e, dc=dc, sw_b=sw_b, pb=pb, e_=e_: e.tensor_tensor(
                    pb.rearrange("p (b t) -> p b t", b=B),
                    XLN(dc).rearrange("p (b t) -> p b t", b=B), sw_b, AL.mult),
                    base=rg0, inc=("s_dv", 1))
                bd.dep(PE, bank_free[stats])
                bd.dep(PE, t_p)
                for hf in range(2):
                    t_mm = pe_mm(ps[stats][:, hf * 512:(hf + 1) * 512], ONES_B[:, 0:128],
                                 pb[:, hf * 512:(hf + 1) * 512],
                                 start=(dc == 0), stop=(dc == DC - 1),
                                 inc=("s_pe", 1) if hf == 1 else None)
                pbuf_free[dc % 2] = t_mm
                t_p_last = t_p
            sw_free[e_ % 2] = t_p_last
            if e_ + 2 < 8:
                bd.dep(P, sw_free[e_ % 2])
                sw_dma[e_ % 2] = bd.run(P, lambda e, e_=e_: e.dma_start(
                    out=SWbuf(e_ + 2),
                    in_=sw_t[e_ + 2].rearrange("p dc t -> p (dc t)")),
                    inc=("s_sw0" if e_ % 2 == 0 else "s_sw1", 16))
            bd.dep(V, t_mm)
            bd.dep(V, free_tok["LGS"])
            lt = None
            for b_ in range(B):
                lt = bd.run(V, lambda e, e_=e_, b_=b_, stats=stats: e.tensor_reduce(
                    LGS[:, b_ * 8 + e_: b_ * 8 + e_ + 1],
                    ps[stats][0:1, b_ * 256:(b_ + 1) * 256], AX.X, AL.add),
                    inc=("s_dv", 1) if b_ == B - 1 else None)
            bank_free[stats] = lt
            lg_toks.append(lt)

        B3R = D_BF[0:1, 0:2048]
        bd.dep(P, sw_free[0])
        t_b3 = bd.run(P, lambda e: e.dma_start(out=B3R, in_=b3row_in), inc=("s_b3", 16))

        bd.deps(SP, lg_toks)
        t_s1 = bd.run(SP, lambda e: e.dma_start(
            out=rt_in[call].rearrange("(o c) -> o c", o=1), in_=LGS[0:1, :]), inc=("s_io", 16))
        free_tok["LGS"] = t_s1
        bd.dep(P, t_s1)
        t_cc = bd.run(P, lambda e: e.collective_compute(
            "AllReduce", AL.add, replica_groups=[list(range(NCORES))],
            ins=[rt_in[call].opt()], outs=[rt_out[call].opt()]), inc=("s_cc", 1))
        bd.dep(SP, t_cc)
        t_lg = bd.run(SP, lambda e: e.dma_start(
            out=LG[:, 0:8], in_=rt_out[call].rearrange("(b e) -> b e", b=4)),
            inc=("s_io", 16))

        lg0 = LG[:, 0:8]; ex_ = LG[:, 8:16]; pr = LG[:, 16:24]; pr2 = LG[:, 24:32]
        cs = LG[:, 32:40]; tmp8 = LG[:, 40:48]; mx = LG[:, 48:49]; nmx = LG[:, 49:50]
        sm = LG[:, 50:51]; rs = LG[:, 51:52]
        msk1 = MSK[:, 0:8]; msk2 = MSK[:, 8:16]; gm1 = MSK[:, 16:24]
        gm2 = MSK[:, 24:32]; gmc = MSK[:, 32:40]
        bd.dep(V, t_lg)
        bd.dep(V, t_smalls)
        bd.dep(V, free_tok["MSK"])
        bd.run(V, lambda e: e.tensor_tensor(lg0, lg0, SWB[:], AL.add))
        bd.run(V, lambda e: e.reduce_max(mx, lg0, AX.X))
        t_nm = bd.run(V, lambda e: e.tensor_scalar(nmx, mx, -1.0, None, AL.mult), inc=("s_dv", 1))
        bd.dep(ACT, t_nm)
        t_ex = bd.run(ACT, lambda e: e.activation(ex_, lg0, AF.Exp, bias=nmx[:, 0:1]),
                      inc=("s_ac", 1))
        bd.dep(V, t_ex)
        bd.run(V, lambda e: e.reduce_sum(sm, ex_, AX.X))
        bd.run(V, lambda e: e.reciprocal(rs, sm))
        bd.run(V, lambda e: e.tensor_scalar(pr, ex_, rs[:, 0:1], None, AL.mult))
        bd.run(V, lambda e: e.reduce_max(mx, pr, AX.X))
        bd.run(V, lambda e: e.tensor_scalar(tmp8, pr, mx[:, 0:1], None, AL.is_ge))
        bd.run(V, lambda e: e.tensor_tensor_scan(cs, ONES_F[0:4, 0:8], tmp8, 0.0, AL.mult, AL.add))
        bd.run(V, lambda e: e.tensor_scalar(cs, cs, 1.0, None, AL.is_equal))
        bd.run(V, lambda e: e.tensor_tensor(msk1, tmp8, cs, AL.mult))
        bd.run(V, lambda e: e.tensor_tensor(gm1, pr, msk1, AL.mult))
        bd.run(V, lambda e: e.tensor_scalar(tmp8, msk1, -1.0, 1.0, AL.mult, AL.add))
        bd.run(V, lambda e: e.tensor_tensor(pr2, pr, tmp8, AL.mult))
        bd.run(V, lambda e: e.reduce_max(mx, pr2, AX.X))
        bd.run(V, lambda e: e.tensor_scalar(tmp8, pr2, mx[:, 0:1], None, AL.is_ge))
        bd.run(V, lambda e: e.tensor_tensor_scan(cs, ONES_F[0:4, 0:8], tmp8, 0.0, AL.mult, AL.add))
        bd.run(V, lambda e: e.tensor_scalar(cs, cs, 1.0, None, AL.is_equal))
        bd.run(V, lambda e: e.tensor_tensor(msk2, tmp8, cs, AL.mult))
        bd.run(V, lambda e: e.tensor_tensor(gm2, pr2, msk2, AL.mult))
        t_msk = bd.run(V, lambda e: e.tensor_tensor(gmc, gm1, gm2, AL.add), inc=("s_dv", 1))

        bd.dep(SP, t_msk)
        t_sb = bd.run(SP, lambda e: e.dma_start(out=selb[call], in_=MSK[:, 0:40]),
                      inc=("s_io", 16))
        free_tok["MSK"] = t_sb
        bd.dep(SP, t_sb)
        t_sr = bd.run(SP, lambda e: e.dma_start(
            out=GROW[:], in_=selb[call].rearrange("b c -> (b c)")[None, :]), inc=("s_io", 16))
        bd.dep(PE, t_sr)
        bd.dep(PE, bank_free["bcA"])
        t_gb = pe_mm(ps["bcA"][:, 0:160], ONES_F[0:1, :], GROW[:], True, True, inc=("s_pe", 1))
        bd.dep(V, t_gb)
        t_gs = bd.run(V, lambda e: e.tensor_copy(GSEL[:, 0:160], ps["bcA"][:, 0:160]),
                      inc=("s_dv", 1))
        bank_free["bcA"] = t_gs

        bd.dep(V, t_exw2)
        sel_last = None
        chains = []
        for b_ in range(B):
            for k_ in range(2):
                bk = b_ * 2 + k_
                moff = b_ * 40 + (0 if k_ == 0 else 8)
                goff = b_ * 40 + (16 if k_ == 0 else 24)
                chains.append((SELW1(bk), EXW1, moff))
                chains.append((SELW2(bk), EXW2, goff))
        for grp in range(0, len(chains), 4):
            batch = chains[grp:grp + 4]
            prev = [None] * len(batch)
            for e_ in range(8):
                for lane, (dst, src, off) in enumerate(batch):
                    pp = [B_BF[:, 6144 + lane * 256:6144 + lane * 256 + 128],
                          B_BF[:, 6144 + lane * 256 + 128:6144 + lane * 256 + 256]]
                    o = dst if e_ == 7 else pp[e_ % 2]
                    if e_ == 0:
                        prev[lane] = bd.run(V, lambda e, o=o, src=src, off=off:
                            e.tensor_scalar(o, src(0), GSEL[:, off:off + 1], None,
                            AL.mult), base=max(free_tok["sel_lane"] or 0, t_gs[1]),
                            inc=("s_dv", 1))
                    else:
                        prev[lane] = bd.run(V, lambda e, o=o, src=src, e_=e_, off=off,
                            i=pp[(e_ + 1) % 2]: e.scalar_tensor_tensor(
                            o, src(e_), GSEL[:, off + e_: off + e_ + 1], i,
                            AL.mult, AL.add), base=prev[lane][1], inc=("s_dv", 1))
            sel_last = prev[-1]
            free_tok["sel_lane"] = max(t[1] for t in prev)
        bchains = []
        for b_ in range(B):
            for k_ in range(2):
                bk = b_ * 2 + k_
                moff = b_ * 40 + (0 if k_ == 0 else 8)
                bchains.append((B1S[:, bk:bk + 1], B1T, moff))
        for b_ in range(B):
            bchains.append((B2G[:, b_:b_ + 1], B2T, b_ * 40 + 32))
        for grp in range(0, len(bchains), 4):
            batch = bchains[grp:grp + 4]
            prev = [None] * len(batch)
            for e_ in range(8):
                for lane, (dst, tsrc, off) in enumerate(batch):
                    ppf = [PREF[:, 272 + 2 * lane:273 + 2 * lane],
                           PREF[:, 273 + 2 * lane:274 + 2 * lane]]
                    o = dst if e_ == 7 else ppf[e_ % 2]
                    if e_ == 0:
                        prev[lane] = bd.run(V, lambda e, o=o, tsrc=tsrc, off=off:
                            e.tensor_scalar(o, tsrc[:, 0:1], GSEL[:, off:off + 1], None,
                            AL.mult), base=max(free_tok["selb_lane"] or 0, t_gs[1]),
                            inc=("s_dv", 1))
                    else:
                        prev[lane] = bd.run(V, lambda e, o=o, tsrc=tsrc, e_=e_, off=off,
                            i=ppf[(e_ + 1) % 2]: e.scalar_tensor_tensor(
                            o, tsrc[:, e_:e_ + 1], GSEL[:, off + e_: off + e_ + 1], i,
                            AL.mult, AL.add), base=prev[lane][1], inc=("s_dv", 1))
            sel_last = prev[-1]
            free_tok["selb_lane"] = max(t[1] for t in prev)
        t_sel = sel_last

        y1_toks = []
        obanks = ["mmA", "mmB", "bcA", "bcB"]
        hbanks = ["st1", "st2"]
        HB8 = lambda s: B_BF[:, 4096 + s * 256: 4096 + (s + 1) * 256]
        hb_i = 0
        bd.dep(PE, t_sel)
        bd.dep(PE, t_ln)
        for b_ in range(B):
            for g in range(4):
                hcs = [g * 4 + i for i in range(4)]
                gel = {}
                mm1t = {}
                for k_ in range(2):
                    bk = b_ * 2 + k_
                    for i, hc in enumerate(hcs):
                        s = k_ * 4 + i
                        hname = hbanks[hb_i % 2]
                        hb = ps[hname]
                        bd.dep(PE, bank_free[hname])
                        t_mm1 = pe_mm(hb[:, 0:256], SELW1(bk),
                                      XLN(hc)[:, b_ * 256:(b_ + 1) * 256], True, True,
                                      inc=("s_pe", 1))
                        bd.dep(ACT, t_mm1)
                        t_h = bd.run(ACT, lambda e, hb=hb, bk=bk, s=s: e.activation(
                            HB8(s), hb[:, 0:256], AF.Gelu, bias=B1S[:, bk:bk + 1]),
                            inc=("s_ac", 1))
                        bank_free[hname] = t_h
                        gel[s] = t_h
                        hb_i += 1
                for k_ in range(2):
                    bk = b_ * 2 + k_
                    for i, hc in enumerate(hcs):
                        s = k_ * 4 + i
                        ob = ps[obanks[i]]
                        bd.dep(PE, gel[s])
                        if k_ == 0:
                            bd.dep(PE, bank_free[obanks[i]])
                        t_mm2 = pe_mm(ob[:, 0:256], SELW2(bk), HB8(s),
                                      start=(k_ == 0), stop=(k_ == 1),
                                      inc=("s_pe", 1))
                        if k_ == 1:
                            bd.dep(V, t_mm2)
                            bd.dep(V, t_sel)
                            t_y = bd.run(V, lambda e, ob=ob, hc=hc, b_=b_: e.scalar_tensor_tensor(
                                Y1(hc)[:, b_ * 256:(b_ + 1) * 256], ob[:, 0:256],
                                B2G[:, b_:b_ + 1],
                                XLN(hc)[:, b_ * 256:(b_ + 1) * 256], AL.add, AL.add),
                                inc=("s_dv", 1))
                            bank_free[obanks[i]] = t_y
                            y1_toks.append(t_y)
        free_tok["a_ext"] = t_mm2
        for j in range(16):
            wb_i = j % 2
            bd.dep(P, wring_free[wb_i])
            t_wd = bd.run(P, lambda e, j=j, wb_i=wb_i: e.dma_start(
                out=WBUF[:, wb_i * 2048:wb_i * 2048 + 2048],
                in_=w3_t[j].rearrange("p dc q -> p (dc q)")),
                inc=("s_w0" if wb_i == 0 else "s_w1", 16))
            bd.dep(PE, t_wd)
            bd.dep(PE, t_b3)
            bd.deps(PE, y1_toks)
            bnA = BN4[(j % 2) * 2]
            bnB = BN4[(j % 2) * 2 + 1]
            bd.dep(PE, bank_free[bnA])
            bd.dep(PE, bank_free[bnB])
            for dc in range(DC):
                wv = WBUF[:, wb_i * 2048 + dc * 128:wb_i * 2048 + (dc + 1) * 128]
                pe_mm(ps[bnA][:, 0:512], wv, Y1(dc)[:, 0:512], start=(dc == 0), stop=False)
                pe_mm(ps[bnB][:, 0:512], wv, Y1(dc)[:, 512:1024], start=(dc == 0), stop=False)
            t_mja = bd.run(PE, lambda e, bnA=bnA, j=j, B3R=B3R: e.matmul(
                ps[bnA][:, 0:512], B3R[:, j * 128:(j + 1) * 128],
                ONES_B[0:1, 0:512], start=False, stop=True), inc=("s_pe", 1))
            t_mjb = bd.run(PE, lambda e, bnB=bnB, j=j, B3R=B3R: e.matmul(
                ps[bnB][:, 0:512], B3R[:, j * 128:(j + 1) * 128],
                ONES_B[0:1, 0:512], start=False, stop=True), inc=("s_pe", 1))
            wring_free[wb_i] = t_mjb
            for hf, (bn, t_mj) in enumerate(((bnA, t_mja), (bnB, t_mjb))):
                bd.dep(V, t_mj)
                bd.dep(V, free_tok["scr_w3"])
                scr = SCR[:, 512:1024]
                xs = Xc(j)[:, hf * 512:hf * 512 + 512]
                t_e1 = bd.run(V, lambda e, bn=bn, scr=scr, xs=xs: e.scalar_tensor_tensor(
                    scr, ps[bn][:, 0:512], 0.5, xs, AL.mult, AL.add), inc=("s_dv", 1))
                bank_free[bn] = t_e1
                bd.dep(ACT, t_e1)
                t_e2 = bd.run(ACT, lambda e, scr=scr, xs=xs: e.copy(xs, scr), inc=("s_ac", 1))
                free_tok["scr_w3"] = t_e2
                xup_toks.append(t_e2)
                x_ready[j].append(t_e2)

    def emit_attn_pair(p):
        pc0 = p * PW
        A_F = A_BF.bitcast(F32)
        C_BF = C_F.bitcast(BF16)
        XLNP = lambda dc: A_BF[:, dc * 512:(dc + 1) * 512]
        YB = lambda hc: A_BF[:, hc * 512:(hc + 1) * 512]
        PREFIX = lambda hc: B_BF[:, hc * 512:(hc + 1) * 512]
        ACC = lambda hc: C_BF[:, hc * 512:(hc + 1) * 512]
        P1 = lambda hc: D_BF[:, hc * 512:(hc + 1) * 512]
        P2M = lambda hc: D_BF[:, 8192 + hc * 512: 8192 + (hc + 1) * 512]
        LNxb = lambda dc: D_BF[:, (dc % 2) * 512:(dc % 2) * 512 + 512]
        LNx2 = lambda dc: D_BF[:, 1024 + (dc % 2) * 512:1024 + (dc % 2) * 512 + 512]
        SSET = [
            dict(sig=SCR[:, 0:512], a=SCR[:, 512:1024], k=SCR[:, 1024:1536],
                 u=SCR[:, 1536:2048], pf=A_F[:, 6144:6656], x2=A_F[:, 6656:7168],
                 qb=SCB[:, 0:512], vb=SCB[:, 512:1024], q2b=SCB[:, 1024:1536],
                 stb="st1"),
            dict(sig=C_F[:, 4096:4608], a=C_F[:, 4608:5120], k=C_F[:, 5120:5632],
                 u=C_F[:, 5632:6144], pf=C_F[:, 6144:6656], x2=C_F[:, 6656:7168],
                 qb=C_BF[:, 14336:14848], vb=C_BF[:, 14848:15360],
                 q2b=C_BF[:, 15360:15872], stb="st2"),
        ]
        WR = [WBUF[:, 0:2048], WBUF[:, 2048:4096],
              A_BF[:, 8192:10240], A_BF[:, 10240:12288]]
        WSEM = ["s_w0", "s_w1", "s_w2", "s_w3"]

        D_F = D_BF.bitcast(F32)
        lnp_toks = {}
        t_lnp = emit_ln(XLNP, pc0, PW, LNxb, LNx2,
                        lambda i: D_F[:, i * 512:(i + 1) * 512], 2,
                        apply_toks=lnp_toks)

        t_bc_toks = []
        phead_toks = []
        prev_tacc = [None, None]
        prev_lnpf = [None, None]
        prev_p2m = [None, None]
        bd.dep(P, free_tok["a_ext"])
        bd.dep(V, free_tok["S_LOC"])
        for hc in range(H):
            par = hc % 2
            S = SSET[par]
            s_sig, s_a, s_k, s_u = S["sig"], S["a"], S["k"], S["u"]
            s_pf, s_x2 = S["pf"], S["x2"]
            QBF, VBF, Q2BF = S["qb"], S["vb"], S["q2b"]
            stb = S["stb"]
            evac = {}
            for sidx, jp in enumerate((3, 1, 2, 4, 0)):
                wb_i = (p * 80 + hc * 5 + sidx + 2) % 4
                bd.dep(P, wring_free[wb_i])
                t_wd = bd.run(P, lambda e, hc=hc, jp=jp, wb_i=wb_i: e.dma_start(
                    out=WR[wb_i],
                    in_=w_in_t[hc, jp].rearrange("p dc q -> p (dc q)")),
                    inc=(WSEM[wb_i], 16))
                bn = BN4[(hc * 5 + sidx) % 4]
                bd.dep(PE, t_wd)
                bd.dep(PE, bank_free[bn])
                t_mm = None
                for dc in range(DC):
                    bd.deps(PE, lnp_toks.get(dc, []))
                    wv = WR[wb_i][:, dc * 128:(dc + 1) * 128]
                    t_mm = pe_mm(ps[bn][:, 0:512], wv, XLNP(dc),
                                 start=(dc == 0), stop=(dc == DC - 1),
                                 inc=("s_pe", 1) if dc == DC - 1 else None)
                wring_free[wb_i] = t_mm
                pb = ps[bn][:, 0:512]
                if jp == 3:
                    bd.dep(ACT, t_mm)
                    bd.dep(ACT, prev_tacc[par])
                    t1 = bd.run(ACT, lambda e, pb=pb, s_sig=s_sig: e.activation(
                        s_sig, pb, AF.Sigmoid, bias=C_TWO), inc=("s_ac", 1))
                    bank_free[bn] = t1
                    bd.dep(V, t1)
                    evac["a"] = bd.run(V, lambda e, s_a=s_a, s_sig=s_sig: e.tensor_scalar(
                        s_a, s_sig, 0.6, 0.9995, AL.max, AL.min), inc=("s_dv", 1))
                elif jp == 1:
                    bd.dep(ACT, t_mm)
                    t1 = bd.run(ACT, lambda e, pb=pb, s_k=s_k: e.activation(
                        s_k, pb, AF.Tanh), inc=("s_ac", 1))
                    bank_free[bn] = t1
                    evac["k"] = t1
                elif jp == 2:
                    bd.dep(ACT, t_mm)
                    t1 = bd.run(ACT, lambda e, pb=pb, VBF=VBF: e.copy(VBF, pb),
                                inc=("s_ac", 1))
                    bank_free[bn] = t1
                    bd.dep(V, t1)
                    bd.dep(V, evac["k"])
                    evac["u"] = bd.run(V, lambda e, s_u=s_u, s_k=s_k, VBF=VBF:
                                       e.tensor_tensor(s_u, s_k, VBF, AL.mult),
                                       inc=("s_dv", 1))
                elif jp == 4:
                    bd.dep(ACT, t_mm)
                    bd.dep(ACT, evac["a"])
                    t1 = bd.run(ACT, lambda e, pb=pb, s_sig=s_sig: e.activation(
                        s_sig, pb, AF.Sigmoid), inc=("s_ac", 1))
                    bank_free[bn] = t1
                    evac["g"] = t1
                else:
                    bd.dep(ACT, t_mm)
                    bd.dep(ACT, prev_p2m[par])
                    t1 = bd.run(ACT, lambda e, pb=pb, QBF=QBF: e.copy(QBF, pb),
                                inc=("s_ac", 1))
                    t1b = bd.run(ACT, lambda e, pb=pb, Q2BF=Q2BF: e.activation(
                        Q2BF, pb, AF.Square), inc=("s_ac", 1))
                    bank_free[bn] = t1b
                    bd.dep(PE, t1b)
                    bd.dep(PE, bank_free[stb])
                    t_q2 = pe_mm(ps[stb][:, 0:512], ONES_B[:, 0:128], Q2BF, True, True,
                                 inc=("s_pe", 1))
                    bd.dep(ACT, t_q2)
                    t_r1 = bd.run(ACT, lambda e, s_x2=s_x2, stb=stb: e.activation(
                        s_x2, ps[stb][:, 0:512], AF.Ln, bias=C_REPS, scale=C_I128),
                        inc=("s_ac", 1))
                    bank_free[stb] = t_r1
                    bd.dep(ACT, evac["u"])
                    t_r2 = bd.run(ACT, lambda e, s_k=s_k, s_x2=s_x2: e.activation(
                        s_k, s_x2, AF.Exp, scale=C_NH), inc=("s_ac", 1))
                    bd.dep(V, t_r2)
                    bd.dep(V, t_q2)
                    bd.run(V, lambda e, Q2BF=Q2BF, QBF=QBF, s_k=s_k:
                           e.tensor_tensor(Q2BF, QBF, s_k, AL.mult))
                    bd.dep(V, evac["g"])
                    bd.run(V, lambda e, hc=hc, Q2BF=Q2BF, s_sig=s_sig:
                           e.tensor_tensor(P1(hc), Q2BF, s_sig, AL.mult))
                    t_p2m = bd.run(V, lambda e, hc=hc, s_sig=s_sig, VBF=VBF:
                                   e.scalar_tensor_tensor(P2M(hc), s_sig, 1.0, VBF,
                                   AL.subtract, AL.mult), inc=("s_dv", 1))
                    phead_toks.append(t_p2m)
                    prev_p2m[par] = t_p2m

            bd.dep(V, t_ones)
            bd.dep(V, prev_lnpf[par])
            clip_i = evac["a"][1]
            for b_ in range(2):
                for c in range(NCH):
                    cs0 = b_ * 256 + c * CH
                    t_scan = bd.run(V, lambda e, cs0=cs0, s_pf=s_pf, s_a=s_a:
                        e.tensor_tensor_scan(
                        s_pf[:, cs0:cs0 + CH], s_a[:, cs0:cs0 + CH],
                        ONES_F[:, 0:CH], 1.0, AL.mult, AL.mult), base=clip_i,
                        inc=("s_dv", 1) if (b_ == 1 and c == NCH - 1) else None)
            scan_i = bd.vidx(V)
            bd.run(V, lambda e, hc=hc, s_pf=s_pf: e.tensor_copy(PREFIX(hc), s_pf),
                   base=scan_i)
            for b_ in range(2):
                base = hc * 8 + b_ * 4
                for c in range(NCH):
                    bd.run(V, lambda e, base=base, c=c, b_=b_, s_pf=s_pf: e.tensor_copy(
                        PFE[:, base + c:base + c + 1],
                        s_pf[:, b_ * 256 + c * CH + CH - 1: b_ * 256 + c * CH + CH]),
                        base=scan_i)
            for b_ in range(2):
                base = hc * 8 + b_ * 4
                bd.run(V, lambda e, base=base: e.tensor_copy(
                    APF[:, base:base + 1], ONES_F[:, 0:1]))
                for c in range(1, NCH):
                    bd.run(V, lambda e, base=base, c=c: e.tensor_tensor(
                        APF[:, base + c:base + c + 1], APF[:, base + c - 1:base + c],
                        PFE[:, base + c - 1:base + c], AL.mult))
                bd.run(V, lambda e, hc=hc, b_=b_, base=base: e.tensor_tensor(
                    S_LOC[:, hc * 2 + b_:hc * 2 + b_ + 1],
                    APF[:, base + 3:base + 4], PFE[:, base + 3:base + 4], AL.mult))
            bd.dep(ACT, t_scan)
            bd.dep(ACT, phead_toks[-1])
            t_lp = bd.run(ACT, lambda e, s_x2=s_x2, s_pf=s_pf: e.activation(
                s_x2, s_pf, AF.Ln), inc=("s_ac", 1))
            prev_lnpf[par] = t_lp
            t_rp = bd.run(ACT, lambda e, s_sig=s_sig, s_x2=s_x2: e.activation(
                s_sig, s_x2, AF.Exp, scale=C_N1), inc=("s_ac", 1))
            bd.dep(V, t_rp)
            t_term = bd.run(V, lambda e, s_a=s_a, s_sig=s_sig, s_u=s_u:
                            e.scalar_tensor_tensor(s_a, s_sig, 1e6, s_u,
                            AL.min, AL.mult), inc=("s_dv", 1))
            term_i = t_term[1]
            t_acc = None
            for b_ in range(2):
                for c in range(NCH):
                    cs0 = b_ * 256 + c * CH
                    t_acc = bd.run(V, lambda e, hc=hc, cs0=cs0, s_a=s_a:
                        e.tensor_tensor_scan(
                        ACC(hc)[:, cs0:cs0 + CH], ONES_F[:, 0:CH],
                        s_a[:, cs0:cs0 + CH], 0.0, AL.mult, AL.add), base=term_i,
                        inc=("s_dv", 1) if (b_ == 1 and c == NCH - 1) else None)
            for b_ in range(2):
                base = hc * 8 + b_ * 4
                bd.run(V, lambda e, base=base: e.memset(Z0[:, base:base + 1], 0.0))
                for c in range(NCH):
                    accv = ACC(hc)[:, b_ * 256 + c * CH + CH - 1: b_ * 256 + c * CH + CH]
                    dst = (Z0[:, base + c + 1:base + c + 2] if c < NCH - 1
                           else S_LOC[:, 32 + hc * 2 + b_: 32 + hc * 2 + b_ + 1])
                    tok = bd.run(V, lambda e, dst=dst, base=base, c=c, accv=accv:
                                 e.scalar_tensor_tensor(dst,
                                 accv, Z0[:, base + c:base + c + 1],
                                 PFE[:, base + c:base + c + 1], AL.add, AL.mult),
                                 inc=("s_dv", 1) if c == NCH - 1 else None)
            t_bc_toks.append(tok)
            phead_toks.append(t_acc)
            prev_tacc[par] = tok

        bd.deps(SP, t_bc_toks)
        t_s1 = bd.run(SP, lambda e, p=p: e.dma_start(out=sc_in[p], in_=S_LOC[:]),
                      inc=("s_io", 16))
        free_tok["S_LOC"] = t_s1
        bd.dep(P, t_s1)
        t_cc = bd.run(P, lambda e, p=p: e.collective_compute(
            "AllGather", AL.bypass, replica_groups=[list(range(NCORES))],
            ins=[sc_in[p].opt()], outs=[sc_out[p].opt()]), inc=("s_cc", 1))
        bd.dep(SP, t_cc)
        bd.dep(SP, free_tok["S_ALL"])
        t_sa = bd.run(SP, lambda e, p=p: e.dma_start(
            out=S_ALL[:].rearrange("p (r c) -> p r c", r=NCORES),
            in_=sc_out[p].rearrange("(r p) c -> p r c", p=128)), inc=("s_io", 16))

        bd.dep(V, t_sa)
        bd.run(V, lambda e: e.memset(PREF[:, 0:32], 0.0))
        for r in range(NCORES):
            Ar = S_ALL[:, r * 64:r * 64 + 32]
            Br = S_ALL[:, r * 64 + 32:r * 64 + 64]
            bd.run(V, lambda e, Ar=Ar, r=r: e.tensor_tensor(
                CSC[:, 0:32], Ar, PREF[:, r * 32:(r + 1) * 32], AL.mult))
            bd.run(V, lambda e, Br=Br, r=r: e.tensor_tensor(
                PREF[:, (r + 1) * 32:(r + 2) * 32], CSC[:, 0:32], Br, AL.add))
        pp = [CSC[:, 0:32], CSC[:, 32:64]]
        for r in range(NCORES):
            o = SINIT[:] if r == NCORES - 1 else pp[r % 2]
            if r == 0:
                bd.run(V, lambda e, o=o: e.tensor_scalar(
                    o, PREF[:, 0:32], OH[:, 0:1], None, AL.mult))
            else:
                bd.run(V, lambda e, o=o, r=r, i=pp[(r + 1) % 2]: e.scalar_tensor_tensor(
                    o, PREF[:, r * 32:(r + 1) * 32], OH[:, r:r + 1], i, AL.mult, AL.add))
        sr0 = bd.vidx("dve")
        for hc in range(H):
            for b_ in range(2):
                base = hc * 8 + b_ * 4
                sic = SINIT[:, hc * 2 + b_:hc * 2 + b_ + 1]
                for c in range(NCH):
                    bd.run(V, lambda e, base=base, c=c, sic=sic: e.scalar_tensor_tensor(
                        SREAL[:, base + c:base + c + 1], APF[:, base + c:base + c + 1],
                        sic, Z0[:, base + c:base + c + 1], AL.mult, AL.add), base=sr0)
        t_sreal = bd.run(V, lambda e: e.engine_nop(), inc=("s_dv", 1))
        free_tok["S_ALL"] = t_sreal

        y_toks = []
        attn_t1 = []
        bd.dep(V, t_sreal)
        bd.deps(V, wring_free)
        g0 = bd.vidx("dve")
        last_t1 = [g0, g0]
        for hc in range(H):
            par = hc % 2
            sa = SSET[par]["a"]
            sk = SSET[par]["k"]
            for b_ in range(2):
                base = hc * 8 + b_ * 4
                for c in range(NCH):
                    cs0 = b_ * 256 + c * CH
                    bd.run(V, lambda e, hc=hc, cs0=cs0, base=base, c=c, sa=sa:
                        e.scalar_tensor_tensor(
                        sa[:, cs0:cs0 + CH], ACC(hc)[:, cs0:cs0 + CH],
                        SREAL[:, base + c:base + c + 1], PREFIX(hc)[:, cs0:cs0 + CH],
                        AL.add, AL.mult), base=last_t1[par])
            t1 = bd.run(V, lambda e, hc=hc, sa=sa, sk=sk: e.tensor_tensor(
                sk, P1(hc), sa, AL.mult), inc=("s_dv", 1))
            attn_t1.append(t1)
            last_t1[par] = t1[1]
            y_toks.append(bd.run(V, lambda e, hc=hc, sk=sk: e.tensor_tensor(
                YB(hc), sk, P2M(hc), AL.subtract), inc=("s_dv", 1)))
        free_tok["b_arena"] = y_toks[-1]
        free_tok["d_arena"] = attn_t1[7]
        free_tok["d_arena2"] = attn_t1[15]

        for j in range(16):
            wb_i = j % 2
            bd.dep(P, wring_free[wb_i])
            t_wd = bd.run(P, lambda e, j=j, wb_i=wb_i: e.dma_start(
                out=WBUF[:, wb_i * 2048:wb_i * 2048 + 2048],
                in_=w_out_t[j].rearrange("p dc q -> p (dc q)")),
                inc=("s_w0" if wb_i == 0 else "s_w1", 16))
            bn = "mmA" if wb_i == 0 else "mmB"
            bd.dep(PE, t_wd)
            bd.dep(PE, bank_free[bn])
            t_mj = None
            for ch in range(DC):
                if ch % 4 == 0:
                    bd.dep(PE, y_toks[min(ch + 3, DC - 1)])
                wv = WBUF[:, wb_i * 2048 + ch * 128:wb_i * 2048 + (ch + 1) * 128]
                t_mj = pe_mm(ps[bn][:, 0:512], wv, YB(ch), start=(ch == 0), stop=(ch == DC - 1),
                             inc=("s_pe", 1) if ch == DC - 1 else None)
            wring_free[wb_i] = t_mj
            bd.dep(V, t_mj)
            bd.dep(V, free_tok["scr_wo"])
            scr = SCR[:, 0:512]
            xs = Xc(j)[:, pc0:pc0 + PW]
            t_e1 = bd.run(V, lambda e, bn=bn, scr=scr, xs=xs: e.scalar_tensor_tensor(
                scr, ps[bn][:, 0:512], 0.0, xs, AL.bypass, AL.add), inc=("s_dv", 1))
            bank_free[bn] = t_e1
            bd.dep(ACT, t_e1)
            t_e2 = bd.run(ACT, lambda e, scr=scr, xs=xs: e.copy(xs, scr), inc=("s_ac", 1))
            free_tok["scr_wo"] = t_e2
            xup_toks.append(t_e2)
            x_ready[j].append(t_e2)

    emit_ffn(0)
    emit_attn_pair(0)
    emit_attn_pair(1)
    emit_ffn(1)

    for dc in range(DC):
        bd.deps(SP, x_ready[dc])
        bd.run(SP, lambda e, dc=dc: e.dma_start(out=out[dc], in_=Xc(dc)), inc=("s_st", 16))
    bd.dep(SP, ("s_st", bd.cnt["s_st"]))

    with ExitStack() as stack:
        handles = {}
        for name in bd.sem_names:
            handles[name] = stack.enter_context(nc.semaphore(name))
        block = stack.enter_context(nc.Block())

        @block.tensor
        def _(t):
            bd.replay("pe", t, handles)

        @block.scalar
        def _(s):
            bd.replay("act", s, handles)

        @block.vector
        def _(v):
            bd.replay("dve", v, handles)

        @block.gpsimd
        def _(g):
            bd.replay("pool", g, handles)

        @block.sync
        def _(sy):
            bd.replay("sp", sy, handles)

    return nc


def kernel(L0_attn_x, L0_attn_w_in, L0_attn_w_out, L0_ffn_switch_w, L0_ffn_switch_b,
           L0_ffn_w1, L0_ffn_b1, L0_ffn_w2, L0_ffn_b2, L0_ffn_w3, L0_ffn_b3):
    x = np.ascontiguousarray(np.asarray(L0_attn_x, dtype=np.float32))
    w_in = np.ascontiguousarray(np.asarray(L0_attn_w_in, dtype=np.float32))
    w_out = np.ascontiguousarray(np.asarray(L0_attn_w_out, dtype=np.float32))
    sw = np.ascontiguousarray(np.asarray(L0_ffn_switch_w, dtype=np.float32))
    swb = np.ascontiguousarray(np.asarray(L0_ffn_switch_b, dtype=np.float32))
    w1 = np.ascontiguousarray(np.asarray(L0_ffn_w1, dtype=np.float32))
    b1 = np.ascontiguousarray(np.asarray(L0_ffn_b1, dtype=np.float32))
    w2 = np.ascontiguousarray(np.asarray(L0_ffn_w2, dtype=np.float32))
    b2 = np.ascontiguousarray(np.asarray(L0_ffn_b2, dtype=np.float32))
    w3 = np.ascontiguousarray(np.asarray(L0_ffn_w3, dtype=np.float32))
    b3 = np.ascontiguousarray(np.asarray(L0_ffn_b3, dtype=np.float32))

    wt = w_in.reshape(5, H, 128, DC, 128)
    w_in_t = np.ascontiguousarray(wt.transpose(1, 0, 4, 3, 2))
    wo = w_out.reshape(16, 128, DC, 128)
    w_out_t = np.ascontiguousarray(wo.transpose(0, 3, 2, 1))
    w3r = w3.reshape(16, 128, DC, 128)
    w3_t = np.ascontiguousarray(w3r.transpose(0, 3, 2, 1))
    sw3 = sw.reshape(8, 2048, 2048)
    w1T = np.ascontiguousarray(w1.transpose(0, 2, 1))
    w2T = np.ascontiguousarray(w2.transpose(0, 2, 1))
    w12T = np.ascontiguousarray(np.concatenate([w1T, w2T], axis=0).transpose(1, 0, 2))
    b1T = np.ascontiguousarray(b1.T)
    b2T = np.ascontiguousarray(b2.T)
    b3row = np.ascontiguousarray(b3[None, :])
    swb4 = np.ascontiguousarray(np.broadcast_to(swb[None, :], (4, 8)))

    nc = build_nc()

    in_maps = []
    for c in range(NCORES):
        sl = slice(256 * c, 256 * c + 256)
        x_fm = np.ascontiguousarray(
            x[:, sl, :].transpose(2, 0, 1).reshape(DC, 128, T))
        sw_c = np.ascontiguousarray(
            sw3[:, sl, :].transpose(0, 2, 1).reshape(8, DC, 128, TB).transpose(0, 2, 1, 3))
        oh = np.zeros((128, 8), np.float32)
        oh[:, c] = 1.0
        in_maps.append({
            "x_fm": x_fm, "w_in_t": w_in_t, "w_out_t": w_out_t, "w3_t": w3_t,
            "sw_t": sw_c, "w12T": w12T, "b1T": b1T, "b2T": b2T,
            "b3row": b3row, "swb": swb4, "onehot": oh,
        })

    res = run_bass_kernel_spmd(nc, in_maps, core_ids=list(range(NCORES)))

    outp = np.empty((4, 2048, 2048), np.float32)
    for c in range(NCORES):
        r = np.asarray(res.results[c]["out"])
        r = r.reshape(2048, 4, 256).transpose(1, 2, 0)
        outp[:, 256 * c:256 * c + 256, :] = r
    return outp


# version 18
# speedup vs baseline: 1.0587x; 1.0080x over previous
import numpy as np
from collections import defaultdict
from contextlib import ExitStack

import concourse.bass as bass
import concourse.mybir as mybir
from concourse.bass_utils import run_bass_kernel_spmd

F32 = mybir.dt.float32
BF16 = mybir.dt.bfloat16
AL = mybir.AluOpType
AF = mybir.ActivationFunctionType
AX = mybir.AxisListType

D = 2048
DC = 16
T = 1024
TB = 256
B = 4
H = 16
PW = 512
CH = 64
NCH = 4
LN_EPS = 1e-5
RMS_EPS = 1.1920928955078125e-07
NCORES = 8

ENGINES = ("pe", "act", "dve", "pool", "sp")


class Build:

    def __init__(self, nc):
        self.nc = nc
        self.streams = {e: [] for e in ENGINES}
        self.cnt = defaultdict(int)
        self.lastwait = {e: defaultdict(int) for e in ENGINES}
        self.sem_names = []

    def sem(self, name):
        if name not in self.sem_names:
            self.sem_names.append(name)
        return name

    def dep(self, eng, tok):
        if tok is None:
            return
        s, v = tok
        if v <= self.lastwait[eng][s]:
            return
        self.lastwait[eng][s] = v
        self.streams[eng].append(("wait", s, v))

    def deps(self, eng, toks):
        best = {}
        for t in toks:
            if t is None:
                continue
            s, v = t
            if v > best.get(s, -1):
                best[s] = v
        for s, v in best.items():
            self.dep(eng, (s, v))

    SERIALIZED = {"dve": "s_dvs", "act": "s_acs"}

    def run(self, eng, fn, inc=None, base=None):
        tok = None
        if eng in self.SERIALIZED:
            s = self.SERIALIZED[eng]
            self.sem(s)
            idx = self.cnt[s] + 1
            waitv = (idx - 1) if base is None else min(base, idx - 1)
            if waitv > 0:
                self.dep(eng, (s, waitv))
            self.cnt[s] = idx
            if inc is not None:
                tok = (s, idx)
            self.streams[eng].append(("op", fn, (s, 1)))
            return tok
        if inc is not None:
            s, amt = inc
            self.sem(s)
            self.cnt[s] += amt
            tok = (s, self.cnt[s])
        self.streams[eng].append(("op", fn, inc))
        return tok

    def vidx(self, eng):
        return self.cnt[self.SERIALIZED[eng]]

    def replay(self, eng, obj, handles):
        for item in self.streams[eng]:
            if item[0] == "wait":
                _, s, v = item
                obj.wait_ge(handles[s], v)
            else:
                _, fn, inc = item
                inst = fn(obj)
                if inc is not None:
                    s, amt = inc
                    inst.then_inc(handles[s], amt)


def build_nc():
    nc = bass.Bass("TRN2", target_bir_lowering=False, debug=False)

    def dinp(name, shape, dt=F32):
        return nc.dram_tensor(name, list(shape), dt, kind="ExternalInput").ap()

    x_fm = dinp("x_fm", [DC, 128, T])
    w_in_t = dinp("w_in_t", [H, 5, 128, DC, 128])
    w_out_t = dinp("w_out_t", [16, 128, DC, 128])
    w3_t = dinp("w3_t", [16, 128, DC, 128])
    sw_t = dinp("sw_t", [8, 128, DC, TB])
    w12T_in = dinp("w12T", [128, 16, 128])
    b1T_in = dinp("b1T", [128, 8])
    b2T_in = dinp("b2T", [128, 8])
    b3row_in = dinp("b3row", [1, D])
    swb_in = dinp("swb", [4, 8])
    onehot_in = dinp("onehot", [128, 8])

    out = nc.dram_tensor("out", [DC, 128, T], F32, kind="ExternalOutput").ap()

    rt_in = [nc.dram_tensor(f"rt_in{i}", [32], F32).ap() for i in range(2)]
    rt_out = [nc.dram_tensor(f"rt_out{i}", [32], F32).ap() for i in range(2)]
    sc_in = [nc.dram_tensor(f"sc_in{i}", [128, 64], F32).ap() for i in range(2)]
    sc_out = [nc.dram_tensor(f"sc_out{i}", [NCORES * 128, 64], F32).ap() for i in range(2)]
    selb = [nc.dram_tensor(f"selb{i}", [4, 40], F32).ap() for i in range(2)]

    def salloc(name, shape, dt):
        return nc.alloc_sbuf_tensor(name, list(shape), dt)

    X = salloc("X", [128, DC * T], F32)
    A_BF = salloc("A_BF", [128, 16384], BF16)
    B_BF = salloc("B_BF", [128, 8192], BF16)
    C_F = salloc("C_F", [128, 8192], F32)
    D_BF = salloc("D_BF", [128, 16384], BF16)
    WBUF = salloc("WBUF", [128, 2 * 2048], BF16)
    SCR = salloc("SCR", [128, 4 * 512], F32)
    SCB = salloc("SCB", [128, 3 * 512], BF16)
    ONES_F = salloc("ONES_F", [128, 128], F32)
    ONES_B = salloc("ONES_B", [128, 512], BF16)
    CONSTS = salloc("CONSTS", [128, 6], F32)
    S_LOC = salloc("S_LOC", [128, 64], F32)
    S_ALL = salloc("S_ALL", [128, NCORES * 64], F32)
    PREF = salloc("PREF", [128, 9 * 32], F32)
    SINIT = salloc("SINIT", [128, 32], F32)
    PFE = salloc("PFE", [128, 128], F32)
    APF = salloc("APF", [128, 128], F32)
    Z0 = salloc("Z0", [128, 128], F32)
    SREAL = salloc("SREAL", [128, 128], F32)
    CSC = salloc("CSC", [128, 64], F32)
    LGS = salloc("LGS", [1, 32], F32)
    LG = salloc("LG", [4, 64], F32)
    MSK = salloc("MSK", [4, 40], F32)
    GROW = salloc("GROW", [1, 160], F32)
    GSEL = salloc("GSEL", [128, 160], F32)
    B1T = salloc("B1T", [128, 8], F32)
    B2T = salloc("B2T", [128, 8], F32)
    B1S = salloc("B1S", [128, 8], F32)
    B2G = salloc("B2G", [128, 4], F32)
    SWB = salloc("SWB", [4, 8], F32)
    OH = salloc("OH", [128, 8], F32)
    SELS = salloc("SELS", [128, 2 * 128], BF16)
    SELF = salloc("SELF", [128, 2], F32)

    ps = {}
    def palloc(name, shape):
        ps[name] = nc.alloc_psum_tensor(name, list(shape), F32)
        return ps[name]

    palloc("mmA", [128, 512])
    palloc("mmB", [128, 512])
    palloc("bcA", [128, 512])
    palloc("bcB", [128, 512])
    palloc("st1", [128, 1024])
    palloc("st2", [128, 1024])

    bd = Build(nc)
    V, P, ACT, PE, SP = "dve", "pool", "act", "pe", "sp"
    for s in ("s_xin", "s_xi2", "s_sm", "s_exw", "s_b3", "s_io", "s_w0", "s_w1",
              "s_w2", "s_w3", "s_sw0", "s_sw1", "s_pe", "s_dv", "s_ac",
              "s_cc", "s_st", "s_dvs", "s_acs"):
        bd.sem(s)

    bank_free = {k: None for k in ps}
    wring_free = [None, None, None, None]
    BN4 = ["mmA", "mmB", "bcA", "bcB"]
    free_tok = defaultdict(lambda: None)

    def Xc(dc):
        return X[:, dc * T:(dc + 1) * T]

    def pe_mm(bank_ap, lhsT, rhs, start, stop, inc=None):
        return bd.run(PE, lambda e, bank_ap=bank_ap, lhsT=lhsT, rhs=rhs, start=start,
                      stop=stop: e.matmul(bank_ap, lhsT, rhs, start=start, stop=stop),
                      inc=inc)

    tx_toks = []
    for dc in range(DC):
        sem = "s_xin" if dc < 8 else "s_xi2"
        tx_toks.append(bd.run(SP, lambda e, dc=dc: e.dma_start(out=Xc(dc), in_=x_fm[dc]),
                              inc=(sem, 16)))
    t_smalls = None
    for (dst, src) in ((B1T, b1T_in), (B2T, b2T_in), (SWB, swb_in), (OH, onehot_in)):
        t_smalls = bd.run(SP, lambda e, d=dst, s=src: e.dma_start(out=d[:], in_=s),
                          inc=("s_sm", 16))
    bd.run(V, lambda e: e.memset(ONES_F[:], 1.0))
    bd.run(V, lambda e: e.memset(CONSTS[:, 0:1], LN_EPS))
    bd.run(V, lambda e: e.memset(CONSTS[:, 1:2], 2.0))
    bd.run(V, lambda e: e.memset(CONSTS[:, 2:3], RMS_EPS))
    bd.run(V, lambda e: e.memset(CONSTS[:, 3:4], 1.0 / 128.0))
    bd.run(V, lambda e: e.memset(CONSTS[:, 4:5], -0.5))
    bd.run(V, lambda e: e.memset(CONSTS[:, 5:6], -1.0))
    t_ones = bd.run(V, lambda e: e.memset(ONES_B[:], 1.0), inc=("s_dv", 1))
    bd.dep(ACT, t_ones)
    C_EPS = CONSTS[:, 0:1]
    C_TWO = CONSTS[:, 1:2]
    C_REPS = CONSTS[:, 2:3]
    C_I128 = CONSTS[:, 3:4]
    C_NH = CONSTS[:, 4:5]
    C_N1 = CONSTS[:, 5:6]

    xup_toks = list(tx_toks)
    x_ready = {dc: [("s_xin", 128) if dc < 8 else ("s_xi2", 128)] for dc in range(DC)}

    def emit_ln(xln_dst, col0, width, xbf_scr, x2bf_scr, ascr, nscr, apply_toks=None):
        nh = (width + 511) // 512
        bd.dep(PE, t_ones)
        bd.dep(PE, bank_free["st1"])
        bd.dep(PE, bank_free["st2"])
        pe_tok = [None, None]
        t_stats = None
        for dc in range(DC):
            xs = Xc(dc)[:, col0:col0 + width]
            bd.dep(V, pe_tok[dc % 2])
            bd.deps(V, x_ready[dc])
            t_c = bd.run(V, lambda e, dc=dc, xs=xs: e.tensor_copy(xbf_scr(dc), xs),
                         inc=("s_dv", 1))
            t_s = bd.run(V, lambda e, dc=dc: e.tensor_tensor(
                x2bf_scr(dc), xbf_scr(dc), xbf_scr(dc), AL.mult),
                base=t_c[1], inc=("s_dv", 1))
            bd.dep(PE, t_c)
            bd.dep(PE, t_s)
            for h in range(nh):
                w = min(512, width - h * 512)
                pe_mm(ps["st1"][:, h * 512:h * 512 + w], ONES_B[:, 0:128],
                      xbf_scr(dc)[:, h * 512:h * 512 + w], start=(dc == 0), stop=(dc == DC - 1))
                t_stats = pe_mm(ps["st2"][:, h * 512:h * 512 + w], ONES_B[:, 0:128],
                                x2bf_scr(dc)[:, h * 512:h * 512 + w],
                                start=(dc == 0), stop=(dc == DC - 1), inc=("s_pe", 1))
            pe_tok[dc % 2] = t_stats
        t_last = None
        t_v = None
        for h in range(nh):
            hs = h * 512
            w = min(512, width - hs)
            SM = SCR[0:1, 0:w]
            SV = SCR[0:1, 512:512 + w]
            S3 = SCR[0:1, 1024:1024 + w]
            S4 = SCR[0:1, 1536:1536 + w]
            bd.dep(V, t_stats)
            if h > 0:
                bd.dep(V, t_bc)
            bd.run(V, lambda e, hs=hs, w=w, SM=SM: e.tensor_scalar(
                SM, ps["st1"][0:1, hs:hs + w], 1.0 / D, None, AL.mult))
            t_v = bd.run(V, lambda e, hs=hs, w=w, SV=SV: e.tensor_scalar(
                SV, ps["st2"][0:1, hs:hs + w], 1.0 / D, None, AL.mult),
                inc=("s_dv", 1))
            bd.run(V, lambda e, SM=SM, S3=S3: e.tensor_tensor(S3, SM, SM, AL.mult))
            t_var = bd.run(V, lambda e, SV=SV, S3=S3, S4=S4: e.tensor_tensor(
                S4, SV, S3, AL.subtract), inc=("s_dv", 1))
            bd.dep(ACT, t_var)
            bd.run(ACT, lambda e, S3=S3, S4=S4: e.activation(
                S3, S4, AF.Ln, bias=CONSTS[0:1, 0:1]), inc=("s_ac", 1))
            t_sd = bd.run(ACT, lambda e, SV=SV, S3=S3: e.activation(
                SV, S3, AF.Exp, scale=CONSTS[0:1, 4:5]), inc=("s_ac", 1))
            bd.dep(V, t_sd)
            bd.run(V, lambda e, SM=SM, SV=SV, S3=S3: e.tensor_tensor(
                S3, SM, SV, AL.mult))
            t_rows = bd.run(V, lambda e, S3=S3, S4=S4: e.tensor_scalar(
                S4, S3, -1.0, None, AL.mult), inc=("s_dv", 1))
            bd.dep(PE, bank_free["bcA"])
            bd.dep(PE, bank_free["bcB"])
            bd.dep(PE, t_rows)
            pe_mm(ps["bcA"][:, 0:w], ONES_F[0:1, :], SV, True, True)
            t_bc = pe_mm(ps["bcB"][:, 0:w], ONES_F[0:1, :], S4,
                         True, True, inc=("s_pe", 1))
            bd.dep(V, t_bc)
            mt = {}
            at = {}
            g0 = bd.vidx("dve")
            for dc in range(DC):
                xs = Xc(dc)[:, col0 + hs:col0 + hs + w]
                scr = ascr(dc % nscr)[:, 0:w]
                bd.deps(V, x_ready[dc])
                mt[dc] = bd.run(V, lambda e, xs=xs, scr=scr, w=w: e.tensor_tensor(
                    scr, xs, ps["bcA"][:, 0:w], AL.mult),
                    base=(at[dc - nscr][1] if dc - nscr in at else g0),
                    inc=("s_dv", 1))
                if dc >= 1:
                    d2 = dc - 1
                    dst = xln_dst(d2)[:, hs:hs + w]
                    scr2 = ascr(d2 % nscr)[:, 0:w]
                    at[d2] = bd.run(V, lambda e, dst=dst, scr2=scr2, w=w: e.tensor_tensor(
                        dst, scr2, ps["bcB"][:, 0:w], AL.add), base=mt[d2][1],
                        inc=("s_dv", 1))
                    if apply_toks is not None:
                        apply_toks.setdefault(d2, []).append(at[d2])
            d2 = DC - 1
            dst = xln_dst(d2)[:, hs:hs + w]
            scr2 = ascr(d2 % nscr)[:, 0:w]
            t_last = bd.run(V, lambda e, dst=dst, scr2=scr2, w=w: e.tensor_tensor(
                dst, scr2, ps["bcB"][:, 0:w], AL.add), base=mt[d2][1],
                inc=("s_dv", 1))
            if apply_toks is not None:
                apply_toks.setdefault(d2, []).append(t_last)
            bank_free["bcA"] = t_last
            bank_free["bcB"] = t_last
        bank_free["st1"] = t_v
        bank_free["st2"] = t_v
        return t_last

    sw_free = [None, None]

    def emit_ffn(call):
        XLN = lambda dc: A_BF[:, dc * 1024:(dc + 1) * 1024]
        SWbuf = lambda i: D_BF[:, (i % 2) * 4096:(i % 2) * 4096 + 4096]
        Pbuf = lambda i: D_BF[:, 8192 + (i % 2) * 1024: 8192 + (i % 2) * 1024 + 1024]
        LNxb = lambda dc: D_BF[:, 8192 + (dc % 2) * 1024: 8192 + (dc % 2) * 1024 + 1024]
        LNx2 = lambda dc: D_BF[:, 10240 + (dc % 2) * 1024: 10240 + (dc % 2) * 1024 + 1024]
        EXW1 = lambda e_: B_BF[:, e_ * 128:(e_ + 1) * 128]
        EXW2 = lambda e_: B_BF[:, 1024 + e_ * 128: 1024 + (e_ + 1) * 128]
        SELW1 = lambda bk: B_BF[:, 2048 + bk * 128: 2048 + (bk + 1) * 128]
        SELW2 = lambda bk: B_BF[:, 3072 + bk * 128: 3072 + (bk + 1) * 128]
        HB = lambda i: B_BF[:, 4096 + (i % 2) * 256: 4096 + (i % 2) * 256 + 256]
        C_bf = C_F.bitcast(BF16)
        Y1 = lambda dc: C_bf[:, dc * 1024:(dc + 1) * 1024]

        bd.deps(V, [wring_free[2], wring_free[3]])
        D_F = D_BF.bitcast(F32)
        ln_toks = {}
        t_ln = emit_ln(XLN, 0, T, LNxb, LNx2,
                       lambda i: D_F[:, 4096 + i * 512: 4096 + (i + 1) * 512], 4,
                       apply_toks=ln_toks)

        bd.dep(P, free_tok["b_arena"])
        t_exw2 = bd.run(P, lambda e: e.dma_start(
            out=B_BF[:, 0:2048],
        in_=w12T_in.rearrange("p e q -> p (e q)")),
            inc=("s_exw", 16))

        sw_dma = [None, None]
        lg_toks = []
        pbuf_free = [None, None]
        for e_ in range(8):
            if e_ < 2:
                bd.dep(P, sw_free[e_ % 2])
                bd.dep(P, free_tok["d_arena"] if e_ == 0 else free_tok["d_arena2"])
                sw_dma[e_ % 2] = bd.run(P, lambda e, e_=e_: e.dma_start(
                    out=SWbuf(e_),
                    in_=sw_t[e_].rearrange("p dc t -> p (dc t)")),
                    inc=("s_sw0" if e_ % 2 == 0 else "s_sw1", 16))
            stats = "st1" if e_ % 2 == 0 else "st2"
            bd.dep(V, sw_dma[e_ % 2])
            t_mm = None
            rg0 = bd.vidx("dve")
            for dc in range(DC):
                swv = SWbuf(e_)[:, dc * 256:(dc + 1) * 256]
                sw_b = swv.rearrange("p (o t) -> p o t", o=1).broadcast_to([128, 4, 256])
                pb = Pbuf(dc)
                bd.dep(V, pbuf_free[dc % 2])
                bd.deps(V, ln_toks.get(dc, []))
                t_p = bd.run(V, lambda e, dc=dc, sw_b=sw_b, pb=pb, e_=e_: e.tensor_tensor(
                    pb.rearrange("p (b t) -> p b t", b=B),
                    XLN(dc).rearrange("p (b t) -> p b t", b=B), sw_b, AL.mult),
                    base=rg0, inc=("s_dv", 1))
                bd.dep(PE, bank_free[stats])
                bd.dep(PE, t_p)
                for hf in range(2):
                    t_mm = pe_mm(ps[stats][:, hf * 512:(hf + 1) * 512], ONES_B[:, 0:128],
                                 pb[:, hf * 512:(hf + 1) * 512],
                                 start=(dc == 0), stop=(dc == DC - 1),
                                 inc=("s_pe", 1) if hf == 1 else None)
                pbuf_free[dc % 2] = t_mm
                t_p_last = t_p
            sw_free[e_ % 2] = t_p_last
            if e_ + 2 < 8:
                bd.dep(P, sw_free[e_ % 2])
                sw_dma[e_ % 2] = bd.run(P, lambda e, e_=e_: e.dma_start(
                    out=SWbuf(e_ + 2),
                    in_=sw_t[e_ + 2].rearrange("p dc t -> p (dc t)")),
                    inc=("s_sw0" if e_ % 2 == 0 else "s_sw1", 16))
            bd.dep(V, t_mm)
            bd.dep(V, free_tok["LGS"])
            lt = None
            for b_ in range(B):
                lt = bd.run(V, lambda e, e_=e_, b_=b_, stats=stats: e.tensor_reduce(
                    LGS[:, b_ * 8 + e_: b_ * 8 + e_ + 1],
                    ps[stats][0:1, b_ * 256:(b_ + 1) * 256], AX.X, AL.add),
                    inc=("s_dv", 1) if b_ == B - 1 else None)
            bank_free[stats] = lt
            lg_toks.append(lt)

        B3R = D_BF[0:1, 0:2048]
        bd.dep(P, sw_free[0])
        t_b3 = bd.run(P, lambda e: e.dma_start(out=B3R, in_=b3row_in), inc=("s_b3", 16))

        bd.deps(SP, lg_toks)
        t_s1 = bd.run(SP, lambda e: e.dma_start(
            out=rt_in[call].rearrange("(o c) -> o c", o=1), in_=LGS[0:1, :]), inc=("s_io", 16))
        free_tok["LGS"] = t_s1
        bd.dep(P, t_s1)
        t_cc = bd.run(P, lambda e: e.collective_compute(
            "AllReduce", AL.add, replica_groups=[list(range(NCORES))],
            ins=[rt_in[call].opt()], outs=[rt_out[call].opt()]), inc=("s_cc", 1))
        bd.dep(SP, t_cc)
        t_lg = bd.run(SP, lambda e: e.dma_start(
            out=LG[:, 0:8], in_=rt_out[call].rearrange("(b e) -> b e", b=4)),
            inc=("s_io", 16))

        lg0 = LG[:, 0:8]; ex_ = LG[:, 8:16]; pr = LG[:, 16:24]; pr2 = LG[:, 24:32]
        cs = LG[:, 32:40]; tmp8 = LG[:, 40:48]; mx = LG[:, 48:49]; nmx = LG[:, 49:50]
        sm = LG[:, 50:51]; rs = LG[:, 51:52]
        msk1 = MSK[:, 0:8]; msk2 = MSK[:, 8:16]; gm1 = MSK[:, 16:24]
        gm2 = MSK[:, 24:32]; gmc = MSK[:, 32:40]
        bd.dep(V, t_lg)
        bd.dep(V, t_smalls)
        bd.dep(V, free_tok["MSK"])
        bd.run(V, lambda e: e.tensor_tensor(lg0, lg0, SWB[:], AL.add))
        bd.run(V, lambda e: e.reduce_max(mx, lg0, AX.X))
        t_nm = bd.run(V, lambda e: e.tensor_scalar(nmx, mx, -1.0, None, AL.mult), inc=("s_dv", 1))
        bd.dep(ACT, t_nm)
        t_ex = bd.run(ACT, lambda e: e.activation(ex_, lg0, AF.Exp, bias=nmx[:, 0:1]),
                      inc=("s_ac", 1))
        bd.dep(V, t_ex)
        bd.run(V, lambda e: e.reduce_sum(sm, ex_, AX.X))
        bd.run(V, lambda e: e.reciprocal(rs, sm))
        bd.run(V, lambda e: e.tensor_scalar(pr, ex_, rs[:, 0:1], None, AL.mult))
        bd.run(V, lambda e: e.reduce_max(mx, pr, AX.X))
        bd.run(V, lambda e: e.tensor_scalar(tmp8, pr, mx[:, 0:1], None, AL.is_ge))
        bd.run(V, lambda e: e.tensor_tensor_scan(cs, ONES_F[0:4, 0:8], tmp8, 0.0, AL.mult, AL.add))
        bd.run(V, lambda e: e.tensor_scalar(cs, cs, 1.0, None, AL.is_equal))
        bd.run(V, lambda e: e.tensor_tensor(msk1, tmp8, cs, AL.mult))
        bd.run(V, lambda e: e.tensor_tensor(gm1, pr, msk1, AL.mult))
        bd.run(V, lambda e: e.tensor_scalar(tmp8, msk1, -1.0, 1.0, AL.mult, AL.add))
        bd.run(V, lambda e: e.tensor_tensor(pr2, pr, tmp8, AL.mult))
        bd.run(V, lambda e: e.reduce_max(mx, pr2, AX.X))
        bd.run(V, lambda e: e.tensor_scalar(tmp8, pr2, mx[:, 0:1], None, AL.is_ge))
        bd.run(V, lambda e: e.tensor_tensor_scan(cs, ONES_F[0:4, 0:8], tmp8, 0.0, AL.mult, AL.add))
        bd.run(V, lambda e: e.tensor_scalar(cs, cs, 1.0, None, AL.is_equal))
        bd.run(V, lambda e: e.tensor_tensor(msk2, tmp8, cs, AL.mult))
        bd.run(V, lambda e: e.tensor_tensor(gm2, pr2, msk2, AL.mult))
        t_msk = bd.run(V, lambda e: e.tensor_tensor(gmc, gm1, gm2, AL.add), inc=("s_dv", 1))

        bd.dep(SP, t_msk)
        t_sb = bd.run(SP, lambda e: e.dma_start(out=selb[call], in_=MSK[:, 0:40]),
                      inc=("s_io", 16))
        free_tok["MSK"] = t_sb
        bd.dep(SP, t_sb)
        t_sr = bd.run(SP, lambda e: e.dma_start(
            out=GROW[:], in_=selb[call].rearrange("b c -> (b c)")[None, :]), inc=("s_io", 16))
        bd.dep(PE, t_sr)
        bd.dep(PE, bank_free["bcA"])
        t_gb = pe_mm(ps["bcA"][:, 0:160], ONES_F[0:1, :], GROW[:], True, True, inc=("s_pe", 1))
        bd.dep(V, t_gb)
        t_gs = bd.run(V, lambda e: e.tensor_copy(GSEL[:, 0:160], ps["bcA"][:, 0:160]),
                      inc=("s_dv", 1))
        bank_free["bcA"] = t_gs

        bd.dep(V, t_exw2)
        sel_last = None
        chains = []
        for b_ in range(B):
            for k_ in range(2):
                bk = b_ * 2 + k_
                moff = b_ * 40 + (0 if k_ == 0 else 8)
                goff = b_ * 40 + (16 if k_ == 0 else 24)
                chains.append((SELW1(bk), EXW1, moff))
                chains.append((SELW2(bk), EXW2, goff))
        for grp in range(0, len(chains), 4):
            batch = chains[grp:grp + 4]
            prev = [None] * len(batch)
            for e_ in range(8):
                for lane, (dst, src, off) in enumerate(batch):
                    pp = [B_BF[:, 6144 + lane * 256:6144 + lane * 256 + 128],
                          B_BF[:, 6144 + lane * 256 + 128:6144 + lane * 256 + 256]]
                    o = dst if e_ == 7 else pp[e_ % 2]
                    if e_ == 0:
                        prev[lane] = bd.run(V, lambda e, o=o, src=src, off=off:
                            e.tensor_scalar(o, src(0), GSEL[:, off:off + 1], None,
                            AL.mult), base=max(free_tok["sel_lane"] or 0, t_gs[1]),
                            inc=("s_dv", 1))
                    else:
                        prev[lane] = bd.run(V, lambda e, o=o, src=src, e_=e_, off=off,
                            i=pp[(e_ + 1) % 2]: e.scalar_tensor_tensor(
                            o, src(e_), GSEL[:, off + e_: off + e_ + 1], i,
                            AL.mult, AL.add), base=prev[lane][1], inc=("s_dv", 1))
            sel_last = prev[-1]
            free_tok["sel_lane"] = max(t[1] for t in prev)
        bchains = []
        for b_ in range(B):
            for k_ in range(2):
                bk = b_ * 2 + k_
                moff = b_ * 40 + (0 if k_ == 0 else 8)
                bchains.append((B1S[:, bk:bk + 1], B1T, moff))
        for b_ in range(B):
            bchains.append((B2G[:, b_:b_ + 1], B2T, b_ * 40 + 32))
        for grp in range(0, len(bchains), 4):
            batch = bchains[grp:grp + 4]
            prev = [None] * len(batch)
            for e_ in range(8):
                for lane, (dst, tsrc, off) in enumerate(batch):
                    ppf = [PREF[:, 272 + 2 * lane:273 + 2 * lane],
                           PREF[:, 273 + 2 * lane:274 + 2 * lane]]
                    o = dst if e_ == 7 else ppf[e_ % 2]
                    if e_ == 0:
                        prev[lane] = bd.run(V, lambda e, o=o, tsrc=tsrc, off=off:
                            e.tensor_scalar(o, tsrc[:, 0:1], GSEL[:, off:off + 1], None,
                            AL.mult), base=max(free_tok["selb_lane"] or 0, t_gs[1]),
                            inc=("s_dv", 1))
                    else:
                        prev[lane] = bd.run(V, lambda e, o=o, tsrc=tsrc, e_=e_, off=off,
                            i=ppf[(e_ + 1) % 2]: e.scalar_tensor_tensor(
                            o, tsrc[:, e_:e_ + 1], GSEL[:, off + e_: off + e_ + 1], i,
                            AL.mult, AL.add), base=prev[lane][1], inc=("s_dv", 1))
            sel_last = prev[-1]
            free_tok["selb_lane"] = max(t[1] for t in prev)
        t_sel = sel_last

        y1_toks = []
        obanks = ["mmA", "mmB", "bcA", "bcB"]
        hbanks = ["st1", "st2"]
        HB8 = lambda s: B_BF[:, 4096 + s * 256: 4096 + (s + 1) * 256]
        hb_i = 0
        bd.dep(PE, t_sel)
        bd.dep(PE, t_ln)
        for b_ in range(B):
            for g in range(4):
                hcs = [g * 4 + i for i in range(4)]
                gel = {}
                mm1t = {}
                for k_ in range(2):
                    bk = b_ * 2 + k_
                    for i, hc in enumerate(hcs):
                        s = k_ * 4 + i
                        hname = hbanks[hb_i % 2]
                        hb = ps[hname]
                        bd.dep(PE, bank_free[hname])
                        t_mm1 = pe_mm(hb[:, 0:256], SELW1(bk),
                                      XLN(hc)[:, b_ * 256:(b_ + 1) * 256], True, True,
                                      inc=("s_pe", 1))
                        bd.dep(ACT, t_mm1)
                        t_h = bd.run(ACT, lambda e, hb=hb, bk=bk, s=s: e.activation(
                            HB8(s), hb[:, 0:256], AF.Gelu, bias=B1S[:, bk:bk + 1]),
                            inc=("s_ac", 1))
                        bank_free[hname] = t_h
                        gel[s] = t_h
                        hb_i += 1
                for k_ in range(2):
                    bk = b_ * 2 + k_
                    for i, hc in enumerate(hcs):
                        s = k_ * 4 + i
                        ob = ps[obanks[i]]
                        bd.dep(PE, gel[s])
                        if k_ == 0:
                            bd.dep(PE, bank_free[obanks[i]])
                        t_mm2 = pe_mm(ob[:, 0:256], SELW2(bk), HB8(s),
                                      start=(k_ == 0), stop=(k_ == 1),
                                      inc=("s_pe", 1))
                        if k_ == 1:
                            bd.dep(V, t_mm2)
                            bd.dep(V, t_sel)
                            t_y = bd.run(V, lambda e, ob=ob, hc=hc, b_=b_: e.scalar_tensor_tensor(
                                Y1(hc)[:, b_ * 256:(b_ + 1) * 256], ob[:, 0:256],
                                B2G[:, b_:b_ + 1],
                                XLN(hc)[:, b_ * 256:(b_ + 1) * 256], AL.add, AL.add),
                                inc=("s_dv", 1))
                            bank_free[obanks[i]] = t_y
                            y1_toks.append(t_y)
        free_tok["a_ext"] = t_mm2
        for j in range(16):
            wb_i = j % 2
            bd.dep(P, wring_free[wb_i])
            t_wd = bd.run(P, lambda e, j=j, wb_i=wb_i: e.dma_start(
                out=WBUF[:, wb_i * 2048:wb_i * 2048 + 2048],
                in_=w3_t[j].rearrange("p dc q -> p (dc q)")),
                inc=("s_w0" if wb_i == 0 else "s_w1", 16))
            bd.dep(PE, t_wd)
            bd.dep(PE, t_b3)
            bd.deps(PE, y1_toks)
            bnA = BN4[(j % 2) * 2]
            bnB = BN4[(j % 2) * 2 + 1]
            bd.dep(PE, bank_free[bnA])
            bd.dep(PE, bank_free[bnB])
            for dc in range(DC):
                wv = WBUF[:, wb_i * 2048 + dc * 128:wb_i * 2048 + (dc + 1) * 128]
                pe_mm(ps[bnA][:, 0:512], wv, Y1(dc)[:, 0:512], start=(dc == 0), stop=False)
                pe_mm(ps[bnB][:, 0:512], wv, Y1(dc)[:, 512:1024], start=(dc == 0), stop=False)
            t_mja = bd.run(PE, lambda e, bnA=bnA, j=j, B3R=B3R: e.matmul(
                ps[bnA][:, 0:512], B3R[:, j * 128:(j + 1) * 128],
                ONES_B[0:1, 0:512], start=False, stop=True), inc=("s_pe", 1))
            t_mjb = bd.run(PE, lambda e, bnB=bnB, j=j, B3R=B3R: e.matmul(
                ps[bnB][:, 0:512], B3R[:, j * 128:(j + 1) * 128],
                ONES_B[0:1, 0:512], start=False, stop=True), inc=("s_pe", 1))
            wring_free[wb_i] = t_mjb
            for hf, (bn, t_mj) in enumerate(((bnA, t_mja), (bnB, t_mjb))):
                bd.dep(V, t_mj)
                bd.dep(V, free_tok["scr_w3"])
                scr = SCR[:, 512:1024]
                xs = Xc(j)[:, hf * 512:hf * 512 + 512]
                t_e1 = bd.run(V, lambda e, bn=bn, scr=scr, xs=xs: e.scalar_tensor_tensor(
                    scr, ps[bn][:, 0:512], 0.5, xs, AL.mult, AL.add), inc=("s_dv", 1))
                bank_free[bn] = t_e1
                bd.dep(ACT, t_e1)
                t_e2 = bd.run(ACT, lambda e, scr=scr, xs=xs: e.copy(xs, scr), inc=("s_ac", 1))
                free_tok["scr_w3"] = t_e2
                xup_toks.append(t_e2)
                x_ready[j].append(t_e2)

    def emit_attn_pair(p):
        pc0 = p * PW
        A_F = A_BF.bitcast(F32)
        C_BF = C_F.bitcast(BF16)
        XLNP = lambda dc: A_BF[:, dc * 512:(dc + 1) * 512]
        YB = lambda hc: A_BF[:, hc * 512:(hc + 1) * 512]
        PREFIX = lambda hc: B_BF[:, hc * 512:(hc + 1) * 512]
        ACC = lambda hc: C_BF[:, hc * 512:(hc + 1) * 512]
        P1 = lambda hc: D_BF[:, hc * 512:(hc + 1) * 512]
        P2M = lambda hc: D_BF[:, 8192 + hc * 512: 8192 + (hc + 1) * 512]
        LNxb = lambda dc: D_BF[:, (dc % 2) * 512:(dc % 2) * 512 + 512]
        LNx2 = lambda dc: D_BF[:, 1024 + (dc % 2) * 512:1024 + (dc % 2) * 512 + 512]
        SSET = [
            dict(sig=SCR[:, 0:512], a=SCR[:, 512:1024], k=SCR[:, 1024:1536],
                 u=SCR[:, 1536:2048], pf=A_F[:, 6144:6656], x2=A_F[:, 6656:7168],
                 qb=SCB[:, 0:512], vb=SCB[:, 512:1024], q2b=SCB[:, 1024:1536],
                 stb="st1"),
            dict(sig=C_F[:, 4096:4608], a=C_F[:, 4608:5120], k=C_F[:, 5120:5632],
                 u=C_F[:, 5632:6144], pf=C_F[:, 6144:6656], x2=C_F[:, 6656:7168],
                 qb=C_BF[:, 14336:14848], vb=C_BF[:, 14848:15360],
                 q2b=C_BF[:, 15360:15872], stb="st2"),
        ]
        WR = [WBUF[:, 0:2048], WBUF[:, 2048:4096],
              A_BF[:, 8192:10240], A_BF[:, 10240:12288]]
        WSEM = ["s_w0", "s_w1", "s_w2", "s_w3"]

        D_F = D_BF.bitcast(F32)
        lnp_toks = {}
        t_lnp = emit_ln(XLNP, pc0, PW, LNxb, LNx2,
                        lambda i: D_F[:, i * 512:(i + 1) * 512], 2,
                        apply_toks=lnp_toks)

        t_bc_toks = []
        phead_toks = []
        prev_tacc = [None, None]
        prev_lnpf = [None, None]
        prev_p2m = [None, None]
        bd.dep(P, free_tok["a_ext"])
        bd.dep(V, free_tok["S_LOC"])
        for hc in range(H):
            par = hc % 2
            S = SSET[par]
            s_sig, s_a, s_k, s_u = S["sig"], S["a"], S["k"], S["u"]
            s_pf, s_x2 = S["pf"], S["x2"]
            QBF, VBF, Q2BF = S["qb"], S["vb"], S["q2b"]
            stb = S["stb"]
            evac = {}
            for sidx, jp in enumerate((3, 1, 2, 4, 0)):
                wb_i = (p * 80 + hc * 5 + sidx + 2) % 4
                bd.dep(P, wring_free[wb_i])
                t_wd = bd.run(P, lambda e, hc=hc, jp=jp, wb_i=wb_i: e.dma_start(
                    out=WR[wb_i],
                    in_=w_in_t[hc, jp].rearrange("p dc q -> p (dc q)")),
                    inc=(WSEM[wb_i], 16))
                bn = BN4[(hc * 5 + sidx) % 4]
                bd.dep(PE, t_wd)
                bd.dep(PE, bank_free[bn])
                t_mm = None
                for dc in range(DC):
                    bd.deps(PE, lnp_toks.get(dc, []))
                    wv = WR[wb_i][:, dc * 128:(dc + 1) * 128]
                    t_mm = pe_mm(ps[bn][:, 0:512], wv, XLNP(dc),
                                 start=(dc == 0), stop=(dc == DC - 1),
                                 inc=("s_pe", 1) if dc == DC - 1 else None)
                wring_free[wb_i] = t_mm
                pb = ps[bn][:, 0:512]
                if jp == 3:
                    bd.dep(ACT, t_mm)
                    bd.dep(ACT, prev_tacc[par])
                    t1 = bd.run(ACT, lambda e, pb=pb, s_sig=s_sig: e.activation(
                        s_sig, pb, AF.Sigmoid, bias=C_TWO), inc=("s_ac", 1))
                    bank_free[bn] = t1
                    bd.dep(V, t1)
                    evac["a"] = bd.run(V, lambda e, s_a=s_a, s_sig=s_sig: e.tensor_scalar(
                        s_a, s_sig, 0.6, 0.9995, AL.max, AL.min), inc=("s_dv", 1))
                elif jp == 1:
                    bd.dep(ACT, t_mm)
                    t1 = bd.run(ACT, lambda e, pb=pb, s_k=s_k: e.activation(
                        s_k, pb, AF.Tanh), inc=("s_ac", 1))
                    bank_free[bn] = t1
                    evac["k"] = t1
                elif jp == 2:
                    bd.dep(ACT, t_mm)
                    t1 = bd.run(ACT, lambda e, pb=pb, VBF=VBF: e.copy(VBF, pb),
                                inc=("s_ac", 1))
                    bank_free[bn] = t1
                    bd.dep(V, t1)
                    bd.dep(V, evac["k"])
                    evac["u"] = bd.run(V, lambda e, s_u=s_u, s_k=s_k, VBF=VBF:
                                       e.tensor_tensor(s_u, s_k, VBF, AL.mult),
                                       inc=("s_dv", 1))
                elif jp == 4:
                    bd.dep(ACT, t_mm)
                    bd.dep(ACT, evac["a"])
                    t1 = bd.run(ACT, lambda e, pb=pb, s_sig=s_sig: e.activation(
                        s_sig, pb, AF.Sigmoid), inc=("s_ac", 1))
                    bank_free[bn] = t1
                    evac["g"] = t1
                else:
                    bd.dep(ACT, t_mm)
                    bd.dep(ACT, prev_p2m[par])
                    t1 = bd.run(ACT, lambda e, pb=pb, QBF=QBF: e.copy(QBF, pb),
                                inc=("s_ac", 1))
                    t1b = bd.run(ACT, lambda e, pb=pb, Q2BF=Q2BF: e.activation(
                        Q2BF, pb, AF.Square), inc=("s_ac", 1))
                    bank_free[bn] = t1b
                    bd.dep(PE, t1b)
                    bd.dep(PE, bank_free[stb])
                    t_q2 = pe_mm(ps[stb][:, 0:512], ONES_B[:, 0:128], Q2BF, True, True,
                                 inc=("s_pe", 1))
                    bd.dep(ACT, t_q2)
                    t_r1 = bd.run(ACT, lambda e, s_x2=s_x2, stb=stb: e.activation(
                        s_x2, ps[stb][:, 0:512], AF.Ln, bias=C_REPS, scale=C_I128),
                        inc=("s_ac", 1))
                    bank_free[stb] = t_r1
                    bd.dep(ACT, evac["u"])
                    t_r2 = bd.run(ACT, lambda e, s_k=s_k, s_x2=s_x2: e.activation(
                        s_k, s_x2, AF.Exp, scale=C_NH), inc=("s_ac", 1))
                    bd.dep(V, t_r2)
                    bd.dep(V, t_q2)
                    bd.run(V, lambda e, Q2BF=Q2BF, QBF=QBF, s_k=s_k:
                           e.tensor_tensor(Q2BF, QBF, s_k, AL.mult))
                    bd.dep(V, evac["g"])
                    bd.run(V, lambda e, hc=hc, Q2BF=Q2BF, s_sig=s_sig:
                           e.tensor_tensor(P1(hc), Q2BF, s_sig, AL.mult))
                    t_p2m = bd.run(V, lambda e, hc=hc, s_sig=s_sig, VBF=VBF:
                                   e.scalar_tensor_tensor(P2M(hc), s_sig, 1.0, VBF,
                                   AL.subtract, AL.mult), inc=("s_dv", 1))
                    phead_toks.append(t_p2m)
                    prev_p2m[par] = t_p2m

            bd.dep(V, t_ones)
            bd.dep(V, prev_lnpf[par])
            clip_i = evac["a"][1]
            for b_ in range(2):
                for c in range(NCH):
                    cs0 = b_ * 256 + c * CH
                    t_scan = bd.run(V, lambda e, cs0=cs0, s_pf=s_pf, s_a=s_a:
                        e.tensor_tensor_scan(
                        s_pf[:, cs0:cs0 + CH], s_a[:, cs0:cs0 + CH],
                        ONES_F[:, 0:CH], 1.0, AL.mult, AL.mult), base=clip_i,
                        inc=("s_dv", 1) if (b_ == 1 and c == NCH - 1) else None)
            scan_i = bd.vidx(V)
            bd.run(V, lambda e, hc=hc, s_pf=s_pf: e.tensor_copy(PREFIX(hc), s_pf),
                   base=scan_i)
            for b_ in range(2):
                base = hc * 8 + b_ * 4
                for c in range(NCH):
                    bd.run(V, lambda e, base=base, c=c, b_=b_, s_pf=s_pf: e.tensor_copy(
                        PFE[:, base + c:base + c + 1],
                        s_pf[:, b_ * 256 + c * CH + CH - 1: b_ * 256 + c * CH + CH]),
                        base=scan_i)
            for b_ in range(2):
                base = hc * 8 + b_ * 4
                bd.run(V, lambda e, base=base: e.tensor_copy(
                    APF[:, base:base + 1], ONES_F[:, 0:1]))
                for c in range(1, NCH):
                    bd.run(V, lambda e, base=base, c=c: e.tensor_tensor(
                        APF[:, base + c:base + c + 1], APF[:, base + c - 1:base + c],
                        PFE[:, base + c - 1:base + c], AL.mult))
                bd.run(V, lambda e, hc=hc, b_=b_, base=base: e.tensor_tensor(
                    S_LOC[:, hc * 2 + b_:hc * 2 + b_ + 1],
                    APF[:, base + 3:base + 4], PFE[:, base + 3:base + 4], AL.mult))
            bd.dep(ACT, t_scan)
            bd.dep(ACT, phead_toks[-1])
            t_lp = bd.run(ACT, lambda e, s_x2=s_x2, s_pf=s_pf: e.activation(
                s_x2, s_pf, AF.Ln), inc=("s_ac", 1))
            prev_lnpf[par] = t_lp
            t_rp = bd.run(ACT, lambda e, s_sig=s_sig, s_x2=s_x2: e.activation(
                s_sig, s_x2, AF.Exp, scale=C_N1), inc=("s_ac", 1))
            bd.dep(V, t_rp)
            t_term = bd.run(V, lambda e, s_a=s_a, s_sig=s_sig, s_u=s_u:
                            e.scalar_tensor_tensor(s_a, s_sig, 1e6, s_u,
                            AL.min, AL.mult), inc=("s_dv", 1))
            term_i = t_term[1]
            t_acc = None
            for b_ in range(2):
                for c in range(NCH):
                    cs0 = b_ * 256 + c * CH
                    t_acc = bd.run(V, lambda e, hc=hc, cs0=cs0, s_a=s_a:
                        e.tensor_tensor_scan(
                        ACC(hc)[:, cs0:cs0 + CH], ONES_F[:, 0:CH],
                        s_a[:, cs0:cs0 + CH], 0.0, AL.mult, AL.add), base=term_i,
                        inc=("s_dv", 1) if (b_ == 1 and c == NCH - 1) else None)
            for b_ in range(2):
                base = hc * 8 + b_ * 4
                bd.run(V, lambda e, base=base: e.memset(Z0[:, base:base + 1], 0.0))
                for c in range(NCH):
                    accv = ACC(hc)[:, b_ * 256 + c * CH + CH - 1: b_ * 256 + c * CH + CH]
                    dst = (Z0[:, base + c + 1:base + c + 2] if c < NCH - 1
                           else S_LOC[:, 32 + hc * 2 + b_: 32 + hc * 2 + b_ + 1])
                    tok = bd.run(V, lambda e, dst=dst, base=base, c=c, accv=accv:
                                 e.scalar_tensor_tensor(dst,
                                 accv, Z0[:, base + c:base + c + 1],
                                 PFE[:, base + c:base + c + 1], AL.add, AL.mult),
                                 inc=("s_dv", 1) if c == NCH - 1 else None)
            t_bc_toks.append(tok)
            phead_toks.append(t_acc)
            prev_tacc[par] = tok

        bd.deps(SP, t_bc_toks)
        t_s1 = bd.run(SP, lambda e, p=p: e.dma_start(out=sc_in[p], in_=S_LOC[:]),
                      inc=("s_io", 16))
        free_tok["S_LOC"] = t_s1
        bd.dep(P, t_s1)
        t_cc = bd.run(P, lambda e, p=p: e.collective_compute(
            "AllGather", AL.bypass, replica_groups=[list(range(NCORES))],
            ins=[sc_in[p].opt()], outs=[sc_out[p].opt()]), inc=("s_cc", 1))
        bd.dep(SP, t_cc)
        bd.dep(SP, free_tok["S_ALL"])
        t_sa = bd.run(SP, lambda e, p=p: e.dma_start(
            out=S_ALL[:].rearrange("p (r c) -> p r c", r=NCORES),
            in_=sc_out[p].rearrange("(r p) c -> p r c", p=128)), inc=("s_io", 16))

        bd.dep(V, t_sa)
        bd.run(V, lambda e: e.memset(PREF[:, 0:32], 0.0))
        for r in range(NCORES):
            Ar = S_ALL[:, r * 64:r * 64 + 32]
            Br = S_ALL[:, r * 64 + 32:r * 64 + 64]
            bd.run(V, lambda e, Ar=Ar, r=r: e.tensor_tensor(
                CSC[:, 0:32], Ar, PREF[:, r * 32:(r + 1) * 32], AL.mult))
            bd.run(V, lambda e, Br=Br, r=r: e.tensor_tensor(
                PREF[:, (r + 1) * 32:(r + 2) * 32], CSC[:, 0:32], Br, AL.add))
        pp = [CSC[:, 0:32], CSC[:, 32:64]]
        for r in range(NCORES):
            o = SINIT[:] if r == NCORES - 1 else pp[r % 2]
            if r == 0:
                bd.run(V, lambda e, o=o: e.tensor_scalar(
                    o, PREF[:, 0:32], OH[:, 0:1], None, AL.mult))
            else:
                bd.run(V, lambda e, o=o, r=r, i=pp[(r + 1) % 2]: e.scalar_tensor_tensor(
                    o, PREF[:, r * 32:(r + 1) * 32], OH[:, r:r + 1], i, AL.mult, AL.add))
        sr0 = bd.vidx("dve")
        for hc in range(H):
            for b_ in range(2):
                base = hc * 8 + b_ * 4
                sic = SINIT[:, hc * 2 + b_:hc * 2 + b_ + 1]
                for c in range(NCH):
                    bd.run(V, lambda e, base=base, c=c, sic=sic: e.scalar_tensor_tensor(
                        SREAL[:, base + c:base + c + 1], APF[:, base + c:base + c + 1],
                        sic, Z0[:, base + c:base + c + 1], AL.mult, AL.add), base=sr0)
        t_sreal = bd.run(V, lambda e: e.engine_nop(), inc=("s_dv", 1))
        free_tok["S_ALL"] = t_sreal

        y_toks = []
        attn_t1 = []
        bd.dep(V, t_sreal)
        bd.deps(V, wring_free)
        g0 = bd.vidx("dve")
        last_t1 = [g0, g0]
        for hc in range(H):
            par = hc % 2
            sa = SSET[par]["a"]
            sk = SSET[par]["k"]
            for b_ in range(2):
                base = hc * 8 + b_ * 4
                for c in range(NCH):
                    cs0 = b_ * 256 + c * CH
                    bd.run(V, lambda e, hc=hc, cs0=cs0, base=base, c=c, sa=sa:
                        e.scalar_tensor_tensor(
                        sa[:, cs0:cs0 + CH], ACC(hc)[:, cs0:cs0 + CH],
                        SREAL[:, base + c:base + c + 1], PREFIX(hc)[:, cs0:cs0 + CH],
                        AL.add, AL.mult), base=last_t1[par])
            t1 = bd.run(V, lambda e, hc=hc, sa=sa, sk=sk: e.tensor_tensor(
                sk, P1(hc), sa, AL.mult), inc=("s_dv", 1))
            attn_t1.append(t1)
            last_t1[par] = t1[1]
            y_toks.append(bd.run(V, lambda e, hc=hc, sk=sk: e.tensor_tensor(
                YB(hc), sk, P2M(hc), AL.subtract), inc=("s_dv", 1)))
        free_tok["b_arena"] = y_toks[-1]
        free_tok["d_arena"] = attn_t1[7]
        free_tok["d_arena2"] = attn_t1[15]

        for j in range(16):
            wb_i = j % 2
            bd.dep(P, wring_free[wb_i])
            t_wd = bd.run(P, lambda e, j=j, wb_i=wb_i: e.dma_start(
                out=WBUF[:, wb_i * 2048:wb_i * 2048 + 2048],
                in_=w_out_t[j].rearrange("p dc q -> p (dc q)")),
                inc=("s_w0" if wb_i == 0 else "s_w1", 16))
            bn = "mmA" if wb_i == 0 else "mmB"
            bd.dep(PE, t_wd)
            bd.dep(PE, bank_free[bn])
            t_mj = None
            for ch in range(DC):
                if ch % 4 == 0:
                    bd.dep(PE, y_toks[min(ch + 3, DC - 1)])
                wv = WBUF[:, wb_i * 2048 + ch * 128:wb_i * 2048 + (ch + 1) * 128]
                t_mj = pe_mm(ps[bn][:, 0:512], wv, YB(ch), start=(ch == 0), stop=(ch == DC - 1),
                             inc=("s_pe", 1) if ch == DC - 1 else None)
            wring_free[wb_i] = t_mj
            bd.dep(V, t_mj)
            bd.dep(V, free_tok["scr_wo"])
            scr = SCR[:, 0:512]
            xs = Xc(j)[:, pc0:pc0 + PW]
            t_e1 = bd.run(V, lambda e, bn=bn, scr=scr, xs=xs: e.scalar_tensor_tensor(
                scr, ps[bn][:, 0:512], 0.0, xs, AL.bypass, AL.add), inc=("s_dv", 1))
            bank_free[bn] = t_e1
            bd.dep(ACT, t_e1)
            t_e2 = bd.run(ACT, lambda e, scr=scr, xs=xs: e.copy(xs, scr), inc=("s_ac", 1))
            free_tok["scr_wo"] = t_e2
            xup_toks.append(t_e2)
            x_ready[j].append(t_e2)

    emit_ffn(0)
    emit_attn_pair(0)
    emit_attn_pair(1)
    emit_ffn(1)

    for dc in range(DC):
        bd.deps(SP, x_ready[dc])
        bd.run(SP, lambda e, dc=dc: e.dma_start(out=out[dc], in_=Xc(dc)), inc=("s_st", 16))
    bd.dep(SP, ("s_st", bd.cnt["s_st"]))

    with ExitStack() as stack:
        handles = {}
        for name in bd.sem_names:
            handles[name] = stack.enter_context(nc.semaphore(name))
        block = stack.enter_context(nc.Block())

        @block.tensor
        def _(t):
            bd.replay("pe", t, handles)

        @block.scalar
        def _(s):
            bd.replay("act", s, handles)

        @block.vector
        def _(v):
            bd.replay("dve", v, handles)

        @block.gpsimd
        def _(g):
            bd.replay("pool", g, handles)

        @block.sync
        def _(sy):
            bd.replay("sp", sy, handles)

    return nc


def kernel(L0_attn_x, L0_attn_w_in, L0_attn_w_out, L0_ffn_switch_w, L0_ffn_switch_b,
           L0_ffn_w1, L0_ffn_b1, L0_ffn_w2, L0_ffn_b2, L0_ffn_w3, L0_ffn_b3):
    x = np.ascontiguousarray(np.asarray(L0_attn_x, dtype=np.float32))
    w_in = np.ascontiguousarray(np.asarray(L0_attn_w_in, dtype=np.float32))
    w_out = np.ascontiguousarray(np.asarray(L0_attn_w_out, dtype=np.float32))
    sw = np.ascontiguousarray(np.asarray(L0_ffn_switch_w, dtype=np.float32))
    swb = np.ascontiguousarray(np.asarray(L0_ffn_switch_b, dtype=np.float32))
    w1 = np.ascontiguousarray(np.asarray(L0_ffn_w1, dtype=np.float32))
    b1 = np.ascontiguousarray(np.asarray(L0_ffn_b1, dtype=np.float32))
    w2 = np.ascontiguousarray(np.asarray(L0_ffn_w2, dtype=np.float32))
    b2 = np.ascontiguousarray(np.asarray(L0_ffn_b2, dtype=np.float32))
    w3 = np.ascontiguousarray(np.asarray(L0_ffn_w3, dtype=np.float32))
    b3 = np.ascontiguousarray(np.asarray(L0_ffn_b3, dtype=np.float32))

    wt = w_in.reshape(5, H, 128, DC, 128)
    w_in_t = np.ascontiguousarray(wt.transpose(1, 0, 4, 3, 2))
    wo = w_out.reshape(16, 128, DC, 128)
    w_out_t = np.ascontiguousarray(wo.transpose(0, 3, 2, 1))
    w3r = w3.reshape(16, 128, DC, 128)
    w3_t = np.ascontiguousarray(w3r.transpose(0, 3, 2, 1))
    sw3 = sw.reshape(8, 2048, 2048)
    w1T = np.ascontiguousarray(w1.transpose(0, 2, 1))
    w2T = np.ascontiguousarray(w2.transpose(0, 2, 1))
    w12T = np.ascontiguousarray(np.concatenate([w1T, w2T], axis=0).transpose(1, 0, 2))
    b1T = np.ascontiguousarray(b1.T)
    b2T = np.ascontiguousarray(b2.T)
    b3row = np.ascontiguousarray(b3[None, :])
    swb4 = np.ascontiguousarray(np.broadcast_to(swb[None, :], (4, 8)))

    nc = build_nc()

    in_maps = []
    for c in range(NCORES):
        sl = slice(256 * c, 256 * c + 256)
        x_fm = np.ascontiguousarray(
            x[:, sl, :].transpose(2, 0, 1).reshape(DC, 128, T))
        sw_c = np.ascontiguousarray(
            sw3[:, sl, :].transpose(0, 2, 1).reshape(8, DC, 128, TB).transpose(0, 2, 1, 3))
        oh = np.zeros((128, 8), np.float32)
        oh[:, c] = 1.0
        in_maps.append({
            "x_fm": x_fm, "w_in_t": w_in_t, "w_out_t": w_out_t, "w3_t": w3_t,
            "sw_t": sw_c, "w12T": w12T, "b1T": b1T, "b2T": b2T,
            "b3row": b3row, "swb": swb4, "onehot": oh,
        })

    res = run_bass_kernel_spmd(nc, in_maps, core_ids=list(range(NCORES)))

    outp = np.empty((4, 2048, 2048), np.float32)
    for c in range(NCORES):
        r = np.asarray(res.results[c]["out"])
        r = r.reshape(2048, 4, 256).transpose(1, 2, 0)
        outp[:, 256 * c:256 * c + 256, :] = r
    return outp


# version 19
# speedup vs baseline: 1.0789x; 1.0190x over previous
import numpy as np
from collections import defaultdict
from contextlib import ExitStack

import concourse.bass as bass
import concourse.mybir as mybir
from concourse.bass_utils import run_bass_kernel_spmd

F32 = mybir.dt.float32
BF16 = mybir.dt.bfloat16
AL = mybir.AluOpType
AF = mybir.ActivationFunctionType
AX = mybir.AxisListType

D = 2048
DC = 16
T = 1024
TB = 256
B = 4
H = 16
PW = 512
CH = 64
NCH = 4
LN_EPS = 1e-5
RMS_EPS = 1.1920928955078125e-07
NCORES = 8

ENGINES = ("pe", "act", "dve", "pool", "sp")


class Build:

    def __init__(self, nc):
        self.nc = nc
        self.streams = {e: [] for e in ENGINES}
        self.cnt = defaultdict(int)
        self.lastwait = {e: defaultdict(int) for e in ENGINES}
        self.sem_names = []

    def sem(self, name):
        if name not in self.sem_names:
            self.sem_names.append(name)
        return name

    def dep(self, eng, tok):
        if tok is None:
            return
        s, v = tok
        if v <= self.lastwait[eng][s]:
            return
        self.lastwait[eng][s] = v
        self.streams[eng].append(("wait", s, v))

    def deps(self, eng, toks):
        best = {}
        for t in toks:
            if t is None:
                continue
            s, v = t
            if v > best.get(s, -1):
                best[s] = v
        for s, v in best.items():
            self.dep(eng, (s, v))

    SERIALIZED = {"dve": "s_dvs", "act": "s_acs"}

    def run(self, eng, fn, inc=None, base=None):
        tok = None
        if eng in self.SERIALIZED:
            s = self.SERIALIZED[eng]
            self.sem(s)
            idx = self.cnt[s] + 1
            waitv = (idx - 1) if base is None else min(base, idx - 1)
            if waitv > 0:
                self.dep(eng, (s, waitv))
            self.cnt[s] = idx
            if inc is not None:
                tok = (s, idx)
            self.streams[eng].append(("op", fn, (s, 1)))
            return tok
        if inc is not None:
            s, amt = inc
            self.sem(s)
            self.cnt[s] += amt
            tok = (s, self.cnt[s])
        self.streams[eng].append(("op", fn, inc))
        return tok

    def vidx(self, eng):
        return self.cnt[self.SERIALIZED[eng]]

    def replay(self, eng, obj, handles):
        for item in self.streams[eng]:
            if item[0] == "wait":
                _, s, v = item
                obj.wait_ge(handles[s], v)
            else:
                _, fn, inc = item
                inst = fn(obj)
                if inc is not None:
                    s, amt = inc
                    inst.then_inc(handles[s], amt)


def build_nc():
    nc = bass.Bass("TRN2", target_bir_lowering=False, debug=False)

    def dinp(name, shape, dt=F32):
        return nc.dram_tensor(name, list(shape), dt, kind="ExternalInput").ap()

    x_fm = dinp("x_fm", [DC, 128, T])
    w_in_t = dinp("w_in_t", [H, 5, 128, DC, 128])
    w_out_t = dinp("w_out_t", [16, 128, DC, 128])
    w3_t = dinp("w3_t", [16, 128, DC, 128])
    sw_t = dinp("sw_t", [8, 128, DC, TB])
    w12T_in = dinp("w12T", [128, 16, 128])
    b1T_in = dinp("b1T", [128, 8])
    b2T_in = dinp("b2T", [128, 8])
    b3row_in = dinp("b3row", [1, D])
    swb_in = dinp("swb", [4, 8])
    onehot_in = dinp("onehot", [128, 8])

    out = nc.dram_tensor("out", [DC, 128, T], F32, kind="ExternalOutput").ap()

    rt_in = [nc.dram_tensor(f"rt_in{i}", [32], F32).ap() for i in range(2)]
    rt_out = [nc.dram_tensor(f"rt_out{i}", [32], F32).ap() for i in range(2)]
    sc_in = [nc.dram_tensor(f"sc_in{i}", [128, 64], F32).ap() for i in range(2)]
    sc_out = [nc.dram_tensor(f"sc_out{i}", [NCORES * 128, 64], F32).ap() for i in range(2)]
    selb = [nc.dram_tensor(f"selb{i}", [4, 40], F32).ap() for i in range(2)]

    def salloc(name, shape, dt):
        return nc.alloc_sbuf_tensor(name, list(shape), dt)

    X = salloc("X", [128, DC * T], F32)
    A_BF = salloc("A_BF", [128, 16384], BF16)
    B_BF = salloc("B_BF", [128, 8192], BF16)
    C_F = salloc("C_F", [128, 8192], F32)
    D_BF = salloc("D_BF", [128, 16384], BF16)
    WBUF = salloc("WBUF", [128, 2 * 2048], BF16)
    SCR = salloc("SCR", [128, 4 * 512], F32)
    SCB = salloc("SCB", [128, 3 * 512], BF16)
    ONES_F = salloc("ONES_F", [128, 128], F32)
    ONES_B = salloc("ONES_B", [128, 512], BF16)
    CONSTS = salloc("CONSTS", [128, 6], F32)
    S_LOC = salloc("S_LOC", [128, 64], F32)
    S_ALL = salloc("S_ALL", [128, NCORES * 64], F32)
    PREF = salloc("PREF", [128, 9 * 32], F32)
    SINIT = salloc("SINIT", [128, 32], F32)
    PFE = salloc("PFE", [128, 128], F32)
    APF = salloc("APF", [128, 128], F32)
    Z0 = salloc("Z0", [128, 128], F32)
    SREAL = salloc("SREAL", [128, 128], F32)
    CSC = salloc("CSC", [128, 64], F32)
    LGS = salloc("LGS", [1, 32], F32)
    LG = salloc("LG", [4, 64], F32)
    MSK = salloc("MSK", [4, 40], F32)
    GROW = salloc("GROW", [1, 160], F32)
    GSEL = salloc("GSEL", [128, 160], F32)
    B1T = salloc("B1T", [128, 8], F32)
    B2T = salloc("B2T", [128, 8], F32)
    B1S = salloc("B1S", [128, 8], F32)
    B2G = salloc("B2G", [128, 4], F32)
    SWB = salloc("SWB", [4, 8], F32)
    OH = salloc("OH", [128, 8], F32)
    SELS = salloc("SELS", [128, 2 * 128], BF16)
    SELF = salloc("SELF", [128, 2], F32)

    ps = {}
    def palloc(name, shape):
        ps[name] = nc.alloc_psum_tensor(name, list(shape), F32)
        return ps[name]

    palloc("mmA", [128, 512])
    palloc("mmB", [128, 512])
    palloc("bcA", [128, 512])
    palloc("bcB", [128, 512])
    palloc("st1", [128, 1024])
    palloc("st2", [128, 1024])

    bd = Build(nc)
    V, P, ACT, PE, SP = "dve", "pool", "act", "pe", "sp"
    for s in ("s_xin", "s_xi2", "s_sm", "s_exw", "s_b3", "s_io", "s_w0", "s_w1",
              "s_w2", "s_w3", "s_sw0", "s_sw1", "s_pe", "s_dv", "s_ac",
              "s_cc", "s_st", "s_dvs", "s_acs"):
        bd.sem(s)

    bank_free = {k: None for k in ps}
    wring_free = [None, None, None, None]
    BN4 = ["mmA", "mmB", "bcA", "bcB"]
    free_tok = defaultdict(lambda: None)

    def Xc(dc):
        return X[:, dc * T:(dc + 1) * T]

    def pe_mm(bank_ap, lhsT, rhs, start, stop, inc=None):
        return bd.run(PE, lambda e, bank_ap=bank_ap, lhsT=lhsT, rhs=rhs, start=start,
                      stop=stop: e.matmul(bank_ap, lhsT, rhs, start=start, stop=stop),
                      inc=inc)

    tx_toks = []
    for dc in range(DC):
        sem = "s_xin" if dc < 8 else "s_xi2"
        tx_toks.append(bd.run(SP, lambda e, dc=dc: e.dma_start(out=Xc(dc), in_=x_fm[dc]),
                              inc=(sem, 16)))
    t_smalls = None
    for (dst, src) in ((B1T, b1T_in), (B2T, b2T_in), (SWB, swb_in), (OH, onehot_in)):
        t_smalls = bd.run(SP, lambda e, d=dst, s=src: e.dma_start(out=d[:], in_=s),
                          inc=("s_sm", 16))
    bd.run(V, lambda e: e.memset(ONES_F[:], 1.0))
    bd.run(V, lambda e: e.memset(CONSTS[:, 0:1], LN_EPS))
    bd.run(V, lambda e: e.memset(CONSTS[:, 1:2], 2.0))
    bd.run(V, lambda e: e.memset(CONSTS[:, 2:3], RMS_EPS))
    bd.run(V, lambda e: e.memset(CONSTS[:, 3:4], 1.0 / 128.0))
    bd.run(V, lambda e: e.memset(CONSTS[:, 4:5], -0.5))
    bd.run(V, lambda e: e.memset(CONSTS[:, 5:6], -1.0))
    t_ones = bd.run(V, lambda e: e.memset(ONES_B[:], 1.0), inc=("s_dv", 1))
    bd.dep(ACT, t_ones)
    C_EPS = CONSTS[:, 0:1]
    C_TWO = CONSTS[:, 1:2]
    C_REPS = CONSTS[:, 2:3]
    C_I128 = CONSTS[:, 3:4]
    C_NH = CONSTS[:, 4:5]
    C_N1 = CONSTS[:, 5:6]

    xup_toks = list(tx_toks)
    x_ready = {dc: [("s_xin", 128) if dc < 8 else ("s_xi2", 128)] for dc in range(DC)}

    def emit_ln(xln_dst, col0, width, xbf_scr, x2bf_scr, ascr, nscr, apply_toks=None):
        nh = (width + 511) // 512
        bd.dep(PE, t_ones)
        bd.dep(PE, bank_free["st1"])
        bd.dep(PE, bank_free["st2"])
        pe_tok = [None, None]
        t_stats = None
        for dc in range(DC):
            xs = Xc(dc)[:, col0:col0 + width]
            bd.dep(V, pe_tok[dc % 2])
            bd.deps(V, x_ready[dc])
            t_c = bd.run(V, lambda e, dc=dc, xs=xs: e.tensor_copy(xbf_scr(dc), xs),
                         inc=("s_dv", 1))
            t_s = bd.run(V, lambda e, dc=dc: e.tensor_tensor(
                x2bf_scr(dc), xbf_scr(dc), xbf_scr(dc), AL.mult),
                base=t_c[1], inc=("s_dv", 1))
            bd.dep(PE, t_c)
            bd.dep(PE, t_s)
            for h in range(nh):
                w = min(512, width - h * 512)
                pe_mm(ps["st1"][:, h * 512:h * 512 + w], ONES_B[:, 0:128],
                      xbf_scr(dc)[:, h * 512:h * 512 + w], start=(dc == 0), stop=(dc == DC - 1))
                t_stats = pe_mm(ps["st2"][:, h * 512:h * 512 + w], ONES_B[:, 0:128],
                                x2bf_scr(dc)[:, h * 512:h * 512 + w],
                                start=(dc == 0), stop=(dc == DC - 1), inc=("s_pe", 1))
            pe_tok[dc % 2] = t_stats
        t_last = None
        t_v = None
        for h in range(nh):
            hs = h * 512
            w = min(512, width - hs)
            SM = SCR[0:1, 0:w]
            SV = SCR[0:1, 512:512 + w]
            S3 = SCR[0:1, 1024:1024 + w]
            S4 = SCR[0:1, 1536:1536 + w]
            bd.dep(V, t_stats)
            if h > 0:
                bd.dep(V, t_bc)
            bd.run(V, lambda e, hs=hs, w=w, SM=SM: e.tensor_scalar(
                SM, ps["st1"][0:1, hs:hs + w], 1.0 / D, None, AL.mult))
            t_v = bd.run(V, lambda e, hs=hs, w=w, SV=SV: e.tensor_scalar(
                SV, ps["st2"][0:1, hs:hs + w], 1.0 / D, None, AL.mult),
                inc=("s_dv", 1))
            bd.run(V, lambda e, SM=SM, S3=S3: e.tensor_tensor(S3, SM, SM, AL.mult))
            t_var = bd.run(V, lambda e, SV=SV, S3=S3, S4=S4: e.tensor_tensor(
                S4, SV, S3, AL.subtract), inc=("s_dv", 1))
            bd.dep(ACT, t_var)
            bd.run(ACT, lambda e, S3=S3, S4=S4: e.activation(
                S3, S4, AF.Ln, bias=CONSTS[0:1, 0:1]), inc=("s_ac", 1))
            t_sd = bd.run(ACT, lambda e, SV=SV, S3=S3: e.activation(
                SV, S3, AF.Exp, scale=CONSTS[0:1, 4:5]), inc=("s_ac", 1))
            bd.dep(V, t_sd)
            bd.run(V, lambda e, SM=SM, SV=SV, S3=S3: e.tensor_tensor(
                S3, SM, SV, AL.mult))
            t_rows = bd.run(V, lambda e, S3=S3, S4=S4: e.tensor_scalar(
                S4, S3, -1.0, None, AL.mult), inc=("s_dv", 1))
            bd.dep(PE, bank_free["bcA"])
            bd.dep(PE, bank_free["bcB"])
            bd.dep(PE, t_rows)
            pe_mm(ps["bcA"][:, 0:w], ONES_F[0:1, :], SV, True, True)
            t_bc = pe_mm(ps["bcB"][:, 0:w], ONES_F[0:1, :], S4,
                         True, True, inc=("s_pe", 1))
            bd.dep(V, t_bc)
            mt = {}
            at = {}
            g0 = bd.vidx("dve")
            for dc in range(DC):
                xs = Xc(dc)[:, col0 + hs:col0 + hs + w]
                scr = ascr(dc % nscr)[:, 0:w]
                bd.deps(V, x_ready[dc])
                mt[dc] = bd.run(V, lambda e, xs=xs, scr=scr, w=w: e.tensor_tensor(
                    scr, xs, ps["bcA"][:, 0:w], AL.mult),
                    base=(at[dc - nscr][1] if dc - nscr in at else g0),
                    inc=("s_dv", 1))
                if dc >= 1:
                    d2 = dc - 1
                    dst = xln_dst(d2)[:, hs:hs + w]
                    scr2 = ascr(d2 % nscr)[:, 0:w]
                    at[d2] = bd.run(V, lambda e, dst=dst, scr2=scr2, w=w: e.tensor_tensor(
                        dst, scr2, ps["bcB"][:, 0:w], AL.add), base=mt[d2][1],
                        inc=("s_dv", 1))
                    if apply_toks is not None:
                        apply_toks.setdefault(d2, []).append(at[d2])
            d2 = DC - 1
            dst = xln_dst(d2)[:, hs:hs + w]
            scr2 = ascr(d2 % nscr)[:, 0:w]
            t_last = bd.run(V, lambda e, dst=dst, scr2=scr2, w=w: e.tensor_tensor(
                dst, scr2, ps["bcB"][:, 0:w], AL.add), base=mt[d2][1],
                inc=("s_dv", 1))
            if apply_toks is not None:
                apply_toks.setdefault(d2, []).append(t_last)
            bank_free["bcA"] = t_last
            bank_free["bcB"] = t_last
        bank_free["st1"] = t_v
        bank_free["st2"] = t_v
        return t_last

    sw_free = [None, None]

    def emit_ffn(call):
        XLN = lambda dc: A_BF[:, dc * 1024:(dc + 1) * 1024]
        SWbuf = lambda i: D_BF[:, (i % 2) * 4096:(i % 2) * 4096 + 4096]
        Pbuf = lambda i: D_BF[:, 12288 + (i % 4) * 1024: 12288 + (i % 4) * 1024 + 1024]
        LNxb = lambda dc: D_BF[:, 8192 + (dc % 2) * 1024: 8192 + (dc % 2) * 1024 + 1024]
        LNx2 = lambda dc: D_BF[:, 10240 + (dc % 2) * 1024: 10240 + (dc % 2) * 1024 + 1024]
        EXW1 = lambda e_: B_BF[:, e_ * 128:(e_ + 1) * 128]
        EXW2 = lambda e_: B_BF[:, 1024 + e_ * 128: 1024 + (e_ + 1) * 128]
        SELW1 = lambda bk: B_BF[:, 2048 + bk * 128: 2048 + (bk + 1) * 128]
        SELW2 = lambda bk: B_BF[:, 3072 + bk * 128: 3072 + (bk + 1) * 128]
        HB = lambda i: B_BF[:, 4096 + (i % 2) * 256: 4096 + (i % 2) * 256 + 256]
        C_bf = C_F.bitcast(BF16)
        Y1 = lambda dc: C_bf[:, dc * 1024:(dc + 1) * 1024]

        rg_pre = bd.vidx("dve")
        bd.deps(V, [wring_free[2], wring_free[3]])
        D_F = D_BF.bitcast(F32)
        ln_toks = {}
        t_ln = emit_ln(XLN, 0, T, LNxb, LNx2,
                       lambda i: D_F[:, 4096 + i * 512: 4096 + (i + 1) * 512], 4,
                       apply_toks=ln_toks)

        bd.dep(P, free_tok["b_arena"])
        t_exw2 = bd.run(P, lambda e: e.dma_start(
            out=B_BF[:, 0:2048],
        in_=w12T_in.rearrange("p e q -> p (e q)")),
            inc=("s_exw", 16))

        sw_dma = [None, None]
        lg_toks = []
        pbuf_free = [None, None, None, None]
        for e_ in range(8):
            if e_ < 2:
                bd.dep(P, sw_free[e_ % 2])
                bd.dep(P, free_tok["d_arena"] if e_ == 0 else free_tok["d_arena2"])
                sw_dma[e_ % 2] = bd.run(P, lambda e, e_=e_: e.dma_start(
                    out=SWbuf(e_),
                    in_=sw_t[e_].rearrange("p dc t -> p (dc t)")),
                    inc=("s_sw0" if e_ % 2 == 0 else "s_sw1", 16))
            stats = "st1" if e_ % 2 == 0 else "st2"
            bd.dep(V, sw_dma[e_ % 2])
            t_mm = None
            for dc in range(DC):
                swv = SWbuf(e_)[:, dc * 256:(dc + 1) * 256]
                sw_b = swv.rearrange("p (o t) -> p o t", o=1).broadcast_to([128, 4, 256])
                pb = Pbuf(dc)
                bd.dep(V, pbuf_free[dc % 4])
                bd.deps(V, ln_toks.get(dc, []))
                t_p = bd.run(V, lambda e, dc=dc, sw_b=sw_b, pb=pb, e_=e_: e.tensor_tensor(
                    pb.rearrange("p (b t) -> p b t", b=B),
                    XLN(dc).rearrange("p (b t) -> p b t", b=B), sw_b, AL.mult),
                    base=rg_pre, inc=("s_dv", 1))
                bd.dep(PE, bank_free[stats])
                bd.dep(PE, t_p)
                for hf in range(2):
                    t_mm = pe_mm(ps[stats][:, hf * 512:(hf + 1) * 512], ONES_B[:, 0:128],
                                 pb[:, hf * 512:(hf + 1) * 512],
                                 start=(dc == 0), stop=(dc == DC - 1),
                                 inc=("s_pe", 1) if hf == 1 else None)
                pbuf_free[dc % 4] = t_mm
                t_p_last = t_p
            sw_free[e_ % 2] = t_p_last
            if e_ + 2 < 8:
                bd.dep(P, sw_free[e_ % 2])
                sw_dma[e_ % 2] = bd.run(P, lambda e, e_=e_: e.dma_start(
                    out=SWbuf(e_ + 2),
                    in_=sw_t[e_ + 2].rearrange("p dc t -> p (dc t)")),
                    inc=("s_sw0" if e_ % 2 == 0 else "s_sw1", 16))
            bd.dep(V, t_mm)
            bd.dep(V, free_tok["LGS"])
            lt = None
            for b_ in range(B):
                lt = bd.run(V, lambda e, e_=e_, b_=b_, stats=stats: e.tensor_reduce(
                    LGS[:, b_ * 8 + e_: b_ * 8 + e_ + 1],
                    ps[stats][0:1, b_ * 256:(b_ + 1) * 256], AX.X, AL.add),
                    inc=("s_dv", 1) if b_ == B - 1 else None)
            bank_free[stats] = lt
            lg_toks.append(lt)

        B3R = D_BF[0:1, 0:2048]
        bd.dep(P, sw_free[0])
        t_b3 = bd.run(P, lambda e: e.dma_start(out=B3R, in_=b3row_in), inc=("s_b3", 16))

        bd.deps(SP, lg_toks)
        t_s1 = bd.run(SP, lambda e: e.dma_start(
            out=rt_in[call].rearrange("(o c) -> o c", o=1), in_=LGS[0:1, :]), inc=("s_io", 16))
        free_tok["LGS"] = t_s1
        bd.dep(P, t_s1)
        t_cc = bd.run(P, lambda e: e.collective_compute(
            "AllReduce", AL.add, replica_groups=[list(range(NCORES))],
            ins=[rt_in[call].opt()], outs=[rt_out[call].opt()]), inc=("s_cc", 1))
        bd.dep(SP, t_cc)
        t_lg = bd.run(SP, lambda e: e.dma_start(
            out=LG[:, 0:8], in_=rt_out[call].rearrange("(b e) -> b e", b=4)),
            inc=("s_io", 16))

        lg0 = LG[:, 0:8]; ex_ = LG[:, 8:16]; pr = LG[:, 16:24]; pr2 = LG[:, 24:32]
        cs = LG[:, 32:40]; tmp8 = LG[:, 40:48]; mx = LG[:, 48:49]; nmx = LG[:, 49:50]
        sm = LG[:, 50:51]; rs = LG[:, 51:52]
        msk1 = MSK[:, 0:8]; msk2 = MSK[:, 8:16]; gm1 = MSK[:, 16:24]
        gm2 = MSK[:, 24:32]; gmc = MSK[:, 32:40]
        bd.dep(V, t_lg)
        bd.dep(V, t_smalls)
        bd.dep(V, free_tok["MSK"])
        bd.run(V, lambda e: e.tensor_tensor(lg0, lg0, SWB[:], AL.add))
        bd.run(V, lambda e: e.reduce_max(mx, lg0, AX.X))
        t_nm = bd.run(V, lambda e: e.tensor_scalar(nmx, mx, -1.0, None, AL.mult), inc=("s_dv", 1))
        bd.dep(ACT, t_nm)
        t_ex = bd.run(ACT, lambda e: e.activation(ex_, lg0, AF.Exp, bias=nmx[:, 0:1]),
                      inc=("s_ac", 1))
        bd.dep(V, t_ex)
        bd.run(V, lambda e: e.reduce_sum(sm, ex_, AX.X))
        bd.run(V, lambda e: e.reciprocal(rs, sm))
        bd.run(V, lambda e: e.tensor_scalar(pr, ex_, rs[:, 0:1], None, AL.mult))
        bd.run(V, lambda e: e.reduce_max(mx, pr, AX.X))
        bd.run(V, lambda e: e.tensor_scalar(tmp8, pr, mx[:, 0:1], None, AL.is_ge))
        bd.run(V, lambda e: e.tensor_tensor_scan(cs, ONES_F[0:4, 0:8], tmp8, 0.0, AL.mult, AL.add))
        bd.run(V, lambda e: e.tensor_scalar(cs, cs, 1.0, None, AL.is_equal))
        bd.run(V, lambda e: e.tensor_tensor(msk1, tmp8, cs, AL.mult))
        bd.run(V, lambda e: e.tensor_tensor(gm1, pr, msk1, AL.mult))
        bd.run(V, lambda e: e.tensor_scalar(tmp8, msk1, -1.0, 1.0, AL.mult, AL.add))
        bd.run(V, lambda e: e.tensor_tensor(pr2, pr, tmp8, AL.mult))
        bd.run(V, lambda e: e.reduce_max(mx, pr2, AX.X))
        bd.run(V, lambda e: e.tensor_scalar(tmp8, pr2, mx[:, 0:1], None, AL.is_ge))
        bd.run(V, lambda e: e.tensor_tensor_scan(cs, ONES_F[0:4, 0:8], tmp8, 0.0, AL.mult, AL.add))
        bd.run(V, lambda e: e.tensor_scalar(cs, cs, 1.0, None, AL.is_equal))
        bd.run(V, lambda e: e.tensor_tensor(msk2, tmp8, cs, AL.mult))
        bd.run(V, lambda e: e.tensor_tensor(gm2, pr2, msk2, AL.mult))
        t_msk = bd.run(V, lambda e: e.tensor_tensor(gmc, gm1, gm2, AL.add), inc=("s_dv", 1))

        bd.dep(SP, t_msk)
        t_sb = bd.run(SP, lambda e: e.dma_start(out=selb[call], in_=MSK[:, 0:40]),
                      inc=("s_io", 16))
        free_tok["MSK"] = t_sb
        bd.dep(SP, t_sb)
        t_sr = bd.run(SP, lambda e: e.dma_start(
            out=GROW[:], in_=selb[call].rearrange("b c -> (b c)")[None, :]), inc=("s_io", 16))
        bd.dep(PE, t_sr)
        bd.dep(PE, bank_free["bcA"])
        t_gb = pe_mm(ps["bcA"][:, 0:160], ONES_F[0:1, :], GROW[:], True, True, inc=("s_pe", 1))
        bd.dep(V, t_gb)
        t_gs = bd.run(V, lambda e: e.tensor_copy(GSEL[:, 0:160], ps["bcA"][:, 0:160]),
                      inc=("s_dv", 1))
        bank_free["bcA"] = t_gs

        bd.dep(V, t_exw2)
        sel_last = None
        chains = []
        for b_ in range(B):
            for k_ in range(2):
                bk = b_ * 2 + k_
                moff = b_ * 40 + (0 if k_ == 0 else 8)
                goff = b_ * 40 + (16 if k_ == 0 else 24)
                chains.append((SELW1(bk), EXW1, moff))
                chains.append((SELW2(bk), EXW2, goff))
        for grp in range(0, len(chains), 4):
            batch = chains[grp:grp + 4]
            prev = [None] * len(batch)
            for e_ in range(8):
                for lane, (dst, src, off) in enumerate(batch):
                    pp = [B_BF[:, 6144 + lane * 256:6144 + lane * 256 + 128],
                          B_BF[:, 6144 + lane * 256 + 128:6144 + lane * 256 + 256]]
                    o = dst if e_ == 7 else pp[e_ % 2]
                    if e_ == 0:
                        prev[lane] = bd.run(V, lambda e, o=o, src=src, off=off:
                            e.tensor_scalar(o, src(0), GSEL[:, off:off + 1], None,
                            AL.mult), base=max(free_tok["sel_lane"] or 0, t_gs[1]),
                            inc=("s_dv", 1))
                    else:
                        prev[lane] = bd.run(V, lambda e, o=o, src=src, e_=e_, off=off,
                            i=pp[(e_ + 1) % 2]: e.scalar_tensor_tensor(
                            o, src(e_), GSEL[:, off + e_: off + e_ + 1], i,
                            AL.mult, AL.add), base=prev[lane][1], inc=("s_dv", 1))
            sel_last = prev[-1]
            free_tok["sel_lane"] = max(t[1] for t in prev)
        bchains = []
        for b_ in range(B):
            for k_ in range(2):
                bk = b_ * 2 + k_
                moff = b_ * 40 + (0 if k_ == 0 else 8)
                bchains.append((B1S[:, bk:bk + 1], B1T, moff))
        for b_ in range(B):
            bchains.append((B2G[:, b_:b_ + 1], B2T, b_ * 40 + 32))
        for grp in range(0, len(bchains), 4):
            batch = bchains[grp:grp + 4]
            prev = [None] * len(batch)
            for e_ in range(8):
                for lane, (dst, tsrc, off) in enumerate(batch):
                    ppf = [PREF[:, 272 + 2 * lane:273 + 2 * lane],
                           PREF[:, 273 + 2 * lane:274 + 2 * lane]]
                    o = dst if e_ == 7 else ppf[e_ % 2]
                    if e_ == 0:
                        prev[lane] = bd.run(V, lambda e, o=o, tsrc=tsrc, off=off:
                            e.tensor_scalar(o, tsrc[:, 0:1], GSEL[:, off:off + 1], None,
                            AL.mult), base=max(free_tok["selb_lane"] or 0, t_gs[1]),
                            inc=("s_dv", 1))
                    else:
                        prev[lane] = bd.run(V, lambda e, o=o, tsrc=tsrc, e_=e_, off=off,
                            i=ppf[(e_ + 1) % 2]: e.scalar_tensor_tensor(
                            o, tsrc[:, e_:e_ + 1], GSEL[:, off + e_: off + e_ + 1], i,
                            AL.mult, AL.add), base=prev[lane][1], inc=("s_dv", 1))
            sel_last = prev[-1]
            free_tok["selb_lane"] = max(t[1] for t in prev)
        t_sel = sel_last

        y1_toks = []
        obanks = ["mmA", "mmB", "bcA", "bcB"]
        hbanks = ["st1", "st2"]
        HB8 = lambda s: B_BF[:, 4096 + s * 256: 4096 + (s + 1) * 256]
        hb_i = 0
        bd.dep(PE, t_sel)
        bd.dep(PE, t_ln)
        for b_ in range(B):
            for g in range(4):
                hcs = [g * 4 + i for i in range(4)]
                gel = {}
                mm1t = {}
                for k_ in range(2):
                    bk = b_ * 2 + k_
                    for i, hc in enumerate(hcs):
                        s = k_ * 4 + i
                        hname = hbanks[hb_i % 2]
                        hb = ps[hname]
                        bd.dep(PE, bank_free[hname])
                        t_mm1 = pe_mm(hb[:, 0:256], SELW1(bk),
                                      XLN(hc)[:, b_ * 256:(b_ + 1) * 256], True, True,
                                      inc=("s_pe", 1))
                        bd.dep(ACT, t_mm1)
                        t_h = bd.run(ACT, lambda e, hb=hb, bk=bk, s=s: e.activation(
                            HB8(s), hb[:, 0:256], AF.Gelu, bias=B1S[:, bk:bk + 1]),
                            inc=("s_ac", 1))
                        bank_free[hname] = t_h
                        gel[s] = t_h
                        hb_i += 1
                for k_ in range(2):
                    bk = b_ * 2 + k_
                    for i, hc in enumerate(hcs):
                        s = k_ * 4 + i
                        ob = ps[obanks[i]]
                        bd.dep(PE, gel[s])
                        if k_ == 0:
                            bd.dep(PE, bank_free[obanks[i]])
                        t_mm2 = pe_mm(ob[:, 0:256], SELW2(bk), HB8(s),
                                      start=(k_ == 0), stop=(k_ == 1),
                                      inc=("s_pe", 1))
                        if k_ == 1:
                            bd.dep(V, t_mm2)
                            bd.dep(V, t_sel)
                            t_y = bd.run(V, lambda e, ob=ob, hc=hc, b_=b_: e.scalar_tensor_tensor(
                                Y1(hc)[:, b_ * 256:(b_ + 1) * 256], ob[:, 0:256],
                                B2G[:, b_:b_ + 1],
                                XLN(hc)[:, b_ * 256:(b_ + 1) * 256], AL.add, AL.add),
                                inc=("s_dv", 1))
                            bank_free[obanks[i]] = t_y
                            y1_toks.append(t_y)
        free_tok["a_ext"] = t_mm2
        for j in range(16):
            wb_i = j % 2
            bd.dep(P, wring_free[wb_i])
            t_wd = bd.run(P, lambda e, j=j, wb_i=wb_i: e.dma_start(
                out=WBUF[:, wb_i * 2048:wb_i * 2048 + 2048],
                in_=w3_t[j].rearrange("p dc q -> p (dc q)")),
                inc=("s_w0" if wb_i == 0 else "s_w1", 16))
            bd.dep(PE, t_wd)
            bd.dep(PE, t_b3)
            bd.deps(PE, y1_toks)
            bnA = BN4[(j % 2) * 2]
            bnB = BN4[(j % 2) * 2 + 1]
            bd.dep(PE, bank_free[bnA])
            bd.dep(PE, bank_free[bnB])
            for dc in range(DC):
                wv = WBUF[:, wb_i * 2048 + dc * 128:wb_i * 2048 + (dc + 1) * 128]
                pe_mm(ps[bnA][:, 0:512], wv, Y1(dc)[:, 0:512], start=(dc == 0), stop=False)
                pe_mm(ps[bnB][:, 0:512], wv, Y1(dc)[:, 512:1024], start=(dc == 0), stop=False)
            t_mja = bd.run(PE, lambda e, bnA=bnA, j=j, B3R=B3R: e.matmul(
                ps[bnA][:, 0:512], B3R[:, j * 128:(j + 1) * 128],
                ONES_B[0:1, 0:512], start=False, stop=True), inc=("s_pe", 1))
            t_mjb = bd.run(PE, lambda e, bnB=bnB, j=j, B3R=B3R: e.matmul(
                ps[bnB][:, 0:512], B3R[:, j * 128:(j + 1) * 128],
                ONES_B[0:1, 0:512], start=False, stop=True), inc=("s_pe", 1))
            wring_free[wb_i] = t_mjb
            for hf, (bn, t_mj) in enumerate(((bnA, t_mja), (bnB, t_mjb))):
                bd.dep(V, t_mj)
                bd.dep(V, free_tok["scr_w3"])
                scr = SCR[:, 512:1024]
                xs = Xc(j)[:, hf * 512:hf * 512 + 512]
                t_e1 = bd.run(V, lambda e, bn=bn, scr=scr, xs=xs: e.scalar_tensor_tensor(
                    scr, ps[bn][:, 0:512], 0.5, xs, AL.mult, AL.add), inc=("s_dv", 1))
                bank_free[bn] = t_e1
                bd.dep(ACT, t_e1)
                t_e2 = bd.run(ACT, lambda e, scr=scr, xs=xs: e.copy(xs, scr), inc=("s_ac", 1))
                free_tok["scr_w3"] = t_e2
                xup_toks.append(t_e2)
                x_ready[j].append(t_e2)

    def emit_attn_pair(p):
        pc0 = p * PW
        A_F = A_BF.bitcast(F32)
        C_BF = C_F.bitcast(BF16)
        XLNP = lambda dc: A_BF[:, dc * 512:(dc + 1) * 512]
        YB = lambda hc: A_BF[:, hc * 512:(hc + 1) * 512]
        PREFIX = lambda hc: B_BF[:, hc * 512:(hc + 1) * 512]
        ACC = lambda hc: C_BF[:, hc * 512:(hc + 1) * 512]
        P1 = lambda hc: D_BF[:, hc * 512:(hc + 1) * 512]
        P2M = lambda hc: D_BF[:, 8192 + hc * 512: 8192 + (hc + 1) * 512]
        LNxb = lambda dc: D_BF[:, (dc % 2) * 512:(dc % 2) * 512 + 512]
        LNx2 = lambda dc: D_BF[:, 1024 + (dc % 2) * 512:1024 + (dc % 2) * 512 + 512]
        SSET = [
            dict(sig=SCR[:, 0:512], a=SCR[:, 512:1024], k=SCR[:, 1024:1536],
                 u=SCR[:, 1536:2048], pf=A_F[:, 6144:6656], x2=A_F[:, 6656:7168],
                 qb=SCB[:, 0:512], vb=SCB[:, 512:1024], q2b=SCB[:, 1024:1536],
                 stb="st1"),
            dict(sig=C_F[:, 4096:4608], a=C_F[:, 4608:5120], k=C_F[:, 5120:5632],
                 u=C_F[:, 5632:6144], pf=C_F[:, 6144:6656], x2=C_F[:, 6656:7168],
                 qb=C_BF[:, 14336:14848], vb=C_BF[:, 14848:15360],
                 q2b=C_BF[:, 15360:15872], stb="st2"),
        ]
        WR = [WBUF[:, 0:2048], WBUF[:, 2048:4096],
              A_BF[:, 8192:10240], A_BF[:, 10240:12288]]
        WSEM = ["s_w0", "s_w1", "s_w2", "s_w3"]

        D_F = D_BF.bitcast(F32)
        lnp_toks = {}
        t_lnp = emit_ln(XLNP, pc0, PW, LNxb, LNx2,
                        lambda i: D_F[:, i * 512:(i + 1) * 512], 2,
                        apply_toks=lnp_toks)

        t_bc_toks = []
        phead_toks = []
        prev_tacc = [None, None]
        prev_lnpf = [None, None]
        prev_p2m = [None, None]
        bd.dep(P, free_tok["a_ext"])
        bd.dep(V, free_tok["S_LOC"])
        for hc in range(H):
            par = hc % 2
            S = SSET[par]
            s_sig, s_a, s_k, s_u = S["sig"], S["a"], S["k"], S["u"]
            s_pf, s_x2 = S["pf"], S["x2"]
            QBF, VBF, Q2BF = S["qb"], S["vb"], S["q2b"]
            stb = S["stb"]
            evac = {}
            for sidx, jp in enumerate((3, 1, 2, 4, 0)):
                wb_i = (p * 80 + hc * 5 + sidx + 2) % 4
                bd.dep(P, wring_free[wb_i])
                t_wd = bd.run(P, lambda e, hc=hc, jp=jp, wb_i=wb_i: e.dma_start(
                    out=WR[wb_i],
                    in_=w_in_t[hc, jp].rearrange("p dc q -> p (dc q)")),
                    inc=(WSEM[wb_i], 16))
                bn = BN4[(hc * 5 + sidx) % 4]
                bd.dep(PE, t_wd)
                bd.dep(PE, bank_free[bn])
                t_mm = None
                for dc in range(DC):
                    bd.deps(PE, lnp_toks.get(dc, []))
                    wv = WR[wb_i][:, dc * 128:(dc + 1) * 128]
                    t_mm = pe_mm(ps[bn][:, 0:512], wv, XLNP(dc),
                                 start=(dc == 0), stop=(dc == DC - 1),
                                 inc=("s_pe", 1) if dc == DC - 1 else None)
                wring_free[wb_i] = t_mm
                pb = ps[bn][:, 0:512]
                if jp == 3:
                    bd.dep(ACT, t_mm)
                    bd.dep(ACT, prev_tacc[par])
                    t1 = bd.run(ACT, lambda e, pb=pb, s_sig=s_sig: e.activation(
                        s_sig, pb, AF.Sigmoid, bias=C_TWO), inc=("s_ac", 1))
                    bank_free[bn] = t1
                    bd.dep(V, t1)
                    evac["a"] = bd.run(V, lambda e, s_a=s_a, s_sig=s_sig: e.tensor_scalar(
                        s_a, s_sig, 0.6, 0.9995, AL.max, AL.min), inc=("s_dv", 1))
                elif jp == 1:
                    bd.dep(ACT, t_mm)
                    t1 = bd.run(ACT, lambda e, pb=pb, s_k=s_k: e.activation(
                        s_k, pb, AF.Tanh), inc=("s_ac", 1))
                    bank_free[bn] = t1
                    evac["k"] = t1
                elif jp == 2:
                    bd.dep(ACT, t_mm)
                    t1 = bd.run(ACT, lambda e, pb=pb, VBF=VBF: e.copy(VBF, pb),
                                inc=("s_ac", 1))
                    bank_free[bn] = t1
                    bd.dep(V, t1)
                    bd.dep(V, evac["k"])
                    evac["u"] = bd.run(V, lambda e, s_u=s_u, s_k=s_k, VBF=VBF:
                                       e.tensor_tensor(s_u, s_k, VBF, AL.mult),
                                       inc=("s_dv", 1))
                elif jp == 4:
                    bd.dep(ACT, t_mm)
                    bd.dep(ACT, evac["a"])
                    t1 = bd.run(ACT, lambda e, pb=pb, s_sig=s_sig: e.activation(
                        s_sig, pb, AF.Sigmoid), inc=("s_ac", 1))
                    bank_free[bn] = t1
                    evac["g"] = t1
                else:
                    bd.dep(ACT, t_mm)
                    bd.dep(ACT, prev_p2m[par])
                    t1 = bd.run(ACT, lambda e, pb=pb, QBF=QBF: e.copy(QBF, pb),
                                inc=("s_ac", 1))
                    t1b = bd.run(ACT, lambda e, pb=pb, Q2BF=Q2BF: e.activation(
                        Q2BF, pb, AF.Square), inc=("s_ac", 1))
                    bank_free[bn] = t1b
                    bd.dep(PE, t1b)
                    bd.dep(PE, bank_free[stb])
                    t_q2 = pe_mm(ps[stb][:, 0:512], ONES_B[:, 0:128], Q2BF, True, True,
                                 inc=("s_pe", 1))
                    bd.dep(ACT, t_q2)
                    t_r1 = bd.run(ACT, lambda e, s_x2=s_x2, stb=stb: e.activation(
                        s_x2, ps[stb][:, 0:512], AF.Ln, bias=C_REPS, scale=C_I128),
                        inc=("s_ac", 1))
                    bank_free[stb] = t_r1
                    bd.dep(ACT, evac["u"])
                    t_r2 = bd.run(ACT, lambda e, s_k=s_k, s_x2=s_x2: e.activation(
                        s_k, s_x2, AF.Exp, scale=C_NH), inc=("s_ac", 1))
                    bd.dep(V, t_r2)
                    bd.dep(V, t_q2)
                    bd.run(V, lambda e, Q2BF=Q2BF, QBF=QBF, s_k=s_k:
                           e.tensor_tensor(Q2BF, QBF, s_k, AL.mult))
                    bd.dep(V, evac["g"])
                    bd.run(V, lambda e, hc=hc, Q2BF=Q2BF, s_sig=s_sig:
                           e.tensor_tensor(P1(hc), Q2BF, s_sig, AL.mult))
                    t_p2m = bd.run(V, lambda e, hc=hc, s_sig=s_sig, VBF=VBF:
                                   e.scalar_tensor_tensor(P2M(hc), s_sig, 1.0, VBF,
                                   AL.subtract, AL.mult), inc=("s_dv", 1))
                    phead_toks.append(t_p2m)
                    prev_p2m[par] = t_p2m

            bd.dep(V, t_ones)
            bd.dep(V, prev_lnpf[par])
            clip_i = evac["a"][1]
            for b_ in range(2):
                for c in range(NCH):
                    cs0 = b_ * 256 + c * CH
                    t_scan = bd.run(V, lambda e, cs0=cs0, s_pf=s_pf, s_a=s_a:
                        e.tensor_tensor_scan(
                        s_pf[:, cs0:cs0 + CH], s_a[:, cs0:cs0 + CH],
                        ONES_F[:, 0:CH], 1.0, AL.mult, AL.mult), base=clip_i,
                        inc=("s_dv", 1) if (b_ == 1 and c == NCH - 1) else None)
            scan_i = bd.vidx(V)
            bd.run(V, lambda e, hc=hc, s_pf=s_pf: e.tensor_copy(PREFIX(hc), s_pf),
                   base=scan_i)
            for b_ in range(2):
                base = hc * 8 + b_ * 4
                for c in range(NCH):
                    bd.run(V, lambda e, base=base, c=c, b_=b_, s_pf=s_pf: e.tensor_copy(
                        PFE[:, base + c:base + c + 1],
                        s_pf[:, b_ * 256 + c * CH + CH - 1: b_ * 256 + c * CH + CH]),
                        base=scan_i)
            for b_ in range(2):
                base = hc * 8 + b_ * 4
                bd.run(V, lambda e, base=base: e.tensor_copy(
                    APF[:, base:base + 1], ONES_F[:, 0:1]))
                for c in range(1, NCH):
                    bd.run(V, lambda e, base=base, c=c: e.tensor_tensor(
                        APF[:, base + c:base + c + 1], APF[:, base + c - 1:base + c],
                        PFE[:, base + c - 1:base + c], AL.mult))
                bd.run(V, lambda e, hc=hc, b_=b_, base=base: e.tensor_tensor(
                    S_LOC[:, hc * 2 + b_:hc * 2 + b_ + 1],
                    APF[:, base + 3:base + 4], PFE[:, base + 3:base + 4], AL.mult))
            bd.dep(ACT, t_scan)
            bd.dep(ACT, phead_toks[-1])
            t_lp = bd.run(ACT, lambda e, s_x2=s_x2, s_pf=s_pf: e.activation(
                s_x2, s_pf, AF.Ln), inc=("s_ac", 1))
            prev_lnpf[par] = t_lp
            t_rp = bd.run(ACT, lambda e, s_sig=s_sig, s_x2=s_x2: e.activation(
                s_sig, s_x2, AF.Exp, scale=C_N1), inc=("s_ac", 1))
            bd.dep(V, t_rp)
            t_term = bd.run(V, lambda e, s_a=s_a, s_sig=s_sig, s_u=s_u:
                            e.scalar_tensor_tensor(s_a, s_sig, 1e6, s_u,
                            AL.min, AL.mult), inc=("s_dv", 1))
            term_i = t_term[1]
            t_acc = None
            for b_ in range(2):
                for c in range(NCH):
                    cs0 = b_ * 256 + c * CH
                    t_acc = bd.run(V, lambda e, hc=hc, cs0=cs0, s_a=s_a:
                        e.tensor_tensor_scan(
                        ACC(hc)[:, cs0:cs0 + CH], ONES_F[:, 0:CH],
                        s_a[:, cs0:cs0 + CH], 0.0, AL.mult, AL.add), base=term_i,
                        inc=("s_dv", 1) if (b_ == 1 and c == NCH - 1) else None)
            for b_ in range(2):
                base = hc * 8 + b_ * 4
                bd.run(V, lambda e, base=base: e.memset(Z0[:, base:base + 1], 0.0))
                for c in range(NCH):
                    accv = ACC(hc)[:, b_ * 256 + c * CH + CH - 1: b_ * 256 + c * CH + CH]
                    dst = (Z0[:, base + c + 1:base + c + 2] if c < NCH - 1
                           else S_LOC[:, 32 + hc * 2 + b_: 32 + hc * 2 + b_ + 1])
                    tok = bd.run(V, lambda e, dst=dst, base=base, c=c, accv=accv:
                                 e.scalar_tensor_tensor(dst,
                                 accv, Z0[:, base + c:base + c + 1],
                                 PFE[:, base + c:base + c + 1], AL.add, AL.mult),
                                 inc=("s_dv", 1) if c == NCH - 1 else None)
            t_bc_toks.append(tok)
            phead_toks.append(t_acc)
            prev_tacc[par] = tok

        bd.deps(SP, t_bc_toks)
        t_s1 = bd.run(SP, lambda e, p=p: e.dma_start(out=sc_in[p], in_=S_LOC[:]),
                      inc=("s_io", 16))
        free_tok["S_LOC"] = t_s1
        bd.dep(P, t_s1)
        t_cc = bd.run(P, lambda e, p=p: e.collective_compute(
            "AllGather", AL.bypass, replica_groups=[list(range(NCORES))],
            ins=[sc_in[p].opt()], outs=[sc_out[p].opt()]), inc=("s_cc", 1))
        bd.dep(SP, t_cc)
        bd.dep(SP, free_tok["S_ALL"])
        t_sa = bd.run(SP, lambda e, p=p: e.dma_start(
            out=S_ALL[:].rearrange("p (r c) -> p r c", r=NCORES),
            in_=sc_out[p].rearrange("(r p) c -> p r c", p=128)), inc=("s_io", 16))

        bd.dep(V, t_sa)
        bd.run(V, lambda e: e.memset(PREF[:, 0:32], 0.0))
        for r in range(NCORES):
            Ar = S_ALL[:, r * 64:r * 64 + 32]
            Br = S_ALL[:, r * 64 + 32:r * 64 + 64]
            bd.run(V, lambda e, Ar=Ar, r=r: e.tensor_tensor(
                CSC[:, 0:32], Ar, PREF[:, r * 32:(r + 1) * 32], AL.mult))
            bd.run(V, lambda e, Br=Br, r=r: e.tensor_tensor(
                PREF[:, (r + 1) * 32:(r + 2) * 32], CSC[:, 0:32], Br, AL.add))
        pp = [CSC[:, 0:32], CSC[:, 32:64]]
        for r in range(NCORES):
            o = SINIT[:] if r == NCORES - 1 else pp[r % 2]
            if r == 0:
                bd.run(V, lambda e, o=o: e.tensor_scalar(
                    o, PREF[:, 0:32], OH[:, 0:1], None, AL.mult))
            else:
                bd.run(V, lambda e, o=o, r=r, i=pp[(r + 1) % 2]: e.scalar_tensor_tensor(
                    o, PREF[:, r * 32:(r + 1) * 32], OH[:, r:r + 1], i, AL.mult, AL.add))
        sr0 = bd.vidx("dve")
        for hc in range(H):
            for b_ in range(2):
                base = hc * 8 + b_ * 4
                sic = SINIT[:, hc * 2 + b_:hc * 2 + b_ + 1]
                for c in range(NCH):
                    bd.run(V, lambda e, base=base, c=c, sic=sic: e.scalar_tensor_tensor(
                        SREAL[:, base + c:base + c + 1], APF[:, base + c:base + c + 1],
                        sic, Z0[:, base + c:base + c + 1], AL.mult, AL.add), base=sr0)
        t_sreal = bd.run(V, lambda e: e.engine_nop(), inc=("s_dv", 1))
        free_tok["S_ALL"] = t_sreal

        y_toks = []
        attn_t1 = []
        bd.dep(V, t_sreal)
        bd.deps(V, wring_free)
        g0 = bd.vidx("dve")
        last_t1 = [g0, g0]
        for hc in range(H):
            par = hc % 2
            sa = SSET[par]["a"]
            sk = SSET[par]["k"]
            for b_ in range(2):
                base = hc * 8 + b_ * 4
                for c in range(NCH):
                    cs0 = b_ * 256 + c * CH
                    bd.run(V, lambda e, hc=hc, cs0=cs0, base=base, c=c, sa=sa:
                        e.scalar_tensor_tensor(
                        sa[:, cs0:cs0 + CH], ACC(hc)[:, cs0:cs0 + CH],
                        SREAL[:, base + c:base + c + 1], PREFIX(hc)[:, cs0:cs0 + CH],
                        AL.add, AL.mult), base=last_t1[par])
            t1 = bd.run(V, lambda e, hc=hc, sa=sa, sk=sk: e.tensor_tensor(
                sk, P1(hc), sa, AL.mult), inc=("s_dv", 1))
            attn_t1.append(t1)
            last_t1[par] = t1[1]
            y_toks.append(bd.run(V, lambda e, hc=hc, sk=sk: e.tensor_tensor(
                YB(hc), sk, P2M(hc), AL.subtract), inc=("s_dv", 1)))
        free_tok["b_arena"] = y_toks[-1]
        free_tok["d_arena"] = attn_t1[7]
        free_tok["d_arena2"] = attn_t1[15]

        for j in range(16):
            wb_i = j % 2
            bd.dep(P, wring_free[wb_i])
            t_wd = bd.run(P, lambda e, j=j, wb_i=wb_i: e.dma_start(
                out=WBUF[:, wb_i * 2048:wb_i * 2048 + 2048],
                in_=w_out_t[j].rearrange("p dc q -> p (dc q)")),
                inc=("s_w0" if wb_i == 0 else "s_w1", 16))
            bn = "mmA" if wb_i == 0 else "mmB"
            bd.dep(PE, t_wd)
            bd.dep(PE, bank_free[bn])
            t_mj = None
            for ch in range(DC):
                if ch % 4 == 0:
                    bd.dep(PE, y_toks[min(ch + 3, DC - 1)])
                wv = WBUF[:, wb_i * 2048 + ch * 128:wb_i * 2048 + (ch + 1) * 128]
                t_mj = pe_mm(ps[bn][:, 0:512], wv, YB(ch), start=(ch == 0), stop=(ch == DC - 1),
                             inc=("s_pe", 1) if ch == DC - 1 else None)
            wring_free[wb_i] = t_mj
            bd.dep(V, t_mj)
            bd.dep(V, free_tok["scr_wo"])
            scr = SCR[:, 0:512]
            xs = Xc(j)[:, pc0:pc0 + PW]
            t_e1 = bd.run(V, lambda e, bn=bn, scr=scr, xs=xs: e.scalar_tensor_tensor(
                scr, ps[bn][:, 0:512], 0.0, xs, AL.bypass, AL.add), inc=("s_dv", 1))
            bank_free[bn] = t_e1
            bd.dep(ACT, t_e1)
            t_e2 = bd.run(ACT, lambda e, scr=scr, xs=xs: e.copy(xs, scr), inc=("s_ac", 1))
            free_tok["scr_wo"] = t_e2
            xup_toks.append(t_e2)
            x_ready[j].append(t_e2)

    emit_ffn(0)
    emit_attn_pair(0)
    emit_attn_pair(1)
    emit_ffn(1)

    for dc in range(DC):
        bd.deps(SP, x_ready[dc])
        bd.run(SP, lambda e, dc=dc: e.dma_start(out=out[dc], in_=Xc(dc)), inc=("s_st", 16))
    bd.dep(SP, ("s_st", bd.cnt["s_st"]))

    with ExitStack() as stack:
        handles = {}
        for name in bd.sem_names:
            handles[name] = stack.enter_context(nc.semaphore(name))
        block = stack.enter_context(nc.Block())

        @block.tensor
        def _(t):
            bd.replay("pe", t, handles)

        @block.scalar
        def _(s):
            bd.replay("act", s, handles)

        @block.vector
        def _(v):
            bd.replay("dve", v, handles)

        @block.gpsimd
        def _(g):
            bd.replay("pool", g, handles)

        @block.sync
        def _(sy):
            bd.replay("sp", sy, handles)

    return nc


def kernel(L0_attn_x, L0_attn_w_in, L0_attn_w_out, L0_ffn_switch_w, L0_ffn_switch_b,
           L0_ffn_w1, L0_ffn_b1, L0_ffn_w2, L0_ffn_b2, L0_ffn_w3, L0_ffn_b3):
    x = np.ascontiguousarray(np.asarray(L0_attn_x, dtype=np.float32))
    w_in = np.ascontiguousarray(np.asarray(L0_attn_w_in, dtype=np.float32))
    w_out = np.ascontiguousarray(np.asarray(L0_attn_w_out, dtype=np.float32))
    sw = np.ascontiguousarray(np.asarray(L0_ffn_switch_w, dtype=np.float32))
    swb = np.ascontiguousarray(np.asarray(L0_ffn_switch_b, dtype=np.float32))
    w1 = np.ascontiguousarray(np.asarray(L0_ffn_w1, dtype=np.float32))
    b1 = np.ascontiguousarray(np.asarray(L0_ffn_b1, dtype=np.float32))
    w2 = np.ascontiguousarray(np.asarray(L0_ffn_w2, dtype=np.float32))
    b2 = np.ascontiguousarray(np.asarray(L0_ffn_b2, dtype=np.float32))
    w3 = np.ascontiguousarray(np.asarray(L0_ffn_w3, dtype=np.float32))
    b3 = np.ascontiguousarray(np.asarray(L0_ffn_b3, dtype=np.float32))

    wt = w_in.reshape(5, H, 128, DC, 128)
    w_in_t = np.ascontiguousarray(wt.transpose(1, 0, 4, 3, 2))
    wo = w_out.reshape(16, 128, DC, 128)
    w_out_t = np.ascontiguousarray(wo.transpose(0, 3, 2, 1))
    w3r = w3.reshape(16, 128, DC, 128)
    w3_t = np.ascontiguousarray(w3r.transpose(0, 3, 2, 1))
    sw3 = sw.reshape(8, 2048, 2048)
    w1T = np.ascontiguousarray(w1.transpose(0, 2, 1))
    w2T = np.ascontiguousarray(w2.transpose(0, 2, 1))
    w12T = np.ascontiguousarray(np.concatenate([w1T, w2T], axis=0).transpose(1, 0, 2))
    b1T = np.ascontiguousarray(b1.T)
    b2T = np.ascontiguousarray(b2.T)
    b3row = np.ascontiguousarray(b3[None, :])
    swb4 = np.ascontiguousarray(np.broadcast_to(swb[None, :], (4, 8)))

    nc = build_nc()

    in_maps = []
    for c in range(NCORES):
        sl = slice(256 * c, 256 * c + 256)
        x_fm = np.ascontiguousarray(
            x[:, sl, :].transpose(2, 0, 1).reshape(DC, 128, T))
        sw_c = np.ascontiguousarray(
            sw3[:, sl, :].transpose(0, 2, 1).reshape(8, DC, 128, TB).transpose(0, 2, 1, 3))
        oh = np.zeros((128, 8), np.float32)
        oh[:, c] = 1.0
        in_maps.append({
            "x_fm": x_fm, "w_in_t": w_in_t, "w_out_t": w_out_t, "w3_t": w3_t,
            "sw_t": sw_c, "w12T": w12T, "b1T": b1T, "b2T": b2T,
            "b3row": b3row, "swb": swb4, "onehot": oh,
        })

    res = run_bass_kernel_spmd(nc, in_maps, core_ids=list(range(NCORES)))

    outp = np.empty((4, 2048, 2048), np.float32)
    for c in range(NCORES):
        r = np.asarray(res.results[c]["out"])
        r = r.reshape(2048, 4, 256).transpose(1, 2, 0)
        outp[:, 256 * c:256 * c + 256, :] = r
    return outp


# version 21 (final)
# speedup vs baseline: 1.0929x; 1.0130x over previous
import numpy as np
from collections import defaultdict
from contextlib import ExitStack

import concourse.bass as bass
import concourse.mybir as mybir
from concourse.bass_utils import run_bass_kernel_spmd

F32 = mybir.dt.float32
BF16 = mybir.dt.bfloat16
AL = mybir.AluOpType
AF = mybir.ActivationFunctionType
AX = mybir.AxisListType

D = 2048
DC = 16
T = 1024
TB = 256
B = 4
H = 16
PW = 512
CH = 64
NCH = 4
LN_EPS = 1e-5
RMS_EPS = 1.1920928955078125e-07
NCORES = 8

ENGINES = ("pe", "act", "dve", "pool", "sp")


class Build:

    def __init__(self, nc):
        self.nc = nc
        self.streams = {e: [] for e in ENGINES}
        self.cnt = defaultdict(int)
        self.lastwait = {e: defaultdict(int) for e in ENGINES}
        self.sem_names = []

    def sem(self, name):
        if name not in self.sem_names:
            self.sem_names.append(name)
        return name

    def dep(self, eng, tok):
        if tok is None:
            return
        s, v = tok
        if v <= self.lastwait[eng][s]:
            return
        self.lastwait[eng][s] = v
        self.streams[eng].append(("wait", s, v))

    def deps(self, eng, toks):
        best = {}
        for t in toks:
            if t is None:
                continue
            s, v = t
            if v > best.get(s, -1):
                best[s] = v
        for s, v in best.items():
            self.dep(eng, (s, v))

    SERIALIZED = {"dve": "s_dvs", "act": "s_acs"}

    def run(self, eng, fn, inc=None, base=None):
        tok = None
        if eng in self.SERIALIZED:
            s = self.SERIALIZED[eng]
            self.sem(s)
            idx = self.cnt[s] + 1
            waitv = (idx - 1) if base is None else min(base, idx - 1)
            if waitv > 0:
                self.dep(eng, (s, waitv))
            self.cnt[s] = idx
            if inc is not None:
                tok = (s, idx)
            self.streams[eng].append(("op", fn, (s, 1)))
            return tok
        if inc is not None:
            s, amt = inc
            self.sem(s)
            self.cnt[s] += amt
            tok = (s, self.cnt[s])
        self.streams[eng].append(("op", fn, inc))
        return tok

    def vidx(self, eng):
        return self.cnt[self.SERIALIZED[eng]]

    def replay(self, eng, obj, handles):
        for item in self.streams[eng]:
            if item[0] == "wait":
                _, s, v = item
                obj.wait_ge(handles[s], v)
            else:
                _, fn, inc = item
                inst = fn(obj)
                if inc is not None:
                    s, amt = inc
                    inst.then_inc(handles[s], amt)


def build_nc():
    nc = bass.Bass("TRN2", target_bir_lowering=False, debug=False)

    def dinp(name, shape, dt=F32):
        return nc.dram_tensor(name, list(shape), dt, kind="ExternalInput").ap()

    x_fm = dinp("x_fm", [DC, 128, T])
    w_in_t = dinp("w_in_t", [H, 5, 128, DC, 128])
    w_out_t = dinp("w_out_t", [16, 128, DC, 128])
    w3_t = dinp("w3_t", [16, 128, DC, 128])
    sw_t = dinp("sw_t", [8, 128, DC, TB])
    w12T_in = dinp("w12T", [128, 16, 128])
    b1T_in = dinp("b1T", [128, 8])
    b2T_in = dinp("b2T", [128, 8])
    b3row_in = dinp("b3row", [1, D])
    swb_in = dinp("swb", [4, 8])
    onehot_in = dinp("onehot", [128, 8])

    out = nc.dram_tensor("out", [DC, 128, T], F32, kind="ExternalOutput").ap()

    rt_in = [nc.dram_tensor(f"rt_in{i}", [32], F32).ap() for i in range(2)]
    rt_out = [nc.dram_tensor(f"rt_out{i}", [32], F32).ap() for i in range(2)]
    sc_in = [nc.dram_tensor(f"sc_in{i}", [128, 64], F32).ap() for i in range(2)]
    sc_out = [nc.dram_tensor(f"sc_out{i}", [NCORES * 128, 64], F32).ap() for i in range(2)]
    selb = [nc.dram_tensor(f"selb{i}", [4, 40], F32).ap() for i in range(2)]

    def salloc(name, shape, dt):
        return nc.alloc_sbuf_tensor(name, list(shape), dt)

    X = salloc("X", [128, DC * T], F32)
    A_BF = salloc("A_BF", [128, 16384], BF16)
    B_BF = salloc("B_BF", [128, 8192], BF16)
    C_F = salloc("C_F", [128, 8192], F32)
    D_BF = salloc("D_BF", [128, 16384], BF16)
    WBUF = salloc("WBUF", [128, 2 * 2048], BF16)
    SCR = salloc("SCR", [128, 4 * 512], F32)
    SCB = salloc("SCB", [128, 3 * 512], BF16)
    ONES_F = salloc("ONES_F", [128, 128], F32)
    ONES_B = salloc("ONES_B", [128, 512], BF16)
    CONSTS = salloc("CONSTS", [128, 6], F32)
    S_LOC = salloc("S_LOC", [128, 64], F32)
    S_ALL = salloc("S_ALL", [128, NCORES * 64], F32)
    PREF = salloc("PREF", [128, 9 * 32], F32)
    SINIT = salloc("SINIT", [128, 32], F32)
    PFE = salloc("PFE", [128, 128], F32)
    APF = salloc("APF", [128, 128], F32)
    Z0 = salloc("Z0", [128, 128], F32)
    SREAL = salloc("SREAL", [128, 128], F32)
    CSC = salloc("CSC", [128, 64], F32)
    LGS = salloc("LGS", [1, 32], F32)
    LG = salloc("LG", [4, 64], F32)
    MSK = salloc("MSK", [4, 40], F32)
    GROW = salloc("GROW", [1, 160], F32)
    GSEL = salloc("GSEL", [128, 160], F32)
    B1T = salloc("B1T", [128, 8], F32)
    B2T = salloc("B2T", [128, 8], F32)
    B1S = salloc("B1S", [128, 8], F32)
    B2G = salloc("B2G", [128, 4], F32)
    SWB = salloc("SWB", [4, 8], F32)
    OH = salloc("OH", [128, 8], F32)
    SELS = salloc("SELS", [128, 2 * 128], BF16)
    SELF = salloc("SELF", [128, 2], F32)

    ps = {}
    def palloc(name, shape):
        ps[name] = nc.alloc_psum_tensor(name, list(shape), F32)
        return ps[name]

    palloc("mmA", [128, 512])
    palloc("mmB", [128, 512])
    palloc("bcA", [128, 512])
    palloc("bcB", [128, 512])
    palloc("st1", [128, 1024])
    palloc("st2", [128, 1024])

    bd = Build(nc)
    V, P, ACT, PE, SP = "dve", "pool", "act", "pe", "sp"
    for s in ("s_xin", "s_xi2", "s_sm", "s_exw", "s_b3", "s_io", "s_w0", "s_w1",
              "s_w2", "s_w3", "s_sw0", "s_sw1", "s_pe", "s_dv", "s_ac",
              "s_cc", "s_st", "s_dvs", "s_acs"):
        bd.sem(s)

    bank_free = {k: None for k in ps}
    wring_free = [None, None, None, None]
    BN4 = ["mmA", "mmB", "bcA", "bcB"]
    free_tok = defaultdict(lambda: None)

    def Xc(dc):
        return X[:, dc * T:(dc + 1) * T]

    def pe_mm(bank_ap, lhsT, rhs, start, stop, inc=None):
        return bd.run(PE, lambda e, bank_ap=bank_ap, lhsT=lhsT, rhs=rhs, start=start,
                      stop=stop: e.matmul(bank_ap, lhsT, rhs, start=start, stop=stop),
                      inc=inc)

    tx_toks = []
    for dc in range(DC):
        sem = "s_xin" if dc < 8 else "s_xi2"
        tx_toks.append(bd.run(SP, lambda e, dc=dc: e.dma_start(out=Xc(dc), in_=x_fm[dc]),
                              inc=(sem, 16)))
    t_smalls = None
    for (dst, src) in ((B1T, b1T_in), (B2T, b2T_in), (SWB, swb_in), (OH, onehot_in)):
        t_smalls = bd.run(SP, lambda e, d=dst, s=src: e.dma_start(out=d[:], in_=s),
                          inc=("s_sm", 16))
    bd.run(V, lambda e: e.memset(ONES_F[:], 1.0))
    bd.run(V, lambda e: e.memset(CONSTS[:, 0:1], LN_EPS))
    bd.run(V, lambda e: e.memset(CONSTS[:, 1:2], 2.0))
    bd.run(V, lambda e: e.memset(CONSTS[:, 2:3], RMS_EPS))
    bd.run(V, lambda e: e.memset(CONSTS[:, 3:4], 1.0 / 128.0))
    bd.run(V, lambda e: e.memset(CONSTS[:, 4:5], -0.5))
    bd.run(V, lambda e: e.memset(CONSTS[:, 5:6], -1.0))
    t_ones = bd.run(V, lambda e: e.memset(ONES_B[:], 1.0), inc=("s_dv", 1))
    bd.dep(ACT, t_ones)
    C_EPS = CONSTS[:, 0:1]
    C_TWO = CONSTS[:, 1:2]
    C_REPS = CONSTS[:, 2:3]
    C_I128 = CONSTS[:, 3:4]
    C_NH = CONSTS[:, 4:5]
    C_N1 = CONSTS[:, 5:6]

    xup_toks = list(tx_toks)
    x_ready = {dc: [("s_xin", 128) if dc < 8 else ("s_xi2", 128)] for dc in range(DC)}

    def emit_ln(xln_dst, col0, width, xbf_scr, x2bf_scr, ascr, nscr, apply_toks=None):
        nh = (width + 511) // 512
        bd.dep(PE, t_ones)
        bd.dep(PE, bank_free["st1"])
        bd.dep(PE, bank_free["st2"])
        pe_tok = [None, None]
        t_stats = None
        for dc in range(DC):
            xs = Xc(dc)[:, col0:col0 + width]
            bd.dep(V, pe_tok[dc % 2])
            bd.deps(V, x_ready[dc])
            t_c = bd.run(V, lambda e, dc=dc, xs=xs: e.tensor_copy(xbf_scr(dc), xs),
                         inc=("s_dv", 1))
            t_s = bd.run(V, lambda e, dc=dc: e.tensor_tensor(
                x2bf_scr(dc), xbf_scr(dc), xbf_scr(dc), AL.mult),
                base=t_c[1], inc=("s_dv", 1))
            bd.dep(PE, t_c)
            bd.dep(PE, t_s)
            for h in range(nh):
                w = min(512, width - h * 512)
                pe_mm(ps["st1"][:, h * 512:h * 512 + w], ONES_B[:, 0:128],
                      xbf_scr(dc)[:, h * 512:h * 512 + w], start=(dc == 0), stop=(dc == DC - 1))
                t_stats = pe_mm(ps["st2"][:, h * 512:h * 512 + w], ONES_B[:, 0:128],
                                x2bf_scr(dc)[:, h * 512:h * 512 + w],
                                start=(dc == 0), stop=(dc == DC - 1), inc=("s_pe", 1))
            pe_tok[dc % 2] = t_stats
        t_last = None
        t_v = None
        for h in range(nh):
            hs = h * 512
            w = min(512, width - hs)
            SM = SCR[0:1, 0:w]
            SV = SCR[0:1, 512:512 + w]
            S3 = SCR[0:1, 1024:1024 + w]
            S4 = SCR[0:1, 1536:1536 + w]
            bd.dep(V, t_stats)
            if h > 0:
                bd.dep(V, t_bc)
            bd.run(V, lambda e, hs=hs, w=w, SM=SM: e.tensor_scalar(
                SM, ps["st1"][0:1, hs:hs + w], 1.0 / D, None, AL.mult))
            t_v = bd.run(V, lambda e, hs=hs, w=w, SV=SV: e.tensor_scalar(
                SV, ps["st2"][0:1, hs:hs + w], 1.0 / D, None, AL.mult),
                inc=("s_dv", 1))
            bd.run(V, lambda e, SM=SM, S3=S3: e.tensor_tensor(S3, SM, SM, AL.mult))
            t_var = bd.run(V, lambda e, SV=SV, S3=S3, S4=S4: e.tensor_tensor(
                S4, SV, S3, AL.subtract), inc=("s_dv", 1))
            bd.dep(ACT, t_var)
            bd.run(ACT, lambda e, S3=S3, S4=S4: e.activation(
                S3, S4, AF.Ln, bias=CONSTS[0:1, 0:1]), inc=("s_ac", 1))
            t_sd = bd.run(ACT, lambda e, SV=SV, S3=S3: e.activation(
                SV, S3, AF.Exp, scale=CONSTS[0:1, 4:5]), inc=("s_ac", 1))
            bd.dep(V, t_sd)
            bd.run(V, lambda e, SM=SM, SV=SV, S3=S3: e.tensor_tensor(
                S3, SM, SV, AL.mult))
            t_rows = bd.run(V, lambda e, S3=S3, S4=S4: e.tensor_scalar(
                S4, S3, -1.0, None, AL.mult), inc=("s_dv", 1))
            bd.dep(PE, bank_free["bcA"])
            bd.dep(PE, bank_free["bcB"])
            bd.dep(PE, t_rows)
            pe_mm(ps["bcA"][:, 0:w], ONES_F[0:1, :], SV, True, True)
            t_bc = pe_mm(ps["bcB"][:, 0:w], ONES_F[0:1, :], S4,
                         True, True, inc=("s_pe", 1))
            bd.dep(V, t_bc)
            mt = {}
            at = {}
            g0 = bd.vidx("dve")
            for dc in range(DC):
                xs = Xc(dc)[:, col0 + hs:col0 + hs + w]
                scr = ascr(dc % nscr)[:, 0:w]
                bd.deps(V, x_ready[dc])
                mt[dc] = bd.run(V, lambda e, xs=xs, scr=scr, w=w: e.tensor_tensor(
                    scr, xs, ps["bcA"][:, 0:w], AL.mult),
                    base=(at[dc - nscr][1] if dc - nscr in at else g0),
                    inc=("s_dv", 1))
                if dc >= 1:
                    d2 = dc - 1
                    dst = xln_dst(d2)[:, hs:hs + w]
                    scr2 = ascr(d2 % nscr)[:, 0:w]
                    at[d2] = bd.run(V, lambda e, dst=dst, scr2=scr2, w=w: e.tensor_tensor(
                        dst, scr2, ps["bcB"][:, 0:w], AL.add), base=mt[d2][1],
                        inc=("s_dv", 1))
                    if apply_toks is not None:
                        apply_toks.setdefault(d2, []).append(at[d2])
            d2 = DC - 1
            dst = xln_dst(d2)[:, hs:hs + w]
            scr2 = ascr(d2 % nscr)[:, 0:w]
            t_last = bd.run(V, lambda e, dst=dst, scr2=scr2, w=w: e.tensor_tensor(
                dst, scr2, ps["bcB"][:, 0:w], AL.add), base=mt[d2][1],
                inc=("s_dv", 1))
            if apply_toks is not None:
                apply_toks.setdefault(d2, []).append(t_last)
            bank_free["bcA"] = t_last
            bank_free["bcB"] = t_last
        bank_free["st1"] = t_v
        bank_free["st2"] = t_v
        return t_last

    sw_free = [None, None]

    def emit_ffn(call):
        XLN = lambda dc: A_BF[:, dc * 1024:(dc + 1) * 1024]
        SWbuf = lambda i: D_BF[:, (i % 2) * 4096:(i % 2) * 4096 + 4096]
        Pbuf = lambda i: D_BF[:, 12288 + (i % 4) * 1024: 12288 + (i % 4) * 1024 + 1024]
        LNxb = lambda dc: D_BF[:, 8192 + (dc % 2) * 1024: 8192 + (dc % 2) * 1024 + 1024]
        LNx2 = lambda dc: D_BF[:, 10240 + (dc % 2) * 1024: 10240 + (dc % 2) * 1024 + 1024]
        EXW1 = lambda e_: B_BF[:, e_ * 128:(e_ + 1) * 128]
        EXW2 = lambda e_: B_BF[:, 1024 + e_ * 128: 1024 + (e_ + 1) * 128]
        SELW1 = lambda bk: B_BF[:, 2048 + bk * 128: 2048 + (bk + 1) * 128]
        SELW2 = lambda bk: B_BF[:, 3072 + bk * 128: 3072 + (bk + 1) * 128]
        HB = lambda i: B_BF[:, 4096 + (i % 2) * 256: 4096 + (i % 2) * 256 + 256]
        C_bf = C_F.bitcast(BF16)
        Y1 = lambda dc: C_bf[:, dc * 1024:(dc + 1) * 1024]

        rg_pre = bd.vidx("dve")
        bd.deps(V, [wring_free[2], wring_free[3]])
        D_F = D_BF.bitcast(F32)
        ln_toks = {}
        t_ln = emit_ln(XLN, 0, T, LNxb, LNx2,
                       lambda i: D_F[:, 4096 + i * 512: 4096 + (i + 1) * 512], 4,
                       apply_toks=ln_toks)

        bd.dep(P, free_tok["b_arena"])
        t_exw2 = bd.run(P, lambda e: e.dma_start(
            out=B_BF[:, 0:2048],
        in_=w12T_in.rearrange("p e q -> p (e q)")),
            inc=("s_exw", 16))

        sw_dma = [None, None]
        lg_toks = []
        pbuf_free = [None, None, None, None]

        def emit_logit_reduce(pr):
            t_mm_p, stats_p, e_p = pr
            bd.dep(V, t_mm_p)
            bd.dep(V, free_tok["LGS"])
            lt = bd.run(V, lambda e, e_p=e_p, stats_p=stats_p: e.tensor_reduce(
                LGS.rearrange("p (b e8) -> p b e8", e8=8)[:, :, e_p:e_p + 1],
                ps[stats_p][0:1, :].rearrange("p (b t) -> p b t", b=B),
                AX.X, AL.add), inc=("s_dv", 1))
            bank_free[stats_p] = lt
            lg_toks.append(lt)

        pending_red = None
        for e_ in range(8):
            if e_ < 2:
                bd.dep(P, sw_free[e_ % 2])
                bd.dep(P, free_tok["d_arena"] if e_ == 0 else free_tok["d_arena2"])
                sw_dma[e_ % 2] = bd.run(P, lambda e, e_=e_: e.dma_start(
                    out=SWbuf(e_),
                    in_=sw_t[e_].rearrange("p dc t -> p (dc t)")),
                    inc=("s_sw0" if e_ % 2 == 0 else "s_sw1", 16))
            stats = "st1" if e_ % 2 == 0 else "st2"
            bd.dep(V, sw_dma[e_ % 2])
            t_mm = None
            for dc in range(DC):
                if dc == 1 and pending_red is not None:
                    emit_logit_reduce(pending_red)
                    pending_red = None
                swv = SWbuf(e_)[:, dc * 256:(dc + 1) * 256]
                sw_b = swv.rearrange("p (o t) -> p o t", o=1).broadcast_to([128, 4, 256])
                pb = Pbuf(dc)
                bd.dep(V, pbuf_free[dc % 4])
                bd.deps(V, ln_toks.get(dc, []))
                t_p = bd.run(V, lambda e, dc=dc, sw_b=sw_b, pb=pb, e_=e_: e.tensor_tensor(
                    pb.rearrange("p (b t) -> p b t", b=B),
                    XLN(dc).rearrange("p (b t) -> p b t", b=B), sw_b, AL.mult),
                    base=rg_pre, inc=("s_dv", 1))
                bd.dep(PE, bank_free[stats])
                bd.dep(PE, t_p)
                for hf in range(2):
                    t_mm = pe_mm(ps[stats][:, hf * 512:(hf + 1) * 512], ONES_B[:, 0:128],
                                 pb[:, hf * 512:(hf + 1) * 512],
                                 start=(dc == 0), stop=(dc == DC - 1),
                                 inc=("s_pe", 1) if hf == 1 else None)
                pbuf_free[dc % 4] = t_mm
                t_p_last = t_p
            sw_free[e_ % 2] = t_p_last
            if e_ + 2 < 8:
                bd.dep(P, sw_free[e_ % 2])
                sw_dma[e_ % 2] = bd.run(P, lambda e, e_=e_: e.dma_start(
                    out=SWbuf(e_ + 2),
                    in_=sw_t[e_ + 2].rearrange("p dc t -> p (dc t)")),
                    inc=("s_sw0" if e_ % 2 == 0 else "s_sw1", 16))
            pending_red = (t_mm, stats, e_)
        emit_logit_reduce(pending_red)
        pending_red = None

        B3R = D_BF[0:1, 0:2048]
        bd.dep(P, sw_free[0])
        t_b3 = bd.run(P, lambda e: e.dma_start(out=B3R, in_=b3row_in), inc=("s_b3", 16))

        bd.deps(SP, lg_toks)
        t_s1 = bd.run(SP, lambda e: e.dma_start(
            out=rt_in[call].rearrange("(o c) -> o c", o=1), in_=LGS[0:1, :]), inc=("s_io", 16))
        free_tok["LGS"] = t_s1
        bd.dep(P, t_s1)
        t_cc = bd.run(P, lambda e: e.collective_compute(
            "AllReduce", AL.add, replica_groups=[list(range(NCORES))],
            ins=[rt_in[call].opt()], outs=[rt_out[call].opt()]), inc=("s_cc", 1))
        bd.dep(SP, t_cc)
        t_lg = bd.run(SP, lambda e: e.dma_start(
            out=LG[:, 0:8], in_=rt_out[call].rearrange("(b e) -> b e", b=4)),
            inc=("s_io", 16))

        lg0 = LG[:, 0:8]; ex_ = LG[:, 8:16]; pr = LG[:, 16:24]; pr2 = LG[:, 24:32]
        cs = LG[:, 32:40]; tmp8 = LG[:, 40:48]; mx = LG[:, 48:49]; nmx = LG[:, 49:50]
        sm = LG[:, 50:51]; rs = LG[:, 51:52]
        msk1 = MSK[:, 0:8]; msk2 = MSK[:, 8:16]; gm1 = MSK[:, 16:24]
        gm2 = MSK[:, 24:32]; gmc = MSK[:, 32:40]
        bd.dep(V, t_lg)
        bd.dep(V, t_smalls)
        bd.dep(V, free_tok["MSK"])
        bd.run(V, lambda e: e.tensor_tensor(lg0, lg0, SWB[:], AL.add))
        bd.run(V, lambda e: e.reduce_max(mx, lg0, AX.X))
        t_nm = bd.run(V, lambda e: e.tensor_scalar(nmx, mx, -1.0, None, AL.mult), inc=("s_dv", 1))
        bd.dep(ACT, t_nm)
        t_ex = bd.run(ACT, lambda e: e.activation(ex_, lg0, AF.Exp, bias=nmx[:, 0:1]),
                      inc=("s_ac", 1))
        bd.dep(V, t_ex)
        bd.run(V, lambda e: e.reduce_sum(sm, ex_, AX.X))
        bd.run(V, lambda e: e.reciprocal(rs, sm))
        bd.run(V, lambda e: e.tensor_scalar(pr, ex_, rs[:, 0:1], None, AL.mult))
        bd.run(V, lambda e: e.reduce_max(mx, pr, AX.X))
        bd.run(V, lambda e: e.tensor_scalar(tmp8, pr, mx[:, 0:1], None, AL.is_ge))
        bd.run(V, lambda e: e.tensor_tensor_scan(cs, ONES_F[0:4, 0:8], tmp8, 0.0, AL.mult, AL.add))
        bd.run(V, lambda e: e.tensor_scalar(cs, cs, 1.0, None, AL.is_equal))
        bd.run(V, lambda e: e.tensor_tensor(msk1, tmp8, cs, AL.mult))
        bd.run(V, lambda e: e.tensor_tensor(gm1, pr, msk1, AL.mult))
        bd.run(V, lambda e: e.tensor_scalar(tmp8, msk1, -1.0, 1.0, AL.mult, AL.add))
        bd.run(V, lambda e: e.tensor_tensor(pr2, pr, tmp8, AL.mult))
        bd.run(V, lambda e: e.reduce_max(mx, pr2, AX.X))
        bd.run(V, lambda e: e.tensor_scalar(tmp8, pr2, mx[:, 0:1], None, AL.is_ge))
        bd.run(V, lambda e: e.tensor_tensor_scan(cs, ONES_F[0:4, 0:8], tmp8, 0.0, AL.mult, AL.add))
        bd.run(V, lambda e: e.tensor_scalar(cs, cs, 1.0, None, AL.is_equal))
        bd.run(V, lambda e: e.tensor_tensor(msk2, tmp8, cs, AL.mult))
        bd.run(V, lambda e: e.tensor_tensor(gm2, pr2, msk2, AL.mult))
        t_msk = bd.run(V, lambda e: e.tensor_tensor(gmc, gm1, gm2, AL.add), inc=("s_dv", 1))

        bd.dep(SP, t_msk)
        t_sb = bd.run(SP, lambda e: e.dma_start(out=selb[call], in_=MSK[:, 0:40]),
                      inc=("s_io", 16))
        free_tok["MSK"] = t_sb
        bd.dep(SP, t_sb)
        t_sr = bd.run(SP, lambda e: e.dma_start(
            out=GROW[:], in_=selb[call].rearrange("b c -> (b c)")[None, :]), inc=("s_io", 16))
        bd.dep(PE, t_sr)
        bd.dep(PE, bank_free["bcA"])
        t_gb = pe_mm(ps["bcA"][:, 0:160], ONES_F[0:1, :], GROW[:], True, True, inc=("s_pe", 1))
        bd.dep(V, t_gb)
        t_gs = bd.run(V, lambda e: e.tensor_copy(GSEL[:, 0:160], ps["bcA"][:, 0:160]),
                      inc=("s_dv", 1))
        bank_free["bcA"] = t_gs

        bd.dep(V, t_exw2)
        sel_last = None
        chains = []
        for b_ in range(B):
            for k_ in range(2):
                bk = b_ * 2 + k_
                moff = b_ * 40 + (0 if k_ == 0 else 8)
                goff = b_ * 40 + (16 if k_ == 0 else 24)
                chains.append((SELW1(bk), EXW1, moff))
                chains.append((SELW2(bk), EXW2, goff))
        selw_tok = {}
        for grp in range(0, len(chains), 4):
            batch = chains[grp:grp + 4]
            prev = [None] * len(batch)
            for e_ in range(8):
                for lane, (dst, src, off) in enumerate(batch):
                    pp = [B_BF[:, 6144 + lane * 256:6144 + lane * 256 + 128],
                          B_BF[:, 6144 + lane * 256 + 128:6144 + lane * 256 + 256]]
                    o = dst if e_ == 7 else pp[e_ % 2]
                    if e_ == 0:
                        prev[lane] = bd.run(V, lambda e, o=o, src=src, off=off:
                            e.tensor_scalar(o, src(0), GSEL[:, off:off + 1], None,
                            AL.mult), base=max(free_tok["sel_lane"] or 0, t_gs[1]),
                            inc=("s_dv", 1))
                    else:
                        prev[lane] = bd.run(V, lambda e, o=o, src=src, e_=e_, off=off,
                            i=pp[(e_ + 1) % 2]: e.scalar_tensor_tensor(
                            o, src(e_), GSEL[:, off + e_: off + e_ + 1], i,
                            AL.mult, AL.add), base=prev[lane][1], inc=("s_dv", 1))
            sel_last = prev[-1]
            selw_tok[grp // 4] = list(prev)
            free_tok["sel_lane"] = max(t[1] for t in prev)
        bchains = []
        for b_ in range(B):
            for k_ in range(2):
                bk = b_ * 2 + k_
                moff = b_ * 40 + (0 if k_ == 0 else 8)
                bchains.append((B1S[:, bk:bk + 1], B1T, moff))
        for b_ in range(B):
            bchains.append((B2G[:, b_:b_ + 1], B2T, b_ * 40 + 32))
        selb_tok = {}
        selg_tok = None
        for grp in range(0, len(bchains), 4):
            batch = bchains[grp:grp + 4]
            prev = [None] * len(batch)
            for e_ in range(8):
                for lane, (dst, tsrc, off) in enumerate(batch):
                    ppf = [PREF[:, 272 + 2 * lane:273 + 2 * lane],
                           PREF[:, 273 + 2 * lane:274 + 2 * lane]]
                    o = dst if e_ == 7 else ppf[e_ % 2]
                    if e_ == 0:
                        prev[lane] = bd.run(V, lambda e, o=o, tsrc=tsrc, off=off:
                            e.tensor_scalar(o, tsrc[:, 0:1], GSEL[:, off:off + 1], None,
                            AL.mult), base=max(free_tok["selb_lane"] or 0, t_gs[1]),
                            inc=("s_dv", 1))
                    else:
                        prev[lane] = bd.run(V, lambda e, o=o, tsrc=tsrc, e_=e_, off=off,
                            i=ppf[(e_ + 1) % 2]: e.scalar_tensor_tensor(
                            o, tsrc[:, e_:e_ + 1], GSEL[:, off + e_: off + e_ + 1], i,
                            AL.mult, AL.add), base=prev[lane][1], inc=("s_dv", 1))
            sel_last = prev[-1]
            if grp // 4 < 2:
                selb_tok[grp // 4] = list(prev)
            else:
                selg_tok = list(prev)
            free_tok["selb_lane"] = max(t[1] for t in prev)
        t_sel = sel_last

        y1_toks = []
        obanks = ["mmA", "mmB", "bcA", "bcB"]
        hbanks = ["st1", "st2"]
        HB8 = lambda s: B_BF[:, 4096 + s * 256: 4096 + (s + 1) * 256]
        hb_i = 0
        bd.dep(PE, t_ln)
        for b_ in range(B):
            bd.deps(PE, selw_tok[b_])
            bd.deps(PE, selb_tok[b_ // 2])
            for g in range(4):
                hcs = [g * 4 + i for i in range(4)]
                gel = {}
                mm1t = {}
                for k_ in range(2):
                    bk = b_ * 2 + k_
                    for i, hc in enumerate(hcs):
                        s = k_ * 4 + i
                        hname = hbanks[hb_i % 2]
                        hb = ps[hname]
                        bd.dep(PE, bank_free[hname])
                        t_mm1 = pe_mm(hb[:, 0:256], SELW1(bk),
                                      XLN(hc)[:, b_ * 256:(b_ + 1) * 256], True, True,
                                      inc=("s_pe", 1))
                        bd.dep(ACT, t_mm1)
                        t_h = bd.run(ACT, lambda e, hb=hb, bk=bk, s=s: e.activation(
                            HB8(s), hb[:, 0:256], AF.Gelu, bias=B1S[:, bk:bk + 1]),
                            inc=("s_ac", 1))
                        bank_free[hname] = t_h
                        gel[s] = t_h
                        hb_i += 1
                for k_ in range(2):
                    bk = b_ * 2 + k_
                    for i, hc in enumerate(hcs):
                        s = k_ * 4 + i
                        ob = ps[obanks[i]]
                        bd.dep(PE, gel[s])
                        if k_ == 0:
                            bd.dep(PE, bank_free[obanks[i]])
                        t_mm2 = pe_mm(ob[:, 0:256], SELW2(bk), HB8(s),
                                      start=(k_ == 0), stop=(k_ == 1),
                                      inc=("s_pe", 1))
                        if k_ == 1:
                            bd.dep(V, t_mm2)
                            bd.dep(V, selg_tok[b_])
                            t_y = bd.run(V, lambda e, ob=ob, hc=hc, b_=b_: e.scalar_tensor_tensor(
                                Y1(hc)[:, b_ * 256:(b_ + 1) * 256], ob[:, 0:256],
                                B2G[:, b_:b_ + 1],
                                XLN(hc)[:, b_ * 256:(b_ + 1) * 256], AL.add, AL.add),
                                inc=("s_dv", 1))
                            bank_free[obanks[i]] = t_y
                            y1_toks.append(t_y)
        free_tok["a_ext"] = t_mm2
        for j in range(16):
            wb_i = j % 2
            bd.dep(P, wring_free[wb_i])
            t_wd = bd.run(P, lambda e, j=j, wb_i=wb_i: e.dma_start(
                out=WBUF[:, wb_i * 2048:wb_i * 2048 + 2048],
                in_=w3_t[j].rearrange("p dc q -> p (dc q)")),
                inc=("s_w0" if wb_i == 0 else "s_w1", 16))
            bd.dep(PE, t_wd)
            bd.dep(PE, t_b3)
            bd.deps(PE, y1_toks)
            bnA = BN4[(j % 2) * 2]
            bnB = BN4[(j % 2) * 2 + 1]
            bd.dep(PE, bank_free[bnA])
            bd.dep(PE, bank_free[bnB])
            for dc in range(DC):
                wv = WBUF[:, wb_i * 2048 + dc * 128:wb_i * 2048 + (dc + 1) * 128]
                pe_mm(ps[bnA][:, 0:512], wv, Y1(dc)[:, 0:512], start=(dc == 0), stop=False)
                pe_mm(ps[bnB][:, 0:512], wv, Y1(dc)[:, 512:1024], start=(dc == 0), stop=False)
            t_mja = bd.run(PE, lambda e, bnA=bnA, j=j, B3R=B3R: e.matmul(
                ps[bnA][:, 0:512], B3R[:, j * 128:(j + 1) * 128],
                ONES_B[0:1, 0:512], start=False, stop=True), inc=("s_pe", 1))
            t_mjb = bd.run(PE, lambda e, bnB=bnB, j=j, B3R=B3R: e.matmul(
                ps[bnB][:, 0:512], B3R[:, j * 128:(j + 1) * 128],
                ONES_B[0:1, 0:512], start=False, stop=True), inc=("s_pe", 1))
            wring_free[wb_i] = t_mjb
            for hf, (bn, t_mj) in enumerate(((bnA, t_mja), (bnB, t_mjb))):
                bd.dep(V, t_mj)
                bd.dep(V, free_tok["scr_w3"])
                scr = SCR[:, 512:1024]
                xs = Xc(j)[:, hf * 512:hf * 512 + 512]
                t_e1 = bd.run(V, lambda e, bn=bn, scr=scr, xs=xs: e.scalar_tensor_tensor(
                    scr, ps[bn][:, 0:512], 0.5, xs, AL.mult, AL.add), inc=("s_dv", 1))
                bank_free[bn] = t_e1
                bd.dep(ACT, t_e1)
                t_e2 = bd.run(ACT, lambda e, scr=scr, xs=xs: e.copy(xs, scr), inc=("s_ac", 1))
                free_tok["scr_w3"] = t_e2
                xup_toks.append(t_e2)
                x_ready[j].append(t_e2)

    def emit_attn_pair(p):
        pc0 = p * PW
        A_F = A_BF.bitcast(F32)
        C_BF = C_F.bitcast(BF16)
        XLNP = lambda dc: A_BF[:, dc * 512:(dc + 1) * 512]
        YB = lambda hc: A_BF[:, hc * 512:(hc + 1) * 512]
        PREFIX = lambda hc: B_BF[:, hc * 512:(hc + 1) * 512]
        ACC = lambda hc: C_BF[:, hc * 512:(hc + 1) * 512]
        P1 = lambda hc: D_BF[:, hc * 512:(hc + 1) * 512]
        P2M = lambda hc: D_BF[:, 8192 + hc * 512: 8192 + (hc + 1) * 512]
        LNxb = lambda dc: D_BF[:, (dc % 2) * 512:(dc % 2) * 512 + 512]
        LNx2 = lambda dc: D_BF[:, 1024 + (dc % 2) * 512:1024 + (dc % 2) * 512 + 512]
        SSET = [
            dict(sig=SCR[:, 0:512], a=SCR[:, 512:1024], k=SCR[:, 1024:1536],
                 u=SCR[:, 1536:2048], pf=A_F[:, 6144:6656], x2=A_F[:, 6656:7168],
                 qb=SCB[:, 0:512], vb=SCB[:, 512:1024], q2b=SCB[:, 1024:1536],
                 stb="st1"),
            dict(sig=C_F[:, 4096:4608], a=C_F[:, 4608:5120], k=C_F[:, 5120:5632],
                 u=C_F[:, 5632:6144], pf=C_F[:, 6144:6656], x2=C_F[:, 6656:7168],
                 qb=C_BF[:, 14336:14848], vb=C_BF[:, 14848:15360],
                 q2b=C_BF[:, 15360:15872], stb="st2"),
        ]
        WR = [WBUF[:, 0:2048], WBUF[:, 2048:4096],
              A_BF[:, 8192:10240], A_BF[:, 10240:12288]]
        WSEM = ["s_w0", "s_w1", "s_w2", "s_w3"]

        D_F = D_BF.bitcast(F32)
        lnp_toks = {}
        t_lnp = emit_ln(XLNP, pc0, PW, LNxb, LNx2,
                        lambda i: D_F[:, i * 512:(i + 1) * 512], 2,
                        apply_toks=lnp_toks)

        t_bc_toks = []
        phead_toks = []
        prev_tacc = [None, None]
        prev_lnpf = [None, None]
        prev_p2m = [None, None]
        bd.dep(P, free_tok["a_ext"])
        bd.dep(V, free_tok["S_LOC"])
        for hc in range(H):
            par = hc % 2
            S = SSET[par]
            s_sig, s_a, s_k, s_u = S["sig"], S["a"], S["k"], S["u"]
            s_pf, s_x2 = S["pf"], S["x2"]
            QBF, VBF, Q2BF = S["qb"], S["vb"], S["q2b"]
            stb = S["stb"]
            evac = {}
            for sidx, jp in enumerate((3, 1, 2, 4, 0)):
                wb_i = (p * 80 + hc * 5 + sidx + 2) % 4
                bd.dep(P, wring_free[wb_i])
                t_wd = bd.run(P, lambda e, hc=hc, jp=jp, wb_i=wb_i: e.dma_start(
                    out=WR[wb_i],
                    in_=w_in_t[hc, jp].rearrange("p dc q -> p (dc q)")),
                    inc=(WSEM[wb_i], 16))
                bn = BN4[(hc * 5 + sidx) % 4]
                bd.dep(PE, t_wd)
                bd.dep(PE, bank_free[bn])
                t_mm = None
                for dc in range(DC):
                    bd.deps(PE, lnp_toks.get(dc, []))
                    wv = WR[wb_i][:, dc * 128:(dc + 1) * 128]
                    t_mm = pe_mm(ps[bn][:, 0:512], wv, XLNP(dc),
                                 start=(dc == 0), stop=(dc == DC - 1),
                                 inc=("s_pe", 1) if dc == DC - 1 else None)
                wring_free[wb_i] = t_mm
                pb = ps[bn][:, 0:512]
                if jp == 3:
                    bd.dep(ACT, t_mm)
                    bd.dep(ACT, prev_tacc[par])
                    t1 = bd.run(ACT, lambda e, pb=pb, s_sig=s_sig: e.activation(
                        s_sig, pb, AF.Sigmoid, bias=C_TWO), inc=("s_ac", 1))
                    bank_free[bn] = t1
                    bd.dep(V, t1)
                    evac["a"] = bd.run(V, lambda e, s_a=s_a, s_sig=s_sig: e.tensor_scalar(
                        s_a, s_sig, 0.6, 0.9995, AL.max, AL.min), inc=("s_dv", 1))
                elif jp == 1:
                    bd.dep(ACT, t_mm)
                    t1 = bd.run(ACT, lambda e, pb=pb, s_k=s_k: e.activation(
                        s_k, pb, AF.Tanh), inc=("s_ac", 1))
                    bank_free[bn] = t1
                    evac["k"] = t1
                elif jp == 2:
                    bd.dep(ACT, t_mm)
                    t1 = bd.run(ACT, lambda e, pb=pb, VBF=VBF: e.copy(VBF, pb),
                                inc=("s_ac", 1))
                    bank_free[bn] = t1
                    bd.dep(V, t1)
                    bd.dep(V, evac["k"])
                    evac["u"] = bd.run(V, lambda e, s_u=s_u, s_k=s_k, VBF=VBF:
                                       e.tensor_tensor(s_u, s_k, VBF, AL.mult),
                                       inc=("s_dv", 1))
                elif jp == 4:
                    bd.dep(ACT, t_mm)
                    bd.dep(ACT, evac["a"])
                    t1 = bd.run(ACT, lambda e, pb=pb, s_sig=s_sig: e.activation(
                        s_sig, pb, AF.Sigmoid), inc=("s_ac", 1))
                    bank_free[bn] = t1
                    evac["g"] = t1
                else:
                    bd.dep(ACT, t_mm)
                    bd.dep(ACT, prev_p2m[par])
                    t1 = bd.run(ACT, lambda e, pb=pb, QBF=QBF: e.copy(QBF, pb),
                                inc=("s_ac", 1))
                    t1b = bd.run(ACT, lambda e, pb=pb, Q2BF=Q2BF: e.activation(
                        Q2BF, pb, AF.Square), inc=("s_ac", 1))
                    bank_free[bn] = t1b
                    bd.dep(PE, t1b)
                    bd.dep(PE, bank_free[stb])
                    t_q2 = pe_mm(ps[stb][:, 0:512], ONES_B[:, 0:128], Q2BF, True, True,
                                 inc=("s_pe", 1))
                    bd.dep(ACT, t_q2)
                    t_r1 = bd.run(ACT, lambda e, s_x2=s_x2, stb=stb: e.activation(
                        s_x2, ps[stb][:, 0:512], AF.Ln, bias=C_REPS, scale=C_I128),
                        inc=("s_ac", 1))
                    bank_free[stb] = t_r1
                    bd.dep(ACT, evac["u"])
                    t_r2 = bd.run(ACT, lambda e, s_k=s_k, s_x2=s_x2: e.activation(
                        s_k, s_x2, AF.Exp, scale=C_NH), inc=("s_ac", 1))
                    bd.dep(V, t_r2)
                    bd.dep(V, t_q2)
                    bd.run(V, lambda e, Q2BF=Q2BF, QBF=QBF, s_k=s_k:
                           e.tensor_tensor(Q2BF, QBF, s_k, AL.mult))
                    bd.dep(V, evac["g"])
                    bd.run(V, lambda e, hc=hc, Q2BF=Q2BF, s_sig=s_sig:
                           e.tensor_tensor(P1(hc), Q2BF, s_sig, AL.mult))
                    t_p2m = bd.run(V, lambda e, hc=hc, s_sig=s_sig, VBF=VBF:
                                   e.scalar_tensor_tensor(P2M(hc), s_sig, 1.0, VBF,
                                   AL.subtract, AL.mult), inc=("s_dv", 1))
                    phead_toks.append(t_p2m)
                    prev_p2m[par] = t_p2m

            bd.dep(V, t_ones)
            bd.dep(V, prev_lnpf[par])
            clip_i = evac["a"][1]
            for b_ in range(2):
                for c in range(NCH):
                    cs0 = b_ * 256 + c * CH
                    t_scan = bd.run(V, lambda e, cs0=cs0, s_pf=s_pf, s_a=s_a:
                        e.tensor_tensor_scan(
                        s_pf[:, cs0:cs0 + CH], s_a[:, cs0:cs0 + CH],
                        ONES_F[:, 0:CH], 1.0, AL.mult, AL.mult), base=clip_i,
                        inc=("s_dv", 1) if (b_ == 1 and c == NCH - 1) else None)
            scan_i = bd.vidx(V)
            bd.run(V, lambda e, hc=hc, s_pf=s_pf: e.tensor_copy(PREFIX(hc), s_pf),
                   base=scan_i)
            for b_ in range(2):
                base = hc * 8 + b_ * 4
                for c in range(NCH):
                    bd.run(V, lambda e, base=base, c=c, b_=b_, s_pf=s_pf: e.tensor_copy(
                        PFE[:, base + c:base + c + 1],
                        s_pf[:, b_ * 256 + c * CH + CH - 1: b_ * 256 + c * CH + CH]),
                        base=scan_i)
            for b_ in range(2):
                base = hc * 8 + b_ * 4
                bd.run(V, lambda e, base=base: e.tensor_copy(
                    APF[:, base:base + 1], ONES_F[:, 0:1]))
                for c in range(1, NCH):
                    bd.run(V, lambda e, base=base, c=c: e.tensor_tensor(
                        APF[:, base + c:base + c + 1], APF[:, base + c - 1:base + c],
                        PFE[:, base + c - 1:base + c], AL.mult))
                bd.run(V, lambda e, hc=hc, b_=b_, base=base: e.tensor_tensor(
                    S_LOC[:, hc * 2 + b_:hc * 2 + b_ + 1],
                    APF[:, base + 3:base + 4], PFE[:, base + 3:base + 4], AL.mult))
            bd.dep(ACT, t_scan)
            bd.dep(ACT, phead_toks[-1])
            t_lp = bd.run(ACT, lambda e, s_x2=s_x2, s_pf=s_pf: e.activation(
                s_x2, s_pf, AF.Ln), inc=("s_ac", 1))
            prev_lnpf[par] = t_lp
            t_rp = bd.run(ACT, lambda e, s_sig=s_sig, s_x2=s_x2: e.activation(
                s_sig, s_x2, AF.Exp, scale=C_N1), inc=("s_ac", 1))
            bd.dep(V, t_rp)
            t_term = bd.run(V, lambda e, s_a=s_a, s_sig=s_sig, s_u=s_u:
                            e.scalar_tensor_tensor(s_a, s_sig, 1e6, s_u,
                            AL.min, AL.mult), inc=("s_dv", 1))
            term_i = t_term[1]
            t_acc = None
            for b_ in range(2):
                for c in range(NCH):
                    cs0 = b_ * 256 + c * CH
                    t_acc = bd.run(V, lambda e, hc=hc, cs0=cs0, s_a=s_a:
                        e.tensor_tensor_scan(
                        ACC(hc)[:, cs0:cs0 + CH], ONES_F[:, 0:CH],
                        s_a[:, cs0:cs0 + CH], 0.0, AL.mult, AL.add), base=term_i,
                        inc=("s_dv", 1) if (b_ == 1 and c == NCH - 1) else None)
            for b_ in range(2):
                base = hc * 8 + b_ * 4
                bd.run(V, lambda e, base=base: e.memset(Z0[:, base:base + 1], 0.0))
                for c in range(NCH):
                    accv = ACC(hc)[:, b_ * 256 + c * CH + CH - 1: b_ * 256 + c * CH + CH]
                    dst = (Z0[:, base + c + 1:base + c + 2] if c < NCH - 1
                           else S_LOC[:, 32 + hc * 2 + b_: 32 + hc * 2 + b_ + 1])
                    tok = bd.run(V, lambda e, dst=dst, base=base, c=c, accv=accv:
                                 e.scalar_tensor_tensor(dst,
                                 accv, Z0[:, base + c:base + c + 1],
                                 PFE[:, base + c:base + c + 1], AL.add, AL.mult),
                                 inc=("s_dv", 1) if c == NCH - 1 else None)
            t_bc_toks.append(tok)
            phead_toks.append(t_acc)
            prev_tacc[par] = tok

        bd.deps(SP, t_bc_toks)
        t_s1 = bd.run(SP, lambda e, p=p: e.dma_start(out=sc_in[p], in_=S_LOC[:]),
                      inc=("s_io", 16))
        free_tok["S_LOC"] = t_s1
        bd.dep(P, t_s1)
        t_cc = bd.run(P, lambda e, p=p: e.collective_compute(
            "AllGather", AL.bypass, replica_groups=[list(range(NCORES))],
            ins=[sc_in[p].opt()], outs=[sc_out[p].opt()]), inc=("s_cc", 1))
        bd.dep(SP, t_cc)
        bd.dep(SP, free_tok["S_ALL"])
        t_sa = bd.run(SP, lambda e, p=p: e.dma_start(
            out=S_ALL[:].rearrange("p (r c) -> p r c", r=NCORES),
            in_=sc_out[p].rearrange("(r p) c -> p r c", p=128)), inc=("s_io", 16))

        bd.dep(V, t_sa)
        bd.run(V, lambda e: e.memset(PREF[:, 0:32], 0.0))
        for r in range(NCORES):
            Ar = S_ALL[:, r * 64:r * 64 + 32]
            Br = S_ALL[:, r * 64 + 32:r * 64 + 64]
            bd.run(V, lambda e, Ar=Ar, r=r: e.tensor_tensor(
                CSC[:, 0:32], Ar, PREF[:, r * 32:(r + 1) * 32], AL.mult))
            bd.run(V, lambda e, Br=Br, r=r: e.tensor_tensor(
                PREF[:, (r + 1) * 32:(r + 2) * 32], CSC[:, 0:32], Br, AL.add))
        pp = [CSC[:, 0:32], CSC[:, 32:64]]
        for r in range(NCORES):
            o = SINIT[:] if r == NCORES - 1 else pp[r % 2]
            if r == 0:
                bd.run(V, lambda e, o=o: e.tensor_scalar(
                    o, PREF[:, 0:32], OH[:, 0:1], None, AL.mult))
            else:
                bd.run(V, lambda e, o=o, r=r, i=pp[(r + 1) % 2]: e.scalar_tensor_tensor(
                    o, PREF[:, r * 32:(r + 1) * 32], OH[:, r:r + 1], i, AL.mult, AL.add))
        sr0 = bd.vidx("dve")
        for hc in range(H):
            for b_ in range(2):
                base = hc * 8 + b_ * 4
                sic = SINIT[:, hc * 2 + b_:hc * 2 + b_ + 1]
                for c in range(NCH):
                    bd.run(V, lambda e, base=base, c=c, sic=sic: e.scalar_tensor_tensor(
                        SREAL[:, base + c:base + c + 1], APF[:, base + c:base + c + 1],
                        sic, Z0[:, base + c:base + c + 1], AL.mult, AL.add), base=sr0)
        t_sreal = bd.run(V, lambda e: e.engine_nop(), inc=("s_dv", 1))
        free_tok["S_ALL"] = t_sreal

        y_toks = []
        attn_t1 = []
        bd.dep(V, t_sreal)
        bd.deps(V, wring_free)
        g0 = bd.vidx("dve")
        last_t1 = [g0, g0]
        for hc in range(H):
            par = hc % 2
            sa = SSET[par]["a"]
            sk = SSET[par]["k"]
            for b_ in range(2):
                base = hc * 8 + b_ * 4
                for c in range(NCH):
                    cs0 = b_ * 256 + c * CH
                    bd.run(V, lambda e, hc=hc, cs0=cs0, base=base, c=c, sa=sa:
                        e.scalar_tensor_tensor(
                        sa[:, cs0:cs0 + CH], ACC(hc)[:, cs0:cs0 + CH],
                        SREAL[:, base + c:base + c + 1], PREFIX(hc)[:, cs0:cs0 + CH],
                        AL.add, AL.mult), base=last_t1[par])
            t1 = bd.run(V, lambda e, hc=hc, sa=sa, sk=sk: e.tensor_tensor(
                sk, P1(hc), sa, AL.mult), inc=("s_dv", 1))
            attn_t1.append(t1)
            last_t1[par] = t1[1]
            y_toks.append(bd.run(V, lambda e, hc=hc, sk=sk: e.tensor_tensor(
                YB(hc), sk, P2M(hc), AL.subtract), inc=("s_dv", 1)))
        free_tok["b_arena"] = y_toks[-1]
        free_tok["d_arena"] = attn_t1[7]
        free_tok["d_arena2"] = attn_t1[15]

        for j in range(16):
            wb_i = j % 2
            bd.dep(P, wring_free[wb_i])
            t_wd = bd.run(P, lambda e, j=j, wb_i=wb_i: e.dma_start(
                out=WBUF[:, wb_i * 2048:wb_i * 2048 + 2048],
                in_=w_out_t[j].rearrange("p dc q -> p (dc q)")),
                inc=("s_w0" if wb_i == 0 else "s_w1", 16))
            bn = "mmA" if wb_i == 0 else "mmB"
            bd.dep(PE, t_wd)
            bd.dep(PE, bank_free[bn])
            t_mj = None
            for ch in range(DC):
                if ch % 4 == 0:
                    bd.dep(PE, y_toks[min(ch + 3, DC - 1)])
                wv = WBUF[:, wb_i * 2048 + ch * 128:wb_i * 2048 + (ch + 1) * 128]
                t_mj = pe_mm(ps[bn][:, 0:512], wv, YB(ch), start=(ch == 0), stop=(ch == DC - 1),
                             inc=("s_pe", 1) if ch == DC - 1 else None)
            wring_free[wb_i] = t_mj
            bd.dep(V, t_mj)
            bd.dep(V, free_tok["scr_wo"])
            scr = SCR[:, 0:512]
            xs = Xc(j)[:, pc0:pc0 + PW]
            t_e1 = bd.run(V, lambda e, bn=bn, scr=scr, xs=xs: e.scalar_tensor_tensor(
                scr, ps[bn][:, 0:512], 0.0, xs, AL.bypass, AL.add), inc=("s_dv", 1))
            bank_free[bn] = t_e1
            bd.dep(ACT, t_e1)
            t_e2 = bd.run(ACT, lambda e, scr=scr, xs=xs: e.copy(xs, scr), inc=("s_ac", 1))
            free_tok["scr_wo"] = t_e2
            xup_toks.append(t_e2)
            x_ready[j].append(t_e2)

    emit_ffn(0)
    emit_attn_pair(0)
    emit_attn_pair(1)
    emit_ffn(1)

    for dc in range(DC):
        bd.deps(SP, x_ready[dc])
        bd.run(SP, lambda e, dc=dc: e.dma_start(out=out[dc], in_=Xc(dc)), inc=("s_st", 16))
    bd.dep(SP, ("s_st", bd.cnt["s_st"]))

    with ExitStack() as stack:
        handles = {}
        for name in bd.sem_names:
            handles[name] = stack.enter_context(nc.semaphore(name))
        block = stack.enter_context(nc.Block())

        @block.tensor
        def _(t):
            bd.replay("pe", t, handles)

        @block.scalar
        def _(s):
            bd.replay("act", s, handles)

        @block.vector
        def _(v):
            bd.replay("dve", v, handles)

        @block.gpsimd
        def _(g):
            bd.replay("pool", g, handles)

        @block.sync
        def _(sy):
            bd.replay("sp", sy, handles)

    return nc


def kernel(L0_attn_x, L0_attn_w_in, L0_attn_w_out, L0_ffn_switch_w, L0_ffn_switch_b,
           L0_ffn_w1, L0_ffn_b1, L0_ffn_w2, L0_ffn_b2, L0_ffn_w3, L0_ffn_b3):
    x = np.ascontiguousarray(np.asarray(L0_attn_x, dtype=np.float32))
    w_in = np.ascontiguousarray(np.asarray(L0_attn_w_in, dtype=np.float32))
    w_out = np.ascontiguousarray(np.asarray(L0_attn_w_out, dtype=np.float32))
    sw = np.ascontiguousarray(np.asarray(L0_ffn_switch_w, dtype=np.float32))
    swb = np.ascontiguousarray(np.asarray(L0_ffn_switch_b, dtype=np.float32))
    w1 = np.ascontiguousarray(np.asarray(L0_ffn_w1, dtype=np.float32))
    b1 = np.ascontiguousarray(np.asarray(L0_ffn_b1, dtype=np.float32))
    w2 = np.ascontiguousarray(np.asarray(L0_ffn_w2, dtype=np.float32))
    b2 = np.ascontiguousarray(np.asarray(L0_ffn_b2, dtype=np.float32))
    w3 = np.ascontiguousarray(np.asarray(L0_ffn_w3, dtype=np.float32))
    b3 = np.ascontiguousarray(np.asarray(L0_ffn_b3, dtype=np.float32))

    wt = w_in.reshape(5, H, 128, DC, 128)
    w_in_t = np.ascontiguousarray(wt.transpose(1, 0, 4, 3, 2))
    wo = w_out.reshape(16, 128, DC, 128)
    w_out_t = np.ascontiguousarray(wo.transpose(0, 3, 2, 1))
    w3r = w3.reshape(16, 128, DC, 128)
    w3_t = np.ascontiguousarray(w3r.transpose(0, 3, 2, 1))
    sw3 = sw.reshape(8, 2048, 2048)
    w1T = np.ascontiguousarray(w1.transpose(0, 2, 1))
    w2T = np.ascontiguousarray(w2.transpose(0, 2, 1))
    w12T = np.ascontiguousarray(np.concatenate([w1T, w2T], axis=0).transpose(1, 0, 2))
    b1T = np.ascontiguousarray(b1.T)
    b2T = np.ascontiguousarray(b2.T)
    b3row = np.ascontiguousarray(b3[None, :])
    swb4 = np.ascontiguousarray(np.broadcast_to(swb[None, :], (4, 8)))

    nc = build_nc()

    in_maps = []
    for c in range(NCORES):
        sl = slice(256 * c, 256 * c + 256)
        x_fm = np.ascontiguousarray(
            x[:, sl, :].transpose(2, 0, 1).reshape(DC, 128, T))
        sw_c = np.ascontiguousarray(
            sw3[:, sl, :].transpose(0, 2, 1).reshape(8, DC, 128, TB).transpose(0, 2, 1, 3))
        oh = np.zeros((128, 8), np.float32)
        oh[:, c] = 1.0
        in_maps.append({
            "x_fm": x_fm, "w_in_t": w_in_t, "w_out_t": w_out_t, "w3_t": w3_t,
            "sw_t": sw_c, "w12T": w12T, "b1T": b1T, "b2T": b2T,
            "b3row": b3row, "swb": swb4, "onehot": oh,
        })

    res = run_bass_kernel_spmd(nc, in_maps, core_ids=list(range(NCORES)))

    outp = np.empty((4, 2048, 2048), np.float32)
    for c in range(NCORES):
        r = np.asarray(res.results[c]["out"])
        r = r.reshape(2048, 4, 256).transpose(1, 2, 0)
        outp[:, 256 * c:256 * c + 256, :] = r
    return outp
